# version 05
# baseline (speedup 1.0000x reference)
import numpy as np
import ml_dtypes
import contextlib
import concourse.bass as bass
import concourse.mybir as mybir
from concourse.bass_utils import run_bass_kernel_spmd
from concourse.ap import AP

F32 = mybir.dt.float32
BF16 = mybir.dt.bfloat16
AF = mybir.ActivationFunctionType
ALU = mybir.AluOpType
AX = mybir.AxisListType
BF = ml_dtypes.bfloat16

N_CORES = 8
SEQS = 2
L = 2048
D = 1024
H = 16
HD = 64
TOK = SEQS * L
CSH = 24.0
EPS = 1e-5
NBK = 32
TBL = 4095
GW = 3968
E = 8
EH = 64
C2 = 2048
HC = 512
O3 = 3072
FT = 512
NFT = TOK // FT
DT_ = 0.1

PHASE_B = True
DEBUG_Y1 = False
DEBUG_ATT = False
DEBUG_S1 = False
BARRIER_OPROJ = False
DEBUG_B = False

HL = H * 65



def _bucket_of_delta(delta):
    n = np.asarray(delta, np.int64)
    out = np.zeros_like(n)
    small = (n >= 1) & (n < 16)
    out[small] = n[small]
    big = n >= 16
    nf = n[big].astype(np.float32)
    v = np.log(nf / np.float32(16.0)) / np.float32(np.log(128.0 / 16.0)) * np.float32(16.0)
    out[big] = np.minimum(16 + v.astype(np.int32), 31)
    return out


def _host_consts():
    c = {}
    c["idn32"] = np.eye(128, dtype=np.float32)
    c["idnb"] = np.eye(128, dtype=np.float32).astype(BF)
    m = np.arange(TBL)
    bk = _bucket_of_delta(2047 - m)
    oh = np.zeros((NBK, TBL), np.float32)
    oh[bk, m] = 1.0
    c["ohrev"] = oh.astype(BF)
    sel = np.zeros((E, HC), np.float32)
    for e in range(E):
        sel[e, e * EH:(e + 1) * EH] = 1.0
    c["sel"] = sel.astype(BF)
    c["cstf"] = np.ascontiguousarray(
        np.broadcast_to(np.array([[-CSH, EPS]], np.float32), (128, 2)))
    return c


def _host_layout(inp):
    a = {}
    g = lambda k: np.asarray(inp[k])
    for nm in ("wq", "wk", "wv", "wo"):
        a[nm] = np.ascontiguousarray(g("L0_attn_" + nm))
    a["bqT"] = np.ascontiguousarray(g("L0_attn_bq").reshape(8, 128).T)
    a["bkT"] = np.ascontiguousarray(g("L0_attn_bk").reshape(8, 128).T)
    a["bv_row"] = np.ascontiguousarray(g("L0_attn_bv").reshape(1, D))
    a["bo_row"] = np.ascontiguousarray(g("L0_attn_bo").reshape(1, D))
    a["rel_bias"] = np.ascontiguousarray(g("L0_attn_rel_bias"))
    a["rw1"] = np.ascontiguousarray(g("L0_ffn_router_w1"))
    a["rb1T"] = np.ascontiguousarray(g("L0_ffn_router_b1").reshape(2, 128).T)
    a["rw2"] = np.ascontiguousarray(g("L0_ffn_router_w2"))
    a["rb2_row"] = np.ascontiguousarray(g("L0_ffn_router_b2").reshape(1, E))
    a["hw1"] = np.ascontiguousarray(g("L0_ffn_halt_w1"))
    a["hb1T"] = np.ascontiguousarray(g("L0_ffn_halt_b1").reshape(2, 128).T)
    a["hw2"] = np.ascontiguousarray(g("L0_ffn_halt_w2"))
    a["hb2rep"] = np.ascontiguousarray(
        np.broadcast_to(g("L0_ffn_halt_b2").reshape(1, 1), (128, 1)))
    ew1 = g("L0_ffn_expert_w1")
    a["w1cat"] = np.ascontiguousarray(np.transpose(ew1, (1, 0, 2)).reshape(C2, HC))
    a["b1catT"] = np.ascontiguousarray(g("L0_ffn_expert_b1").reshape(-1).reshape(4, 128).T)
    a["w2cat"] = np.ascontiguousarray(g("L0_ffn_expert_w2").reshape(HC, O3))
    a["b2cat"] = np.ascontiguousarray(g("L0_ffn_expert_b2"))
    a["sw1"] = np.ascontiguousarray(g("L0_ffn_shared_w1"))
    a["sb1T"] = np.ascontiguousarray(g("L0_ffn_shared_b1").reshape(64, 1))
    a["sw2p"] = np.ascontiguousarray(
        np.concatenate([g("L0_ffn_shared_w2"), g("L0_ffn_shared_b2").reshape(1, O3)], 0))
    a["swrep"] = np.ascontiguousarray(
        np.broadcast_to(np.asarray(g("L0_ffn_shared_weight"), np.float32).reshape(1, 1), (128, 1)))
    a["refw1"] = np.ascontiguousarray(g("L0_ffn_refine_w1"))
    a["refb1T"] = np.ascontiguousarray(g("L0_ffn_refine_b1").reshape(16, 128).T)
    a["refw2"] = np.ascontiguousarray(g("L0_ffn_refine_w2"))
    a["refb2_row"] = np.ascontiguousarray(g("L0_ffn_refine_b2").reshape(1, D))
    a["mu_rep"] = np.ascontiguousarray(np.broadcast_to(g("L0_ffn_mu").reshape(1, D), (128, D)))
    a["iw_rep"] = np.ascontiguousarray(
        np.broadcast_to(g("L0_ffn_integration_weight").reshape(1, D), (128, D)))
    return a



ENGS = ("tensor", "vector", "scalar", "gpsimd", "sync")
DMA_ENGS = ("gpsimd", "sync")


class Tok:
    __slots__ = ("eng", "idx")
    def __init__(self, eng, idx):
        self.eng, self.idx = eng, idx


class Prog:
    def __init__(self, nc):
        self.nc = nc
        self.ops = {e: [] for e in ENGS}

    def op(self, eng, fn, waits=(), mark=False, keep_same_engine=False):
        ws = []
        for w in waits:
            if w is None or (w.eng == eng and not keep_same_engine):
                continue
            ws.append(w)
        self.ops[eng].append([fn, ws, mark, 0])
        return Tok(eng, len(self.ops[eng]) - 1)

    def emit(self, block, stack, final_toks):
        sems = {e: stack.enter_context(self.nc.semaphore("sem_" + e)) for e in ENGS}
        for e in DMA_ENGS:
            for rec in self.ops[e]:
                rec[2] = True
        for e in ENGS:
            for rec in self.ops[e]:
                for w in rec[1]:
                    self.ops[w.eng][w.idx][2] = True
        for w in final_toks:
            self.ops[w.eng][w.idx][2] = True
        for e in ENGS:
            acc = 0
            step = 16 if e in DMA_ENGS else 1
            for rec in self.ops[e]:
                if rec[2]:
                    acc += step
                rec[3] = acc

        def mk_stream(e):
            ops = self.ops[e]
            step = 16 if e in DMA_ENGS else 1
            def body(eng):
                for fn, ws, marked, val in ops:
                    byeng = {}
                    for w in ws:
                        wv = self.ops[w.eng][w.idx][3]
                        byeng[w.eng] = max(byeng.get(w.eng, 0), wv)
                    for we, wv in sorted(byeng.items()):
                        eng.wait_ge(sems[we], wv)
                    inst = fn(eng)
                    if marked:
                        inst.then_inc(sems[e], step)
            return body

        for e in ENGS:
            if self.ops[e]:
                getattr(block, e)(mk_stream(e))
        byeng = {}
        for w in final_toks:
            byeng[w.eng] = max(byeng.get(w.eng, 0), self.ops[w.eng][w.idx][3])
        def fin(eng):
            for we, wv in sorted(byeng.items()):
                if we != "sync":
                    eng.wait_ge(sems[we], wv)
            if "sync" in byeng:
                eng.wait_ge(sems["sync"], byeng["sync"])
        block.sync(fin)



def build_nc():
    nc = bass.Bass("TRN2", target_bir_lowering=False, debug=False)
    dt = nc.dram_tensor

    x_in = dt("x", [TOK, D], F32, kind="ExternalInput").ap()
    y_out = dt("y", [TOK, D], F32, kind="ExternalOutput").ap()
    y1scr = dt("y1scr", [TOK, D], F32, kind="Internal").ap()
    y1dbg = dt("y1dbg", [TOK, D], F32, kind="ExternalOutput").ap() if DEBUG_Y1 else None
    if DEBUG_ATT:
        dbg = {nm: dt(nm, [128, 8 * L], BF16, kind="ExternalOutput").ap()
               for nm in ("qdbg", "kdbg", "ctxdbg")}
        dbg["vdbg"] = dt("vdbg", [128, 16 * HL], BF16, kind="ExternalOutput").ap()
        dbg["crawdbg"] = dt("crawdbg", [128, 1024], F32, kind="ExternalOutput").ap()
        dbg["srowdbg"] = dt("srowdbg", [1, 1024], F32, kind="ExternalOutput").ap()
    s1dbg = dt("s1dbg", [TOK, D], F32, kind="ExternalOutput").ap() if DEBUG_S1 else None
    lndbg = dt("lndbg", [32, 128, 16], F32, kind="ExternalOutput").ap() if DEBUG_S1 else None

    hb = {}
    for nm, sh, dd in [
            ("wq", (D, D), F32), ("wk", (D, D), F32), ("wv", (D, D), F32), ("wo", (D, D), F32),
            ("bqT", (128, 8), F32), ("bkT", (128, 8), F32),
            ("bv_row", (1, D), F32), ("bo_row", (1, D), F32), ("rel_bias", (NBK, H), F32),
            ("rw1", (D, 256), F32), ("rb1T", (128, 2), F32), ("rw2", (256, E), F32),
            ("rb2_row", (1, E), F32),
            ("hw1", (D, 256), F32), ("hb1T", (128, 2), F32), ("hw2", (256, 1), F32),
            ("hb2rep", (128, 1), F32),
            ("w1cat", (C2, HC), F32), ("b1catT", (128, 4), F32), ("w2cat", (HC, O3), F32),
            ("b2cat", (E, O3), F32),
            ("sw1", (C2, EH), F32), ("sb1T", (EH, 1), F32), ("sw2p", (EH + 1, O3), F32),
            ("swrep", (128, 1), F32),
            ("refw1", (D, C2), F32), ("refb1T", (128, 16), F32), ("refw2", (C2, D), F32),
            ("refb2_row", (1, D), F32),
            ("mu_rep", (128, D), F32), ("iw_rep", (128, D), F32),
            ("idn32", (128, 128), F32), ("idnb", (128, 128), BF16),
            ("cstf", (128, 2), F32),
            ("ohrev", (NBK, TBL), BF16), ("sel", (E, HC), BF16)]:
        hb[nm] = dt(nm, list(sh), dd, kind="ExternalInput").ap()

    WSCR_SHAPES = {"w1cat": (C2, HC), "w2cat": (HC, O3), "sw1": (C2, EH),
                   "sw2p": (EH + 1, O3), "rw1": (D, 256), "hw1": (D, 256),
                   "refw1": (D, C2), "refw2": (C2, D), "rw2": (256, E),
                   "hw2": (256, 1), "b2cat": (E, O3)}
    wscr = {nm: dt("scr_" + nm, list(sh), BF16, kind="Internal").ap()
            for nm, sh in WSCR_SHAPES.items()}

    stack = contextlib.ExitStack()
    stackA = contextlib.ExitStack()
    sb = {}
    def sbuf(name, shape, dtype, st=None):
        sb[name] = (st or stack).enter_context(nc.sbuf_tensor(name, list(shape), dtype))
        return sb[name]
    def sbufA(name, shape, dtype):
        return sbuf(name, shape, dtype, st=stackA)

    onesb = sbuf("onesb", [1, 128], BF16)

    lnst  = sbuf("lnst",  [128, 16], F32)
    idn32 = sbuf("idn32s", [128, 128], F32)
    idnb_ = sbuf("idnbs", [128, 128], BF16)
    cstf  = sbuf("cstfs", [128, 2], F32)
    xbt   = sbufA("xbt",   [128, D], F32)
    xfm   = sbufA("xfm",   [128, 8 * L], BF16)
    qfm   = sbufA("qfm",   [128, 8 * L], BF16)
    kfm   = sbufA("kfm",   [128, 8 * L], BF16)
    vtm   = sbufA("vtm",   [128, 16 * HL], BF16)
    wtm   = sbufA("wtm",   [128, 2 * D], F32)
    wT    = sbufA("wT",    [128, 8 * D], BF16)
    erev  = sbufA("erev",  [H, TBL], BF16)
    ptile = sbufA("ptile", [128, 2 * 1024], BF16)
    craw  = sbufA("craw",  [128, 1024], F32)
    xf32  = sbufA("xf32",  [128, 2 * 1024], F32)
    srow  = sbufA("srow",  [1, 1024], F32)
    scol  = sbufA("scol",  [128, 16], F32)
    brow  = sbufA("brow",  [1, 2 * D], BF16)
    bias8 = sbufA("bias8", [128, 24], F32)
    relb  = sbufA("relb",  [NBK, H], BF16)
    relb32 = sbufA("relb32", [NBK, H], F32)
    ones32 = sbufA("ones32", [1, 64], F32)

    psum = stack.enter_context(nc.psum_tensor("PS", [128, 4096], F32))
    psb = psum[:, :].bitcast(BF16)

    def pbank(i, w=1):
        return psum[:, i * 512:(i + w) * 512]

    def pbankb(i, off, w):
        return psb[:, i * 1024 + off: i * 1024 + off + w]

    P = Prog(nc)

    def dma(dst, src, eng="sync", waits=(), slow=False):
        def fn(e, dst=dst, src=src, slow=slow):
            return e.dma_start(out=dst, in_=src, allow_slow_non_contiguous=slow)
        return P.op(eng, fn, waits)

    def mm(out, lhsT, rhs, start, stop, waits=()):
        def fn(e, out=out, lhsT=lhsT, rhs=rhs, start=start, stop=stop):
            return e.matmul(out, lhsT, rhs, start=start, stop=stop)
        return P.op("tensor", fn, waits)

    def tr(out, in_, ident, waits=()):
        def fn(e, out=out, in_=in_, ident=ident):
            return e.transpose(out, in_, ident)
        return P.op("tensor", fn, waits)

    def act(out, in_, func, bias=0.0, scale=1.0, waits=()):
        def fn(e, out=out, in_=in_, func=func, bias=bias, scale=scale):
            return e.activation(out, in_, func, bias=bias, scale=scale)
        return P.op("scalar", fn, waits)

    def acopy(out, in_, waits=()):
        def fn(e, out=out, in_=in_):
            return e.copy(out, in_)
        return P.op("scalar", fn, waits)

    def vop(name, *args, waits=(), mark=False, keep_same_engine=False, **kw):
        def fn(e, name=name, args=args, kw=kw):
            return getattr(e, name)(*args, **kw)
        return P.op("vector", fn, waits, mark=mark, keep_same_engine=keep_same_engine)

    tk_id32 = dma(idn32[:, :], hb["idn32"][:, :])
    tk_cst = dma(cstf[:, :], hb["cstf"][:, :])
    tk_idb = dma(idnb_[:, :], hb["idnb"][:, :])
    ohv_stack = contextlib.ExitStack()
    ohv = ohv_stack.enter_context(nc.sbuf_tensor("ohv", [NBK, TBL], BF16))
    tk_relb0 = dma(relb32[:, :], hb["rel_bias"][:, :], "sync")
    tk_relb = vop("tensor_copy", relb[:, :], relb32[:, :], waits=[tk_relb0])
    tk_oh = dma(ohv[:, :], hb["ohrev"][:, :], "sync")
    tk_ones = vop("memset", onesb[:, :], 1.0)
    tk_ones32 = vop("memset", ones32[:, :], 1.0)
    tk_srow1 = dma(srow[0:1, :], hb["bv_row"][:, :], "sync")
    tk_brow = acopy(brow[0:1, 0:D], srow[0:1, :], waits=[tk_srow1])
    tk_srow2 = dma(srow[0:1, :], hb["bo_row"][:, :], "sync", waits=[tk_brow])
    tk_brow2 = acopy(brow[0:1, D:2 * D], srow[0:1, :], waits=[tk_srow2])
    tk_srow_free = tk_brow2
    tk_b8 = dma(bias8[:, 0:8], hb["bqT"][:, :])
    tk_b8b = dma(bias8[:, 8:16], hb["bkT"][:, :])
    tk_bqs = vop("tensor_scalar_mul", bias8[:, 16:24], bias8[:, 0:8], 0.125,
                 waits=[tk_b8])

    erev_mm = []
    for j in range(8):
        c0, c1 = j * 512, min((j + 1) * 512, TBL)
        erev_mm.append(mm(psum[0:H, c0:c1], relb[:, :], ohv[:, c0:c1], True, True,
                          waits=[tk_relb, tk_oh]))
    tk_erev = act(erev[:, :], psum[0:H, 0:TBL], AF.Exp, waits=erev_mm)
    ohv_stack.close()
    gtab = sbufA("gtab", [128, 2 * GW], BF16)

    tk_vones = vop("memset", vtm[:, :], 1.0)

    cast_last = None
    cast_prev = [None, None]
    cast_st_prev = [None, None]
    pieces = []
    for nm in WSCR_SHAPES:
        rows, cols = WSCR_SHAPES[nm]
        srct, dsth = hb[nm], wscr[nm]
        nrb = (rows + 127) // 128
        for rb in range(nrb):
            r0 = rb * 128
            rr = min(128, rows - r0)
            for cc0 in range(0, cols, 1024):
                cc1 = min(cc0 + 1024, cols)
                pieces.append((srct, dsth, r0, rr, cc0, cc1))
    cast_lds = []

    def emit_cast_load(ci):
        srct, dsth, r0, rr, cc0, cc1 = pieces[ci]
        sl = ci % 2
        w = [cast_prev[sl]] if cast_prev[sl] else []
        cast_lds.append(dma(wtm[0:rr, sl * D:sl * D + cc1 - cc0],
                            srct[r0:r0 + rr, cc0:cc1], "gpsimd", waits=w))

    emit_cast_load(0)
    for ci in range(len(pieces)):
        srct, dsth, r0, rr, cc0, cc1 = pieces[ci]
        sl = ci % 2
        if ci + 1 < len(pieces):
            emit_cast_load(ci + 1)
            settle = cast_lds[ci + 1]
        else:
            settle = dma(scol[0:1, 8:16], hb["bo_row"][0:1, 0:8], "gpsimd")
        w2 = [cast_lds[ci], settle] + ([cast_st_prev[sl]] if cast_st_prev[sl] else [])
        t2 = acopy(xfm[0:rr, sl * 1024:sl * 1024 + cc1 - cc0],
                   wtm[0:rr, sl * D:sl * D + cc1 - cc0], waits=w2)
        cast_prev[sl] = t2
        cast_last = dma(dsth[r0:r0 + rr, cc0:cc1],
                        xfm[0:rr, sl * 1024:sl * 1024 + cc1 - cc0], "sync",
                        waits=[t2])
        cast_st_prev[sl] = cast_last

    y_toks = []
    y1_toks = []

    def weight_transpose(name, first_wait, bank_waits=(None, None), wT_wait=None):
        xr = hb[name].rearrange("(a p) c -> a p c", p=128)
        evs = []
        slot_free = [None] * 2
        bank_free = list(bank_waits)
        lts = []

        def emit_load(rb):
            sl = rb % 2
            w = []
            if first_wait is not None and rb == 0:
                w.append(first_wait)
            if slot_free[sl] is not None:
                w.append(slot_free[sl])
            lts.append(dma(wtm[:, sl * D:(sl + 1) * D], xr[rb], "sync", waits=w))

        emit_load(0)
        for rb in range(8):
            sl = rb % 2
            if rb + 1 < 8:
                emit_load(rb + 1)
                settle = lts[rb + 1]
            else:
                settle = dma(scol[0:1, 0:8], hb["bo_row"][0:1, 0:8], "sync")
            trs = []
            for cb in range(8):
                bk = cb // 4
                tw = [lts[rb], tk_id32]
                if cb == 0:
                    tw.append(settle)
                if cb % 4 == 0 and bank_free[bk] is not None:
                    tw.append(bank_free[bk])
                trs.append(tr(psum[:, (6 + bk) * 512 + (cb % 4) * 128:
                                   (6 + bk) * 512 + (cb % 4) * 128 + 128],
                              wtm[:, sl * D + cb * 128: sl * D + (cb + 1) * 128],
                              idn32[:, :], waits=tw))
            slot_free[sl] = trs[-1]
            wTt = wT[:, :]
            for bk in range(2):
                dst = AP(wTt.tensor, (4 * bk) * D + rb * 128, [[8 * D, 128], [D, 4], [1, 128]])
                srcp = psum[:, (6 + bk) * 512:(6 + bk) * 512 + 512].rearrange(
                    "p (a b) -> p a b", a=4)
                ew = list(trs[bk * 4:(bk + 1) * 4])
                if rb == 0 and wT_wait is not None:
                    ew.append(wT_wait)
                ev = acopy(dst, srcp, waits=ew)
                evs.append(ev)
                bank_free[bk] = ev
        return evs[-1], evs

    prev_seq_tok = cast_last
    for s in range(SEQS):
        xr = x_in.rearrange("(t p) c -> t p c", p=128)
        x_ev = []
        bank_free = [tk_erev, tk_erev] if s == 0 else [None, None]
        xslots = [xbt[:, :], xf32[:, 0:1024], xf32[:, 1024:2048]]
        xslot_free = [None, None, None]
        xlts = []

        def emit_xload(tt):
            sl3 = tt % 3
            w = []
            if tt == 0 and prev_seq_tok is not None:
                w.append(prev_seq_tok)
            if xslot_free[sl3] is not None:
                w.append(xslot_free[sl3])
            xlts.append(dma(xslots[sl3], xr[s * 16 + tt], "sync", waits=w))

        emit_xload(0)
        for tt in range(16):
            sl3 = tt % 3
            if tt + 1 < 16:
                emit_xload(tt + 1)
                settle = xlts[tt + 1]
            else:
                settle = dma(scol[0:1, 0:8], hb["bo_row"][0:1, 0:8], "sync")
            trs = []
            for cb in range(8):
                bk = cb // 4
                tw = [xlts[tt], tk_id32]
                if cb == 0:
                    tw.append(settle)
                if cb % 4 == 0 and bank_free[bk] is not None:
                    tw.append(bank_free[bk])
                trs.append(tr(psum[:, bk * 512 + (cb % 4) * 128:
                                   bk * 512 + (cb % 4) * 128 + 128],
                              xslots[sl3][:, cb * 128:(cb + 1) * 128],
                              idn32[:, :], waits=tw))
            xslot_free[sl3] = trs[-1]
            xfmt = xfm[:, :]
            for bk in range(2):
                dst = AP(xfmt.tensor, (4 * bk) * L + tt * 128, [[8 * L, 128], [L, 4], [1, 128]])
                srcp = psum[:, bk * 512:bk * 512 + 512].rearrange("p (a b) -> p a b", a=4)
                ev = acopy(dst, srcp, waits=trs[bk * 4:(bk + 1) * 4])
                x_ev.append(ev)
                bank_free[bk] = ev
        xf32_guard = [xslot_free[1], xslot_free[2]]

        wq_tok, _ = weight_transpose("wq", prev_seq_tok,
            bank_waits=((tk_erev, tk_erev) if s == 0 else (prev_seq_tok, prev_seq_tok)),
            wT_wait=prev_seq_tok)
        q_ev = []
        ev_par = [None, None]
        for db in range(8):
            for t4 in range(4):
                par = (4 * db + t4) % 2
                pb = pbank(2 + 2 * par)
                ms = []
                for cb in range(8):
                    w = []
                    if cb == 0:
                        w = [wq_tok] + (x_ev if db == 0 and t4 == 0 else [])
                        if ev_par[par] is not None:
                            w.append(ev_par[par])
                    ms.append(mm(pb, wT[:, cb * D + db * 128: cb * D + (db + 1) * 128],
                                 xfm[:, cb * L + t4 * 512: cb * L + (t4 + 1) * 512],
                                 cb == 0, cb == 7, waits=w))
                ev = act(qfm[:, db * L + t4 * 512: db * L + (t4 + 1) * 512], pb, AF.Identity,
                         bias=bias8[:, 16 + db:17 + db], scale=0.125,
                         waits=[ms[-1], tk_bqs])
                q_ev.append(ev)
                ev_par[par] = ev
        wk_tok, _ = weight_transpose("wk", q_ev[-1],
            bank_waits=(q_ev[-1], q_ev[-1]), wT_wait=q_ev[-1])
        k_ev = []
        ev_par = [None, None]
        for db in range(8):
            for t4 in range(4):
                par = (4 * db + t4) % 2
                pb = pbank(2 + 2 * par)
                ms = []
                for cb in range(8):
                    w = []
                    if cb == 0:
                        w = [wk_tok]
                        if ev_par[par] is not None:
                            w.append(ev_par[par])
                        elif par == (4 * 7 + 3) % 2:
                            w.append(q_ev[-1])
                        else:
                            w.append(q_ev[-2])
                    ms.append(mm(pb, wT[:, cb * D + db * 128: cb * D + (db + 1) * 128],
                                 xfm[:, cb * L + t4 * 512: cb * L + (t4 + 1) * 512],
                                 cb == 0, cb == 7, waits=w))
                ev = act(kfm[:, db * L + t4 * 512: db * L + (t4 + 1) * 512], pb, AF.Identity,
                         bias=bias8[:, 8 + db:9 + db], scale=1.0,
                         waits=[ms[-1], tk_b8b])
                k_ev.append(ev)
                ev_par[par] = ev
        wv_tok, _ = weight_transpose("wv", k_ev[-1],
            bank_waits=(k_ev[-1], k_ev[-1]), wT_wait=k_ev[-1])
        v_ev = []
        ev_par = [None, None]
        gi = 0
        for tt in range(16):
            for dvh in range(2):
                par = gi % 2
                gi += 1
                pb = pbank(2 + 2 * par)
                ms = []
                for cb in range(8):
                    w = []
                    if cb == 0:
                        w = [wv_tok]
                        if ev_par[par] is not None:
                            w.append(ev_par[par])
                        else:
                            w.append(k_ev[-1])
                    ms.append(mm(pb, xfm[:, cb * L + tt * 128: cb * L + (tt + 1) * 128],
                                 wT[:, cb * D + dvh * 512: cb * D + (dvh + 1) * 512],
                                 cb == 0, False, waits=w))
                ms.append(mm(pb, onesb[0:1, 0:128],
                             brow[0:1, dvh * 512:(dvh + 1) * 512], False, True,
                             waits=[tk_ones, tk_brow]))
                vt = vtm[:, :]
                dst = AP(vt.tensor, tt * HL + dvh * 8 * 65,
                         [[16 * HL, 128], [65, 8], [1, 64]])
                src = pb.rearrange("p (a b) -> p a b", a=8)
                ev = acopy(dst, src, waits=[ms[-1], tk_vones])
                v_ev.append(ev)
                ev_par[par] = ev

        ctx_done = []
        g_free = [None, None]
        st_free = [None, None]
        p_free = [None, None]
        pv_free = [None, None]
        craw_free = [None, None]
        glue_free = tk_srow_free if s == 0 else None
        first_st = True
        epi = None
        m = 0

        def emit_epilogue_a(e):
            nonlocal glue_free
            hh_, qh_, pvb_, par_, pvl_ = e
            cr = craw[:, 0:1024]
            w = list(pvl_)
            if craw_free[0] is not None:
                w.append(craw_free[0])
            evt = acopy(cr[0:65, :], psum[0:65, pvb_ * 512: pvb_ * 512 + 1024], waits=w)
            return e + (evt,)

        def emit_sums(e):
            nonlocal glue_free
            hh_, qh_, pvb_, par_, pvl_, evt = e
            w = list(pvl_) + [evt]
            if glue_free is not None:
                w.append(glue_free)
            t3 = vop("reciprocal", srow[0:1, 0:1024],
                     psum[64:65, pvb_ * 512: pvb_ * 512 + 1024], waits=w)
            return e + (t3,)

        def emit_epilogue_b(e):
            nonlocal glue_free
            hh_, qh_, pvb_, par_, pvl_, evt, t3 = e
            hc_, hr_ = hh_ // 2, (hh_ % 2) * 64
            cr = craw[:, 0:1024]
            rep = []
            for jj in range(2):
                rep.append(mm(psum[0:64, (pvb_ + jj) * 512:(pvb_ + jj) * 512 + 512],
                              ones32[0:1, 0:64], srow[0:1, jj * 512:(jj + 1) * 512],
                              True, True, waits=[t3, tk_ones32, evt]))
            glue_free = rep[-1]
            nrm = []
            for jj in range(2):
                dst = xfm[hr_:hr_ + 64, hc_ * L + qh_ * 1024 + jj * 512:
                          hc_ * L + qh_ * 1024 + (jj + 1) * 512]
                w = [rep[jj], evt]
                if hh_ == 0 and qh_ == 0 and jj == 0:
                    w += v_ev
                nrm.append(vop("tensor_tensor", dst, cr[0:64, jj * 512:(jj + 1) * 512],
                               psum[0:64, (pvb_ + jj) * 512:(pvb_ + jj) * 512 + 512],
                               ALU.mult, waits=w))
            craw_free[0] = nrm[-1]
            pv_free[(pvb_ - 4) // 2] = nrm[-1]
            ctx_done.extend(nrm)

        for h in range(H):
            gpar = h % 2
            w = [tk_erev]
            if h < 2:
                w.append(q_ev[0])
            if g_free[gpar] is not None:
                w.append(g_free[gpar])
            ert = erev[:, :]
            gsrc = AP(ert.tensor, h * TBL, [[TBL, 1], [1, 128], [1, GW]])
            gtok = dma(gtab[:, gpar * GW:(gpar + 1) * GW], gsrc, "sync", waits=w)
            hc, hr = h // 2, (h % 2) * 64
            last_pm = None
            for qh in range(2):
                if epi is not None:
                    epi = emit_epilogue_a(epi)
                pvpair = m % 2
                pvb = 4 + 2 * pvpair
                pv_ms = []
                ets = {}
                pms = {}
                def emit_st(kt):
                    spar = kt % 2
                    ms = []
                    nonlocal first_st
                    for jj in range(2):
                        w = []
                        if jj == 0:
                            if first_st:
                                w = q_ev + k_ev + v_ev
                                first_st = False
                            if st_free[spar] is not None:
                                w.append(st_free[spar])
                        ms.append(mm(psum[:, (2 * spar) * 512 + jj * 512:
                                          (2 * spar) * 512 + (jj + 1) * 512],
                                     kfm[hr:hr + 64, hc * L + kt * 128: hc * L + (kt + 1) * 128],
                                     qfm[hr:hr + 64, hc * L + qh * 1024 + jj * 512:
                                         hc * L + qh * 1024 + (jj + 1) * 512],
                                     True, True, waits=w))
                    return ms
                def emit_exp_mult(kt, ms):
                    spar = kt % 2
                    ppar = kt % 2
                    w = ms + ([p_free[ppar]] if p_free[ppar] is not None else [])
                    et = act(ptile[:, ppar * 1024:(ppar + 1) * 1024],
                             psum[:, (2 * spar) * 512:(2 * spar) * 512 + 1024],
                             AF.Exp, bias=cstf[:, 0:1], waits=w)
                    st_free[spar] = et
                    off = 1024 - 1024 * qh + 128 * kt
                    gt_t = gtab[:, :]
                    grev = AP(gt_t.tensor, gpar * GW + off + 1023, [[2 * GW, 128], [-1, 1024]])
                    pm = vop("tensor_tensor",
                             ptile[:, ppar * 1024:(ppar + 1) * 1024],
                             ptile[:, ppar * 1024:(ppar + 1) * 1024],
                             grev, ALU.mult, waits=[et, gtok])
                    return pm
                def emit_pv(kt, pm):
                    ppar = kt % 2
                    for jj in range(2):
                        w = [pm]
                        if kt == 0 and pv_free[pvpair] is not None:
                            w.append(pv_free[pvpair])
                        pv_ms.append(mm(psum[0:65, (pvb + jj) * 512:(pvb + jj + 1) * 512],
                                        vtm[:, kt * HL + h * 65: kt * HL + h * 65 + 65],
                                        ptile[:, ppar * 1024 + jj * 512:
                                              ppar * 1024 + (jj + 1) * 512],
                                        kt == 0, kt == 15, waits=w))
                    p_free[ppar] = pv_ms[-1]
                ms0 = emit_st(0)
                pms[0] = emit_exp_mult(0, ms0)
                for kt in range(16):
                    if kt + 1 < 16:
                        msn = emit_st(kt + 1)
                        pms[kt + 1] = emit_exp_mult(kt + 1, msn)
                    emit_pv(kt, pms[kt])
                    if kt == 8 and epi is not None and len(epi) == 6:
                        epi = emit_sums(epi)
                last_pm = pms[15]
                if epi is not None:
                    emit_epilogue_b(epi)
                epi = (h, qh, pvb, m % 2, [pv_ms[-1], pv_ms[-2]])
                m += 1
            g_free[gpar] = last_pm
        epi = emit_epilogue_a(epi)
        epi = emit_sums(epi)
        emit_epilogue_b(epi)

        wo_tok, _ = weight_transpose("wo", v_ev[-1],
            bank_waits=(pv_free[1], pv_free[1]), wT_wait=v_ev[-1])
        xl_free = [None, None]
        ev_par = [None, None]
        lnst_free = None
        lnst_act_free = None
        first_o = True
        for tt in range(16):
            par = tt % 2
            w = []
            if xl_free[par] is not None:
                w.append(xl_free[par])
            else:
                w.append(xf32_guard[par])
            xl = dma(xf32[:, par * 1024:(par + 1) * 1024], xr[s * 16 + tt], "sync", waits=w)
            s1_toks = []
            for oh2 in range(2):
                mpar = (2 * tt + oh2) % 2
                pb = pbank(2 + 2 * mpar)
                ms = []
                for cb in range(8):
                    w = []
                    if cb == 0:
                        w = [wo_tok]
                        if first_o:
                            w += ctx_done
                            if BARRIER_OPROJ:
                                w += [pv_free, glue_free] + rep_free + \
                                     [t for t in p_free if t is not None] + \
                                     [t for t in craw_free if t is not None]
                            first_o = False
                        if ev_par[mpar] is not None:
                            w.append(ev_par[mpar])
                        else:
                            w += [t for t in st_free if t is not None]
                            w += [t for t in pv_free if t is not None]
                    ms.append(mm(pb, xfm[:, cb * L + tt * 128: cb * L + (tt + 1) * 128],
                                 wT[:, cb * D + oh2 * 512: cb * D + (oh2 + 1) * 512],
                                 cb == 0, False, waits=w))
                ms.append(mm(pb, onesb[0:1, 0:128],
                             brow[0:1, D + oh2 * 512: D + (oh2 + 1) * 512],
                             False, True, waits=[tk_brow2]))
                sl = xf32[:, par * 1024 + oh2 * 512: par * 1024 + (oh2 + 1) * 512]
                stk = vop("tensor_tensor", sl, pb, sl, ALU.add, waits=[ms[-1], xl])
                s1_toks.append(stk)
                ev_par[mpar] = stk
            s1ap = xf32[:, par * 1024:(par + 1) * 1024]
            if DEBUG_S1:
                sdr = s1dbg.rearrange("(t p) c -> t p c", p=128)
                s1_toks.append(dma(sdr[s * 16 + tt], s1ap, "sync", waits=list(s1_toks)))
                y_toks.append(s1_toks[-1])
            w = s1_toks + ([lnst_free] if lnst_free is not None else [])
            b1 = vop("bn_stats", lnst[:, 0:6], s1ap[:, 0:512], waits=w, mark=True)
            b2 = vop("bn_stats", lnst[:, 6:12], s1ap[:, 512:1024], waits=w, mark=True)
            b3 = vop("bn_aggr", lnst[:, 12:14], lnst[:, 0:12],
                     waits=[b1, b2] + ([lnst_act_free] if lnst_act_free is not None else []),
                     keep_same_engine=True)
            l1 = act(lnst[:, 14:15], lnst[:, 13:14], AF.Ln, bias=cstf[:, 1:2], waits=[b3, tk_cst])
            l2 = act(lnst[:, 15:16], lnst[:, 14:15], AF.Exp, scale=-0.5, waits=[l1])
            lnst_act_free = l1
            if DEBUG_S1:
                lnt = dma(lndbg[s * 16 + tt], lnst[:, :], "sync", waits=[l2, b3])
                y_toks.append(lnt)
            yn = vop("tensor_scalar", s1ap, s1ap, lnst[:, 12:13], lnst[:, 15:16],
                     ALU.subtract, ALU.mult,
                     waits=[l2] + ([s1_toks[-1], lnt] if DEBUG_S1 else []))
            lnst_free = yn
            ysr = y1scr.rearrange("(t p) c -> t p c", p=128)
            yst = dma(ysr[s * 16 + tt], s1ap, "sync", waits=[yn])
            y1_toks.append(yst)
            if DEBUG_Y1:
                ydr = y1dbg.rearrange("(t p) c -> t p c", p=128)
                y_toks.append(dma(ydr[s * 16 + tt], s1ap, "sync", waits=[yn]))
            xl_free[par] = yst
        prev_seq_tok = y1_toks[-1]

    stackA.close()
    if PHASE_B:
        env = dict(nc=nc, P=P, stack=stack, sbuf=sbuf, sb=sb, psum=psum, psb=psb,
                   hb=hb, wscr=wscr, y1scr=y1scr, y_out=y_out, y1_toks=y1_toks,
                   dma=dma, mm=mm, tr=tr, act=act, acopy=acopy, vop=vop,
                   onesb=onesb, idnb_=idnb_, idn32=idn32, cstf=cstf, lnst=lnst,
                   prev_seq_tok=prev_seq_tok, APc=AP)
        y_toks += _emit_ffn(env)

    with nc.Block() as block:
        P.emit(block, stack, y_toks if y_toks else [y1_toks[-1]])
    stack.close()
    return nc


def _emit_ffn(env):
    nc, P, sbuf = env["nc"], env["P"], env["sbuf"]
    psum, psb = env["psum"], env["psb"]
    hb, wscr, y1scr, y_out = env["hb"], env["wscr"], env["y1scr"], env["y_out"]
    y1_toks = env["y1_toks"]
    dma, mm, tr, act, acopy, vop = (env["dma"], env["mm"], env["tr"], env["act"],
                                    env["acopy"], env["vop"])
    onesb, idnb_, idn32, cstf, lnst = (env["onesb"], env["idnb_"], env["idn32"],
                                       env["cstf"], env["lnst"])
    gate = env["prev_seq_tok"]
    AP_ = env["APc"]

    y1t  = sbuf("y1t",  [128, 4096], F32)
    itm  = sbuf("itm",  [128, 4096], F32)
    vtm2 = sbuf("vtm2", [128, 4096], F32)
    ctxf = sbuf("ctxf", [128, 16 * 512], BF16)
    hf   = sbuf("hf",   [128, 4 * 512], BF16)
    h1f  = sbuf("h1f",  [128, 4 * 512], BF16)
    h2f  = sbuf("h2f",  [128, 4 * 512], BF16)
    m1r  = sbuf("m1r",  [128, 4 * 512], BF16)
    m2r  = sbuf("m2r",  [128, 4 * 512], BF16)
    hsh  = sbuf("hsh",  [65, 512], BF16)
    rhid = sbuf("rhid", [128, 2 * 512], BF16)
    hhid = sbuf("hhid", [128, 2 * 512], BF16)
    m1f  = sbuf("m1f",  [8, 512], BF16)
    m2f  = sbuf("m2f",  [8, 512], BF16)
    m1t  = sbuf("m1t",  [128, 64], BF16)
    mq   = sbuf("mq",   [128, 32], F32)
    cols = sbuf("cols", [128, 32], F32)
    ew   = sbuf("ew",   [128, 16 * 512], F32)
    wa   = sbuf("wa",   [128, 32768], BF16)
    rha  = sbuf("rha",  [128, 4116], BF16)
    murep = sbuf("murep", [128, D], F32)
    iwrep = sbuf("iwrep", [128, D], F32)
    bct  = sbuf("bct",  [128, 32], F32)
    sbt  = sbuf("sbt",  [64, 1], F32)
    rows = sbuf("rows", [1, D + 16], BF16)
    selb = sbuf("selb", [8, HC], BF16)

    W1O, W2O, S1O, S2O, B2O = 0, 8192, 20480, 21504, 24576
    R1O, R2O = 0, 16384
    RWO, HWO, RW2O, HW2O = 0, 2048, 4096, 4112

    toks_y = []
    dbgB = {}
    if DEBUG_B:
        for nm, sh in (("d_m1t", (128, 64)), ("d_cols", (128, 32)), ("d_mq", (128, 32)),
                       ("d_rhid", (128, 1024)), ("d_hf", (128, 2048)), ("d_hsh", (65, 512)),
                       ("d_h1f", (128, 2048)), ("d_m1r", (128, 2048)),
                       ("d_i0", (128, 4096)), ("d_v0", (128, 4096)),
                       ("d_i1", (128, 4096)), ("d_v1", (128, 4096)),
                       ("d_ln2in", (128, 4096)), ("d_ctxf", (128, 8192)),
                       ("d_sa", (128, 512)), ("d_sb", (128, 512)), ("d_sg", (128, 512)), ("d_vn", (128, 512)), ("d_g", (128, 512)), ("d_cols2", (128, 32)), ("d_sb1", (128, 512)), ("d_sb2", (128, 512)), ("d_refiw", (128, 512)), ("d_vn0", (128, 512)), ("d_vn1", (128, 512)), ("d_vn2", (128, 512))):
            dbgB[nm] = nc.dram_tensor(nm, list(sh), F32 if nm not in ("d_m1t", "d_hf", "d_h1f", "d_m1r", "d_ctxf", "d_rhid", "d_hsh") else BF16, kind="ExternalOutput").ap()
        dbgB["d_hidr"] = nc.dram_tensor("d_hidr", [128, 8192], BF16, kind="ExternalOutput").ap()

    ld = []
    ld.append(dma(murep[:, :], hb["mu_rep"][:, :], "sync", waits=[gate]))
    ld.append(dma(iwrep[:, :], hb["iw_rep"][:, :], "sync", waits=[gate]))
    ld.append(dma(bct[:, 0:2], hb["rb1T"][:, :], "sync", waits=[gate]))
    ld.append(dma(bct[:, 2:4], hb["hb1T"][:, :], "sync", waits=[gate]))
    ld.append(dma(bct[:, 4:8], hb["b1catT"][:, :], "sync", waits=[gate]))
    ld.append(dma(bct[:, 8:24], hb["refb1T"][:, :], "sync", waits=[gate]))
    ld.append(dma(bct[:, 24:25], hb["hb2rep"][:, :], "sync", waits=[gate]))
    ld.append(dma(bct[:, 25:26], hb["swrep"][:, :], "sync", waits=[gate]))
    ld.append(dma(sbt[:, :], hb["sb1T"][:, :], "sync", waits=[gate]))
    tk_b = ld[-1]
    t1 = vop("tensor_scalar_mul", bct[:, 26:27], bct[:, 24:25], 0.5, waits=[ld[6]])
    t2 = act(bct[:, 27:28], bct[:, 25:26], AF.Tanh, scale=0.5, waits=[ld[7]])
    t3 = vop("tensor_scalar", bct[:, 28:29], bct[:, 27:28], 0.5, 0.5,
             ALU.mult, ALU.add, waits=[t2])
    t4 = vop("tensor_scalar", bct[:, 29:30], bct[:, 28:29], -1.0, 1.0,
             ALU.mult, ALU.add, waits=[t3], keep_same_engine=True, mark=True)
    t5 = vop("tensor_scalar_mul", bct[:, 30:31], bct[:, 29:30], 0.5,
             waits=[t4], keep_same_engine=True, mark=True)
    tk_cols = t5
    tmpr = ew[0:1, 0:D]
    r1 = dma(tmpr, hb["refb2_row"][:, :], "sync", waits=[gate])
    r2 = acopy(rows[0:1, 0:D], tmpr, waits=[r1])
    r3 = dma(ew[0:1, 0:E], hb["rb2_row"][:, :], "sync", waits=[r2])
    r4 = acopy(rows[0:1, D:D + E], ew[0:1, 0:E], waits=[r3])
    tk_rows = r4
    tk_hones = vop("memset", hsh[64:65, :], 1.0, waits=[gate])
    tk_sel = dma(selb[:, :], hb["sel"][:, :], "sync", waits=[gate])
    def arena_load(dst_t, dst_off, scr_ap, rows_, cols_, planes, waits):
        srcv = scr_ap.rearrange("(a p) c -> p a c", p=128) if rows_ > 128 else scr_ap
        if rows_ > 128:
            dstap = AP_(dst_t[:, :].tensor, dst_off,
                        [[dst_t.shape[1], 128], [cols_, planes], [1, cols_]])
        else:
            dstap = AP_(dst_t[:, :].tensor, dst_off, [[dst_t.shape[1], rows_], [1, cols_]])
        return dma(dstap, srcv, "sync", waits=waits, slow=(cols_ < 8))
    lw = []
    lw.append(arena_load(rha, RWO, wscr["rw1"], D, 256, 8, [gate]))
    lw.append(arena_load(rha, HWO, wscr["hw1"], D, 256, 8, [gate]))
    lw.append(arena_load(rha, RW2O, wscr["rw2"], 256, 8, 2, [gate]))
    lw.append(arena_load(rha, HW2O, wscr["hw2"], 256, 1, 2, [gate]))
    tk_rh = lw[-1]

    bank_rd = [gate] * 8

    def mmg(outbank, outslice, parts, waits_first=()):
        t = None
        n = len(parts)
        for i, (lh, rh) in enumerate(parts):
            w = list(waits_first) + [bank_rd[outbank]] if i == 0 else []
            t = mm(outslice, lh, rh, i == 0, i == n - 1, waits=w)
        return t

    prev_tile_done = gate
    warena_free = gate
    ew_war = [None]
    for ft in range(NFT):
        tbase = ft * FT
        ysr = y1scr
        ysv = AP_(ysr.tensor, tbase * D, [[D, 128], [128 * D, 4], [1, D]])
        wv = [y1_toks[tbase // 128 + k] for k in range(4)] + [prev_tile_done]
        ly1 = dma(y1t[:, :], ysv, "sync", waits=wv)
        icp = vop("tensor_copy", itm[:, :], y1t[:, :], waits=[ly1])

        def to_fm(srctile, planes0, ctoks):
            evs = []
            for tc in range(4):
                trs = []
                for cb in range(8):
                    bk = 6 + cb // 4
                    w = list(ctoks) if (tc == 0 and cb == 0) else []
                    if cb % 4 == 0:
                        w.append(bank_rd[bk])
                    trs.append(tr(psum[:, bk * 512 + (cb % 4) * 128:
                                       bk * 512 + (cb % 4) * 128 + 128],
                                  srctile[:, tc * 1024 + cb * 128: tc * 1024 + (cb + 1) * 128],
                                  idn32[:, :], waits=w))
                ctxt = ctxf[:, :]
                for bk in range(2):
                    dst = AP_(ctxt.tensor, (planes0 + 4 * bk) * 512 + tc * 128,
                              [[16 * 512, 128], [512, 4], [1, 128]])
                    srcp = psum[:, (6 + bk) * 512:(6 + bk) * 512 + 512].rearrange(
                        "p (a b) -> p a b", a=4)
                    ev = acopy(dst, srcp, waits=trs[bk * 4:(bk + 1) * 4])
                    evs.append(ev)
                    bank_rd[6 + bk] = ev
            return evs
        y1fm_ev = to_fm(y1t, 0, [ly1])

        rh_ev = []
        for hc in range(2):
            bk = hc % 2
            parts = [(rha[:, RWO + cb * 256 + hc * 128: RWO + cb * 256 + (hc + 1) * 128],
                      ctxf[:, cb * 512:(cb + 1) * 512]) for cb in range(8)]
            t = mmg(bk, psum[:, bk * 512:(bk + 1) * 512], parts,
                    waits_first=[tk_rh] + y1fm_ev)
            ev = act(rhid[:, hc * 512:(hc + 1) * 512], psum[:, bk * 512:(bk + 1) * 512],
                     AF.Gelu, bias=bct[:, 0 + hc:1 + hc], scale=1.0, waits=[t, tk_b])
            rh_ev.append(ev)
            bank_rd[bk] = ev
        lg_toks = []
        for tc in range(4):
            sl = psum[:, 2 * 512 + tc * 8: 2 * 512 + (tc + 1) * 8]
            parts = [(rhid[:, hc * 512 + tc * 128: hc * 512 + (tc + 1) * 128],
                      rha[:, RW2O + hc * 8: RW2O + (hc + 1) * 8]) for hc in range(2)]
            parts.append((onesb[0:1, 0:128], rows[0:1, D:D + E]))
            lg_toks.append(mmg(2, sl, parts, waits_first=rh_ev + [tk_rows]))
        mt_toks = []
        alpha_toks = []
        for tc in range(4):
            sl = psum[:, 2 * 512 + tc * 8: 2 * 512 + (tc + 1) * 8]
            c0 = tc * 8
            q1 = vop("reduce_max", cols[:, c0:c0 + 1], sl, axis=AX.X, waits=lg_toks)
            q2 = vop("tensor_single_scalar", mq[:, 0:8], sl, cols[:, c0:c0 + 1],
                     ALU.is_equal, waits=[q1], keep_same_engine=True, mark=True)
            q3 = vop("scalar_tensor_tensor", mq[:, 8:16], mq[:, 0:8], -1e30, sl,
                     ALU.mult, ALU.add, waits=[q2], keep_same_engine=True, mark=True)
            q4 = vop("reduce_max", cols[:, c0 + 1:c0 + 2], mq[:, 8:16], axis=AX.X,
                     waits=[q3], keep_same_engine=True, mark=True)
            q5 = vop("tensor_single_scalar", mq[:, 16:24], mq[:, 8:16],
                     cols[:, c0 + 1:c0 + 2], ALU.is_equal, waits=[q4],
                     keep_same_engine=True, mark=True)
            q6 = vop("tensor_copy", m1t[:, tc * 8: tc * 8 + 8], mq[:, 0:8], waits=[q2])
            q7 = vop("tensor_copy", m1t[:, 32 + tc * 8: 32 + tc * 8 + 8], mq[:, 16:24],
                     waits=[q5], keep_same_engine=True, mark=True)
            mt_toks.append((q6, q7))
            q8 = vop("tensor_sub", cols[:, c0 + 2:c0 + 3], cols[:, c0:c0 + 1],
                     cols[:, c0 + 1:c0 + 2], waits=[q4], keep_same_engine=True, mark=True)
            q9 = act(cols[:, c0 + 3:c0 + 4], cols[:, c0 + 2:c0 + 3], AF.Tanh, scale=0.5,
                     waits=[q8])
            qa = vop("tensor_scalar", cols[:, c0 + 4:c0 + 5], cols[:, c0 + 3:c0 + 4],
                     bct[:, 30:31], bct[:, 30:31], ALU.mult, ALU.add,
                     waits=[q9, tk_cols], mark=True)
            qb = vop("tensor_sub", cols[:, c0 + 5:c0 + 6], bct[:, 29:30],
                     cols[:, c0 + 4:c0 + 5], waits=[qa], keep_same_engine=True,
                     mark=True)
            alpha_toks.append(qb)
        bank_rd[2] = alpha_toks[-1]
        mtr = [[], []]
        for mi, (mtile_off, dstf) in enumerate(((0, m1f), (32, m2f))):
            bk = 6 + mi
            for tc in range(4):
                w = [mt_toks[tc][mi], env["idnb_"] and None] if False else [mt_toks[tc][mi]]
                if tc == 0:
                    w.append(bank_rd[bk])
                mtr[mi].append(tr(psb[0:8, bk * 1024 + tc * 128: bk * 1024 + (tc + 1) * 128],
                                  m1t[:, mtile_off + tc * 8: mtile_off + (tc + 1) * 8],
                                  idnb_[:, :], waits=w))
            ev = acopy(dstf[0:8, :], psb[0:8, bk * 1024: bk * 1024 + 512],
                       waits=mtr[mi])
            bank_rd[bk] = ev
            mtr[mi] = ev
        mrep_ev = []
        for mi, (mf, mr) in enumerate(((m1f, m1r), (m2f, m2r))):
            for hc in range(4):
                bk = hc % 2
                t = mmg(bk, psum[:, bk * 512:(bk + 1) * 512],
                        [(selb[0:8, hc * 128:(hc + 1) * 128], mf[0:8, :])],
                        waits_first=[mtr[mi], tk_sel])
                ev = acopy(mr[:, hc * 512:(hc + 1) * 512],
                           psum[:, bk * 512:(bk + 1) * 512], waits=[t])
                mrep_ev.append(ev)
                bank_rd[bk] = ev

        if DEBUG_B and ft == 0:
            toks_y.append(dma(dbgB["d_m1t"][:, :], m1t[:, :], "sync",
                              waits=[t for pair in mt_toks for t in pair]))
            toks_y.append(dma(dbgB["d_cols"][:, :], cols[:, :], "sync", waits=alpha_toks))
            toks_y.append(dma(dbgB["d_rhid"][:, :], rhid[:, :], "sync", waits=rh_ev))
            toks_y.append(dma(dbgB["d_m1r"][:, :], m1r[:, :], "sync", waits=mrep_ev))

        wvt = [warena_free]
        l1 = arena_load(wa, W1O, wscr["w1cat"], C2, HC, 16, wvt)
        l2 = arena_load(wa, W2O, wscr["w2cat"], HC, O3, 4, wvt)
        l3 = arena_load(wa, S1O, wscr["sw1"], C2, EH, 16, wvt)
        l4 = arena_load(wa, S2O, wscr["sw2p"], EH + 1, O3, 1, wvt)
        l5 = arena_load(wa, B2O, wscr["b2cat"], E, O3, 1, wvt)
        lwt = [l1, l2, l3, l4, l5]

        ctrl_last = None
        for it in range(2):
            ncc = 8 if it == 0 else 16
            if it == 1:
                w_iv = [iup_last, vup_last]
                if DEBUG_B and ft == 0:
                    t_i0 = dma(dbgB["d_i0"][:, :], itm[:, :], "sync", waits=[iup_last])
                    t_v0 = dma(dbgB["d_v0"][:, :], vtm2[:, :], "sync", waits=[vup_last])
                    toks_y += [t_i0, t_v0]
                    w_iv += [t_i0, t_v0]
                ifm_ev = to_fm(itm, 0, w_iv)
                vfm_ev = to_fm(vtm2, 8, w_iv)
                ctx_ev = ifm_ev + vfm_ev
            else:
                ctx_ev = y1fm_ev
            h_ev = []
            for hc in range(4):
                bk = hc % 2
                parts = [(wa[:, W1O + cb * 512 + hc * 128: W1O + cb * 512 + (hc + 1) * 128],
                          ctxf[:, cb * 512:(cb + 1) * 512]) for cb in range(ncc)]
                t = mmg(bk, psum[:, bk * 512:(bk + 1) * 512], parts,
                        waits_first=[l1] + ctx_ev)
                ev = act(hf[:, hc * 512:(hc + 1) * 512], psum[:, bk * 512:(bk + 1) * 512],
                         AF.Gelu, bias=bct[:, 4 + hc:5 + hc], waits=[t, tk_b])
                h_ev.append(ev)
                bank_rd[bk] = ev
            parts = [(wa[0:128, S1O + cb * 64: S1O + (cb + 1) * 64],
                      ctxf[:, cb * 512:(cb + 1) * 512]) for cb in range(ncc)]
            t = mmg(2, psum[0:64, 2 * 512: 2 * 512 + 512], parts,
                    waits_first=[l3] + ctx_ev)
            hsh_ev = act(hsh[0:64, :], psum[0:64, 2 * 512: 2 * 512 + 512], AF.Gelu,
                         bias=sbt[:, 0:1], waits=[t, tk_b, tk_hones])
            bank_rd[2] = hsh_ev
            hm1 = vop("tensor_mul", h1f[:, :], hf[:, :], m1r[:, :],
                      waits=h_ev + mrep_ev)
            hm2 = vop("tensor_mul", h2f[:, :], hf[:, :], m2r[:, :],
                      waits=h_ev + mrep_ev, keep_same_engine=True, mark=True)
            hh_ev = []
            for hc in range(2):
                parts = [(rha[:, HWO + cb * 256 + hc * 128: HWO + cb * 256 + (hc + 1) * 128],
                          ctxf[:, cb * 512:(cb + 1) * 512]) for cb in range(8)]
                t = mmg(3, psum[:, 3 * 512: 3 * 512 + 512], parts,
                        waits_first=[tk_rh] + ctx_ev)
                ev = act(hhid[:, hc * 512:(hc + 1) * 512], psum[:, 3 * 512: 3 * 512 + 512],
                         AF.Gelu, bias=bct[:, 2 + hc:3 + hc], waits=[t, tk_b])
                hh_ev.append(ev)
                bank_rd[3] = ev
            if DEBUG_B and ft == 0 and it == 0:
                toks_y.append(dma(dbgB["d_hf"][:, :], hf[:, :], "sync", waits=h_ev))
                toks_y.append(dma(dbgB["d_hsh"][:, :], hsh[:, :], "sync", waits=[hsh_ev]))
                toks_y.append(dma(dbgB["d_h1f"][:, :], h1f[:, :], "sync", waits=[hm1]))
            hc_mm = []
            for tc in range(4):
                sl = psum[:, 2 * 512 + 16 + tc: 2 * 512 + 17 + tc]
                parts = [(hhid[:, hc * 512 + tc * 128: hc * 512 + (tc + 1) * 128],
                          rha[:, HW2O + hc: HW2O + hc + 1]) for hc in range(2)]
                hc_mm.append(mmg(2, sl, parts, waits_first=hh_ev))
            hcol_toks = []
            for tc in range(4):
                sl = psum[:, 2 * 512 + 16 + tc: 2 * 512 + 17 + tc]
                th = act(cols[:, tc * 8 + 6: tc * 8 + 7], sl, AF.Tanh, scale=0.5,
                         bias=bct[:, 26:27], waits=hc_mm + [t1])
                u1 = vop("tensor_scalar", cols[:, tc * 8 + 7: tc * 8 + 8],
                         cols[:, tc * 8 + 6: tc * 8 + 7], 0.5, 0.5, ALU.mult, ALU.add,
                         waits=[th], mark=True)
                u2 = vop("tensor_scalar", mq[:, 24 + tc: 25 + tc],
                         cols[:, tc * 8 + 6: tc * 8 + 7], 0.05, 0.05, ALU.mult, ALU.add,
                         waits=[th], mark=True)
                hcol_toks.append((u1, u2))
            bank_rd[2] = hcol_toks[-1][0]
            if DEBUG_B and ft == 0 and it == 0:
                toks_y.append(dma(dbgB["d_mq"][:, :], mq[:, :], "sync",
                                  waits=[t for p_ in hcol_toks for t in p_]))
                toks_y.append(dma(dbgB["d_ctxf"][0:128, 0:8192], ctxf[:, 0:8192],
                                  "sync", waits=ctx_ev))
                toks_y.append(dma(dbgB["d_cols2"][:, :], cols[:, :], "sync",
                                  waits=[t for p_ in hcol_toks for t in p_]))
            for tc in range(4):
                def ewb(i):
                    return ew[:, i * 512:(i + 1) * 512]
                for ch in range(2):
                    c0 = tc * 1024 + ch * 512
                    dtok = vop("tensor_sub", ewb(0), itm[:, c0:c0 + 512],
                               murep[:, ch * 512:(ch + 1) * 512],
                               waits=[icp if it == 0 else iup_last, ld[0]])
                    ev_parts = {}
                    s_wave = [("sh", "a"), ("sh", "g"), ("r1", "a"), ("r1", "g"),
                              ("r2", "a"), ("r2", "g")]
                    e_wave = [("sh", "b"), ("r1", "b"), ("r2", "b")]
                    order = (s_wave + e_wave) if (tc * 2 + ch) % 2 == 0 else                             (e_wave + s_wave)
                    for gi, (br, pname) in enumerate(order):
                        bi = ("sh", "r1", "r2").index(br)
                        pi = ("a", "b", "g").index(pname)
                        bk = 4 + gi % 2
                        war = [ew_war[0]] if (gi == 0 and ew_war[0] is not None) else []
                        off = pi * 1024 + ch * 512
                        if br == "sh":
                            parts = [(hsh[0:65, tc * 128:(tc + 1) * 128],
                                      wa[0:65, S2O + off: S2O + off + 512])]
                            wf = [l4, hsh_ev]
                        else:
                            mfm = m1f if br == "r1" else m2f
                            hxf = h1f if br == "r1" else h2f
                            hmt = hm1 if br == "r1" else hm2
                            parts = [(hxf[:, hc2 * 512 + tc * 128: hc2 * 512 + (tc + 1) * 128],
                                      wa[:, W2O + hc2 * 3072 + off: W2O + hc2 * 3072 + off + 512])
                                     for hc2 in range(4)]
                            parts.append((mfm[0:8, tc * 128:(tc + 1) * 128],
                                          wa[0:8, B2O + off: B2O + off + 512]))
                            wf = [l2, l5, hmt, mtr[0 if br == "r1" else 1]]
                        t = mmg(bk, psum[:, bk * 512:(bk + 1) * 512], parts,
                                waits_first=wf)
                        bufi = {"a": 1, "b": 7, "g": 4}[pname] + bi
                        if pname == "b":
                            e1 = act(ewb(bufi), psum[:, bk * 512:(bk + 1) * 512],
                                     AF.Exp, waits=[t] + war)
                            e2 = act(ewb(bufi), ewb(bufi), AF.Ln, bias=1.0,
                                     waits=[e1])
                            ev_parts[(br, pname)] = e2
                            bank_rd[bk] = e1
                        else:
                            e1 = act(ewb(bufi), psum[:, bk * 512:(bk + 1) * 512],
                                     AF.Sigmoid, waits=[t] + war)
                            ev_parts[(br, pname)] = e1
                            bank_rd[bk] = e1
                    if DEBUG_B and ft == 0 and it == 0 and tc == 0 and ch == 0:
                        evp = list(ev_parts.values())
                        for dn, bi_ in (("d_sa", 1), ("d_sb", 7), ("d_sg", 4),
                                        ("d_sb1", 8), ("d_sb2", 9)):
                            toks_y.append(dma(dbgB[dn][:, :], ewb(bi_), "sync", waits=evp))
                        dbg_ev = toks_y[-5:]
                        acopy(ew[0:1, 0:1], ew[0:1, 0:1], waits=dbg_ev)
                    else:
                        dbg_ev = []
                    wv_ = [dtok] + dbg_ev + list(ev_parts.values()) + [alpha_toks[tc],
                                                              hcol_toks[tc][0], hcol_toks[tc][1]]
                    ch_toks = []
                    first = True
                    for bi, br in enumerate(("sh", "r1", "r2")):
                        acol = (bct[:, 28:29] if br == "sh" else
                                cols[:, tc * 8 + 4 + (0 if br == "r1" else 1):
                                     tc * 8 + 5 + (0 if br == "r1" else 1)])
                        SA, SB, SG = ewb(1 + bi), ewb(7 + bi), ewb(4 + bi)
                        if it == 0:
                            na = mq[:, 28 + bi:29 + bi]
                            natok = vop("tensor_scalar_mul", na, acol, -1.0,
                                        waits=wv_ if first else [],
                                        keep_same_engine=True, mark=True)
                            ch_toks.append(natok)
                            vnp = ewb(12 + bi)
                            ch_toks.append(vop("scalar_tensor_tensor", vnp, ewb(0),
                                               na, SB, ALU.mult, ALU.mult,
                                               waits=[natok], keep_same_engine=True,
                                               mark=True))
                        else:
                            A, B = ewb(10), ewb(11)
                            ch_toks.append(vop("scalar_tensor_tensor", A,
                                               vtm2[:, c0:c0 + 512], acol, SA,
                                               ALU.mult, ALU.mult,
                                               waits=(wv_ + [vup_last]) if first else [],
                                               keep_same_engine=True, mark=first))
                            ch_toks.append(vop("scalar_tensor_tensor", B, ewb(0),
                                               acol, SB, ALU.mult, ALU.mult, waits=[],
                                               keep_same_engine=True, mark=True))
                            vnp = ewb(12 + bi)
                            ch_toks.append(vop("tensor_sub", vnp, A, B, waits=[],
                                               keep_same_engine=True, mark=True))
                        first = False
                        if bi == 0:
                            ch_toks.append(vop("tensor_mul", ewb(15), SG, vnp, waits=[],
                                               keep_same_engine=True, mark=True))
                        else:
                            ch_toks.append(vop("tensor_mul", ewb(10), SG, vnp, waits=[],
                                               keep_same_engine=True, mark=True))
                            ch_toks.append(vop("tensor_add", ewb(15), ewb(15), ewb(10),
                                               waits=[], keep_same_engine=True, mark=True))
                    ch_toks.append(vop("tensor_add", ewb(11), ewb(12), ewb(13), waits=[],
                                       keep_same_engine=True, mark=True))
                    ch_toks.append(vop("tensor_add", ewb(11), ewb(11), ewb(14), waits=[],
                                       keep_same_engine=True, mark=True))
                    if DEBUG_B and ft == 0 and it == 0 and tc == 0 and ch == 0:
                        tvn = dma(dbgB["d_vn"][:, :], ewb(11), "sync", waits=[ch_toks[-1]])
                        tgg = dma(dbgB["d_g"][:, :], ewb(15), "sync", waits=[ch_toks[-1]])
                        for dn, bi_ in (("d_vn0", 12), ("d_vn1", 13), ("d_vn2", 14)):
                            toks_y.append(dma(dbgB[dn][:, :], ewb(bi_), "sync",
                                              waits=[ch_toks[-1]]))
                        tvn2 = toks_y[-3:]
                        toks_y += [tvn, tgg]
                        tvn = tvn; tgg = tgg
                        ch_toks.append(vop("tensor_copy", ew[0:1, 0:1], ew[0:1, 0:1],
                                           waits=[tvn, tgg] + tvn2))
                    iup = vop("scalar_tensor_tensor", itm[:, c0:c0 + 512], ewb(15),
                              mq[:, 24 + tc: 25 + tc], itm[:, c0:c0 + 512],
                              ALU.mult, ALU.add, waits=[hcol_toks[tc][1]],
                              keep_same_engine=True, mark=True)
                    if it == 0:
                        vup = vop("tensor_scalar_mul", vtm2[:, c0:c0 + 512], ewb(11),
                                  cols[:, tc * 8 + 7: tc * 8 + 8],
                                  waits=[hcol_toks[tc][0]],
                                  keep_same_engine=True, mark=True)
                    else:
                        ch_toks.append(vop("tensor_sub", ewb(10), ewb(11),
                                           vtm2[:, c0:c0 + 512], waits=[],
                                           keep_same_engine=True, mark=True))
                        vup = vop("scalar_tensor_tensor", vtm2[:, c0:c0 + 512], ewb(10),
                                  cols[:, tc * 8 + 7: tc * 8 + 8], vtm2[:, c0:c0 + 512],
                                  ALU.mult, ALU.add, waits=[hcol_toks[tc][0]],
                                  keep_same_engine=True, mark=True)
                    iup_last, vup_last = iup, vup
                    ew_war[0] = vup
            ctrl_last = iup_last

        if DEBUG_B and ft == 0:
            t_i1 = dma(dbgB["d_i1"][:, :], itm[:, :], "sync", waits=[iup_last])
            t_v1 = dma(dbgB["d_v1"][:, :], vtm2[:, :], "sync", waits=[vup_last])
            toks_y += [t_i1, t_v1]
            dbg_iv = [t_i1, t_v1]
        else:
            dbg_iv = []

        lr1 = arena_load(wa, R1O, wscr["refw1"], D, C2, 8, [ctrl_last] + dbg_iv)
        lr2 = arena_load(wa, R2O, wscr["refw2"], C2, D, 16, [ctrl_last])
        rfm_ev = to_fm(itm, 0, [iup_last] + dbg_iv)
        def hidr_plane(hc):
            tens = (hf, h1f, h2f, m1r)[hc // 4]
            return tens[:, (hc % 4) * 512:((hc % 4) + 1) * 512]
        hr_ev = []
        for hc in range(16):
            bk = hc % 2
            parts = [(wa[:, R1O + cb * 2048 + hc * 128: R1O + cb * 2048 + (hc + 1) * 128],
                      ctxf[:, cb * 512:(cb + 1) * 512]) for cb in range(8)]
            t = mmg(bk, psum[:, bk * 512:(bk + 1) * 512], parts,
                    waits_first=[lr1] + rfm_ev + ([hm1, hm2] if hc == 4 else []) +
                                (mrep_ev if hc == 12 else []))
            ev = act(hidr_plane(hc), psum[:, bk * 512:(bk + 1) * 512], AF.Gelu,
                     bias=bct[:, 8 + hc:9 + hc],
                     waits=[t, tk_b] + ([hm1, hm2] if hc in (4, 8, 12) else []))
            hr_ev.append(ev)
            bank_rd[bk] = ev
        if DEBUG_B and ft == 0:
            hr_t = dma(dbgB["d_hidr"][:, 0:2048], hf[:, :], "sync", waits=hr_ev)
            hr_t2 = dma(dbgB["d_hidr"][:, 2048:4096], h1f[:, :], "sync", waits=hr_ev)
            hr_t3 = dma(dbgB["d_hidr"][:, 4096:6144], h2f[:, :], "sync", waits=hr_ev)
            hr_t4 = dma(dbgB["d_hidr"][:, 6144:8192], m1r[:, :], "sync", waits=hr_ev)
            toks_y += [hr_t, hr_t2, hr_t3, hr_t4]
        ln_in_toks = []
        for tc in range(4):
            for chf in range(2):
                bk = 2 + (2 * tc + chf) % 2
                c0 = tc * 1024 + chf * 512
                parts = [(hidr_plane(hc)[:, tc * 128:(tc + 1) * 128],
                          wa[:, R2O + hc * 1024 + chf * 512: R2O + hc * 1024 + chf * 512 + 512])
                         for hc in range(16)]
                parts.append((onesb[0:1, 0:128], rows[0:1, chf * 512: (chf + 1) * 512]))
                t = mmg(bk, psum[:, bk * 512:(bk + 1) * 512], parts,
                        waits_first=hr_ev + [lr2, tk_rows])
                u1 = vop("tensor_mul", ew[:, 0:512], psum[:, bk * 512:(bk + 1) * 512],
                         iwrep[:, (c0 % 1024):(c0 % 1024) + 512], waits=[t, ld[1]])
                if DEBUG_B and ft == 0 and tc == 0 and chf == 0:
                    t_ri = dma(dbgB["d_refiw"][:, :], ew[:, 0:512], "sync", waits=[u1])
                    toks_y.append(t_ri)
                    u1 = t_ri
                u2 = vop("scalar_tensor_tensor", y1t[:, c0:c0 + 512], y1t[:, c0:c0 + 512],
                         2.0, ew[:, 0:512], ALU.mult, ALU.add, waits=[u1],
                         keep_same_engine=True, mark=True)
                ln_in_toks.append(u2)
                bank_rd[bk] = u2
        if DEBUG_B and ft == 0:
            t_li = dma(dbgB["d_ln2in"][:, :], y1t[:, :], "sync", waits=ln_in_toks)
            toks_y.append(t_li)
            ln_in_toks = ln_in_toks + [t_li]
        lnf = None
        lnaf = None
        for tc in range(4):
            s1ap = y1t[:, tc * 1024:(tc + 1) * 1024]
            w = [ln_in_toks[2 * tc], ln_in_toks[2 * tc + 1]] + ([lnf] if lnf else [])
            b1 = vop("bn_stats", lnst[:, 0:6], s1ap[:, 0:512], waits=w, mark=True)
            b2 = vop("bn_stats", lnst[:, 6:12], s1ap[:, 512:1024], waits=w, mark=True)
            b3 = vop("bn_aggr", lnst[:, 12:14], lnst[:, 0:12],
                     waits=[b1, b2] + ([lnaf] if lnaf else []), keep_same_engine=True)
            l1_ = act(lnst[:, 14:15], lnst[:, 13:14], AF.Ln, bias=cstf[:, 1:2],
                      waits=[b3])
            l2_ = act(lnst[:, 15:16], lnst[:, 14:15], AF.Exp, scale=-0.5, waits=[l1_])
            lnaf = l1_
            yn = vop("tensor_scalar", s1ap, s1ap, lnst[:, 12:13], lnst[:, 15:16],
                     ALU.subtract, ALU.mult, waits=[l2_])
            lnf = yn
            yr = y_out.rearrange("(t p) c -> t p c", p=128)
            yst = dma(yr[(tbase // 128) + tc], s1ap, "sync", waits=[yn])
            toks_y.append(yst)
        prev_tile_done = toks_y[-1]
        warena_free = ln_in_toks[-1]
    return toks_y


def _run(inputs):
    consts = _host_consts()
    lay = _host_layout(inputs)
    x = np.asarray(inputs["L0_attn_x"])
    nc = build_nc()
    in_maps = []
    for c in range(N_CORES):
        m = dict(lay)
        m.update(consts)
        m["x"] = np.ascontiguousarray(x[2 * c:2 * c + 2].reshape(TOK, D))
        in_maps.append(m)
    res = run_bass_kernel_spmd(nc, in_maps, core_ids=list(range(N_CORES)))
    return res


def kernel(**inputs):
    res = _run(inputs)
    return np.concatenate(
        [res.results[c]["y"].reshape(2, L, D) for c in range(N_CORES)], 0)


# version 07
# speedup vs baseline: 1.1285x; 1.1285x over previous
import numpy as np
import ml_dtypes
import contextlib
import concourse.bass as bass
import concourse.mybir as mybir
from concourse.bass_utils import run_bass_kernel_spmd
from concourse.ap import AP

F32 = mybir.dt.float32
BF16 = mybir.dt.bfloat16
AF = mybir.ActivationFunctionType
ALU = mybir.AluOpType
AX = mybir.AxisListType
BF = ml_dtypes.bfloat16

N_CORES = 8
SEQS = 2
L = 2048
D = 1024
H = 16
HD = 64
TOK = SEQS * L
CSH = 24.0
EPS = 1e-5
NBK = 32
TBL = 4095
GW = 3968
E = 8
EH = 64
C2 = 2048
HC = 512
O3 = 3072
FT = 512
NFT = TOK // FT
DT_ = 0.1

PHASE_B = True
DEBUG_Y1 = False
DEBUG_ATT = False
DEBUG_S1 = False
BARRIER_OPROJ = False
DEBUG_B = False

HL = H * 65



def _bucket_of_delta(delta):
    n = np.asarray(delta, np.int64)
    out = np.zeros_like(n)
    small = (n >= 1) & (n < 16)
    out[small] = n[small]
    big = n >= 16
    nf = n[big].astype(np.float32)
    v = np.log(nf / np.float32(16.0)) / np.float32(np.log(128.0 / 16.0)) * np.float32(16.0)
    out[big] = np.minimum(16 + v.astype(np.int32), 31)
    return out


def _host_consts():
    c = {}
    c["idn32"] = np.eye(128, dtype=np.float32)
    c["idnb"] = np.eye(128, dtype=np.float32).astype(BF)
    m = np.arange(TBL)
    bk = _bucket_of_delta(2047 - m)
    oh = np.zeros((NBK, TBL), np.float32)
    oh[bk, m] = 1.0
    c["ohrev"] = oh.astype(BF)
    sel = np.zeros((E, HC), np.float32)
    for e in range(E):
        sel[e, e * EH:(e + 1) * EH] = 1.0
    c["sel"] = sel.astype(BF)
    c["cstf"] = np.ascontiguousarray(
        np.broadcast_to(np.array([[-CSH, EPS]], np.float32), (128, 2)))
    return c


def _host_layout(inp):
    a = {}
    g = lambda k: np.asarray(inp[k])
    for nm in ("wq", "wk", "wv", "wo"):
        a[nm] = np.ascontiguousarray(g("L0_attn_" + nm))
    a["bqT"] = np.ascontiguousarray(g("L0_attn_bq").reshape(8, 128).T)
    a["bkT"] = np.ascontiguousarray(g("L0_attn_bk").reshape(8, 128).T)
    a["bv_row"] = np.ascontiguousarray(g("L0_attn_bv").reshape(1, D))
    a["bo_row"] = np.ascontiguousarray(g("L0_attn_bo").reshape(1, D))
    a["rel_bias"] = np.ascontiguousarray(g("L0_attn_rel_bias"))
    a["rw1"] = np.ascontiguousarray(g("L0_ffn_router_w1"))
    a["rb1T"] = np.ascontiguousarray(g("L0_ffn_router_b1").reshape(2, 128).T)
    a["rw2"] = np.ascontiguousarray(g("L0_ffn_router_w2"))
    a["rb2_row"] = np.ascontiguousarray(g("L0_ffn_router_b2").reshape(1, E))
    a["hw1"] = np.ascontiguousarray(g("L0_ffn_halt_w1"))
    a["hb1T"] = np.ascontiguousarray(g("L0_ffn_halt_b1").reshape(2, 128).T)
    a["hw2"] = np.ascontiguousarray(g("L0_ffn_halt_w2"))
    a["hb2rep"] = np.ascontiguousarray(
        np.broadcast_to(g("L0_ffn_halt_b2").reshape(1, 1), (128, 1)))
    ew1 = g("L0_ffn_expert_w1")
    a["w1cat"] = np.ascontiguousarray(np.transpose(ew1, (1, 0, 2)).reshape(C2, HC))
    a["b1catT"] = np.ascontiguousarray(g("L0_ffn_expert_b1").reshape(-1).reshape(4, 128).T)
    a["w2cat"] = np.ascontiguousarray(g("L0_ffn_expert_w2").reshape(HC, O3))
    a["b2cat"] = np.ascontiguousarray(g("L0_ffn_expert_b2"))
    a["sw1"] = np.ascontiguousarray(g("L0_ffn_shared_w1"))
    a["sb1T"] = np.ascontiguousarray(g("L0_ffn_shared_b1").reshape(64, 1))
    a["sw2p"] = np.ascontiguousarray(
        np.concatenate([g("L0_ffn_shared_w2"), g("L0_ffn_shared_b2").reshape(1, O3)], 0))
    a["swrep"] = np.ascontiguousarray(
        np.broadcast_to(np.asarray(g("L0_ffn_shared_weight"), np.float32).reshape(1, 1), (128, 1)))
    a["refw1"] = np.ascontiguousarray(g("L0_ffn_refine_w1"))
    a["refb1T"] = np.ascontiguousarray(g("L0_ffn_refine_b1").reshape(16, 128).T)
    a["refw2"] = np.ascontiguousarray(g("L0_ffn_refine_w2"))
    a["refb2_row"] = np.ascontiguousarray(g("L0_ffn_refine_b2").reshape(1, D))
    a["mu_rep"] = np.ascontiguousarray(np.broadcast_to(g("L0_ffn_mu").reshape(1, D), (128, D)))
    a["iw_rep"] = np.ascontiguousarray(
        np.broadcast_to(g("L0_ffn_integration_weight").reshape(1, D), (128, D)))
    return a



ENGS = ("tensor", "vector", "scalar", "gpsimd", "sync")
DMA_ENGS = ("gpsimd", "sync")


class Tok:
    __slots__ = ("eng", "idx")
    def __init__(self, eng, idx):
        self.eng, self.idx = eng, idx


class Prog:
    def __init__(self, nc):
        self.nc = nc
        self.ops = {e: [] for e in ENGS}

    def op(self, eng, fn, waits=(), mark=False, keep_same_engine=False):
        ws = []
        for w in waits:
            if w is None or (w.eng == eng and eng not in DMA_ENGS
                             and not keep_same_engine):
                continue
            ws.append(w)
        self.ops[eng].append([fn, ws, mark, 0])
        return Tok(eng, len(self.ops[eng]) - 1)

    def emit(self, block, stack, final_toks):
        sems = {e: stack.enter_context(self.nc.semaphore("sem_" + e)) for e in ENGS}
        for e in DMA_ENGS:
            for rec in self.ops[e]:
                rec[2] = True
        for e in ENGS:
            for rec in self.ops[e]:
                for w in rec[1]:
                    self.ops[w.eng][w.idx][2] = True
        for w in final_toks:
            self.ops[w.eng][w.idx][2] = True
        for e in ENGS:
            acc = 0
            step = 16 if e in DMA_ENGS else 1
            for rec in self.ops[e]:
                if rec[2]:
                    acc += step
                rec[3] = acc

        def mk_stream(e):
            ops = self.ops[e]
            step = 16 if e in DMA_ENGS else 1
            def body(eng):
                for fn, ws, marked, val in ops:
                    byeng = {}
                    for w in ws:
                        wv = self.ops[w.eng][w.idx][3]
                        byeng[w.eng] = max(byeng.get(w.eng, 0), wv)
                    for we, wv in sorted(byeng.items()):
                        eng.wait_ge(sems[we], wv)
                    inst = fn(eng)
                    if marked:
                        inst.then_inc(sems[e], step)
            return body

        for e in ENGS:
            if self.ops[e]:
                getattr(block, e)(mk_stream(e))
        byeng = {}
        for w in final_toks:
            byeng[w.eng] = max(byeng.get(w.eng, 0), self.ops[w.eng][w.idx][3])
        def fin(eng):
            for we, wv in sorted(byeng.items()):
                if we != "sync":
                    eng.wait_ge(sems[we], wv)
            if "sync" in byeng:
                eng.wait_ge(sems["sync"], byeng["sync"])
        block.sync(fin)



def build_nc():
    nc = bass.Bass("TRN2", target_bir_lowering=False, debug=False)
    dt = nc.dram_tensor

    x_in = dt("x", [TOK, D], F32, kind="ExternalInput").ap()
    y_out = dt("y", [TOK, D], F32, kind="ExternalOutput").ap()
    y1scr = dt("y1scr", [TOK, D], F32, kind="Internal").ap()
    y1dbg = dt("y1dbg", [TOK, D], F32, kind="ExternalOutput").ap() if DEBUG_Y1 else None
    if DEBUG_ATT:
        dbg = {nm: dt(nm, [128, 8 * L], BF16, kind="ExternalOutput").ap()
               for nm in ("qdbg", "kdbg", "ctxdbg")}
        dbg["vdbg"] = dt("vdbg", [128, 16 * HL], BF16, kind="ExternalOutput").ap()
        dbg["crawdbg"] = dt("crawdbg", [128, 1024], F32, kind="ExternalOutput").ap()
        dbg["srowdbg"] = dt("srowdbg", [1, 1024], F32, kind="ExternalOutput").ap()
    s1dbg = dt("s1dbg", [TOK, D], F32, kind="ExternalOutput").ap() if DEBUG_S1 else None
    lndbg = dt("lndbg", [32, 128, 16], F32, kind="ExternalOutput").ap() if DEBUG_S1 else None

    hb = {}
    for nm, sh, dd in [
            ("wq", (D, D), F32), ("wk", (D, D), F32), ("wv", (D, D), F32), ("wo", (D, D), F32),
            ("bqT", (128, 8), F32), ("bkT", (128, 8), F32),
            ("bv_row", (1, D), F32), ("bo_row", (1, D), F32), ("rel_bias", (NBK, H), F32),
            ("rw1", (D, 256), F32), ("rb1T", (128, 2), F32), ("rw2", (256, E), F32),
            ("rb2_row", (1, E), F32),
            ("hw1", (D, 256), F32), ("hb1T", (128, 2), F32), ("hw2", (256, 1), F32),
            ("hb2rep", (128, 1), F32),
            ("w1cat", (C2, HC), F32), ("b1catT", (128, 4), F32), ("w2cat", (HC, O3), F32),
            ("b2cat", (E, O3), F32),
            ("sw1", (C2, EH), F32), ("sb1T", (EH, 1), F32), ("sw2p", (EH + 1, O3), F32),
            ("swrep", (128, 1), F32),
            ("refw1", (D, C2), F32), ("refb1T", (128, 16), F32), ("refw2", (C2, D), F32),
            ("refb2_row", (1, D), F32),
            ("mu_rep", (128, D), F32), ("iw_rep", (128, D), F32),
            ("idn32", (128, 128), F32), ("idnb", (128, 128), BF16),
            ("cstf", (128, 2), F32),
            ("ohrev", (NBK, TBL), BF16), ("sel", (E, HC), BF16)]:
        hb[nm] = dt(nm, list(sh), dd, kind="ExternalInput").ap()

    WSCR_SHAPES = {"w1cat": (C2, HC), "w2cat": (HC, O3), "sw1": (C2, EH),
                   "sw2p": (EH + 1, O3), "rw1": (D, 256), "hw1": (D, 256),
                   "refw1": (D, C2), "refw2": (C2, D), "rw2": (256, E),
                   "hw2": (256, 1), "b2cat": (E, O3)}
    wscr = {nm: dt("scr_" + nm, list(sh), BF16, kind="Internal").ap()
            for nm, sh in WSCR_SHAPES.items()}

    stack = contextlib.ExitStack()
    stackA = contextlib.ExitStack()
    sb = {}
    def sbuf(name, shape, dtype, st=None):
        sb[name] = (st or stack).enter_context(nc.sbuf_tensor(name, list(shape), dtype))
        return sb[name]
    def sbufA(name, shape, dtype):
        return sbuf(name, shape, dtype, st=stackA)

    onesb = sbuf("onesb", [1, 128], BF16)

    lnst  = sbuf("lnst",  [128, 16], F32)
    idn32 = sbuf("idn32s", [128, 128], F32)
    idnb_ = sbuf("idnbs", [128, 128], BF16)
    cstf  = sbuf("cstfs", [128, 2], F32)
    xbt   = sbufA("xbt",   [128, D], F32)
    xfm   = sbufA("xfm",   [128, 8 * L], BF16)
    qfm   = sbufA("qfm",   [128, 8 * L], BF16)
    kfm   = sbufA("kfm",   [128, 8 * L], BF16)
    vtm   = sbufA("vtm",   [128, 16 * HL], BF16)
    wtm   = sbufA("wtm",   [128, 2 * D], F32)
    wT    = sbufA("wT",    [128, 8 * D], BF16)
    erev  = sbufA("erev",  [H, TBL], BF16)
    ptile = sbufA("ptile", [128, 2 * 1024], BF16)
    craw  = sbufA("craw",  [128, 1024], F32)
    xf32  = sbufA("xf32",  [128, 2 * 1024], F32)
    srow  = sbufA("srow",  [1, 1024], F32)
    scol  = sbufA("scol",  [128, 16], F32)
    brow  = sbufA("brow",  [1, 2 * D], BF16)
    bias8 = sbufA("bias8", [128, 24], F32)
    relb  = sbufA("relb",  [NBK, H], BF16)
    relb32 = sbufA("relb32", [NBK, H], F32)
    ones32 = sbufA("ones32", [1, 64], F32)

    psum = stack.enter_context(nc.psum_tensor("PS", [128, 4096], F32))
    psb = psum[:, :].bitcast(BF16)

    def pbank(i, w=1):
        return psum[:, i * 512:(i + w) * 512]

    def pbankb(i, off, w):
        return psb[:, i * 1024 + off: i * 1024 + off + w]

    P = Prog(nc)

    def dma(dst, src, eng="sync", waits=(), slow=False):
        def fn(e, dst=dst, src=src, slow=slow):
            return e.dma_start(out=dst, in_=src, allow_slow_non_contiguous=slow)
        return P.op(eng, fn, waits)

    def mm(out, lhsT, rhs, start, stop, waits=()):
        def fn(e, out=out, lhsT=lhsT, rhs=rhs, start=start, stop=stop):
            return e.matmul(out, lhsT, rhs, start=start, stop=stop)
        return P.op("tensor", fn, waits)

    def tr(out, in_, ident, waits=()):
        def fn(e, out=out, in_=in_, ident=ident):
            return e.transpose(out, in_, ident)
        return P.op("tensor", fn, waits)

    def act(out, in_, func, bias=0.0, scale=1.0, waits=()):
        def fn(e, out=out, in_=in_, func=func, bias=bias, scale=scale):
            return e.activation(out, in_, func, bias=bias, scale=scale)
        return P.op("scalar", fn, waits)

    def acopy(out, in_, waits=()):
        def fn(e, out=out, in_=in_):
            return e.copy(out, in_)
        return P.op("scalar", fn, waits)

    def vop(name, *args, waits=(), mark=False, keep_same_engine=False, **kw):
        def fn(e, name=name, args=args, kw=kw):
            return getattr(e, name)(*args, **kw)
        return P.op("vector", fn, waits, mark=mark, keep_same_engine=keep_same_engine)

    tk_id32 = dma(idn32[:, :], hb["idn32"][:, :])
    tk_cst = dma(cstf[:, :], hb["cstf"][:, :])
    tk_idb = dma(idnb_[:, :], hb["idnb"][:, :])
    ohv_stack = contextlib.ExitStack()
    ohv = ohv_stack.enter_context(nc.sbuf_tensor("ohv", [NBK, TBL], BF16))
    tk_relb0 = dma(relb32[:, :], hb["rel_bias"][:, :], "sync")
    tk_relb = vop("tensor_copy", relb[:, :], relb32[:, :], waits=[tk_relb0])
    tk_oh = dma(ohv[:, :], hb["ohrev"][:, :], "sync")
    tk_ones = vop("memset", onesb[:, :], 1.0)
    tk_ones32 = vop("memset", ones32[:, :], 1.0)
    tk_srow1 = dma(srow[0:1, :], hb["bv_row"][:, :], "sync")
    tk_brow = acopy(brow[0:1, 0:D], srow[0:1, :], waits=[tk_srow1])
    tk_srow2 = dma(srow[0:1, :], hb["bo_row"][:, :], "sync", waits=[tk_brow])
    tk_brow2 = acopy(brow[0:1, D:2 * D], srow[0:1, :], waits=[tk_srow2])
    tk_srow_free = tk_brow2
    tk_b8 = dma(bias8[:, 0:8], hb["bqT"][:, :])
    tk_b8b = dma(bias8[:, 8:16], hb["bkT"][:, :])
    tk_bqs = vop("tensor_scalar_mul", bias8[:, 16:24], bias8[:, 0:8], 0.125,
                 waits=[tk_b8])

    erev_mm = []
    for j in range(8):
        c0, c1 = j * 512, min((j + 1) * 512, TBL)
        erev_mm.append(mm(psum[0:H, c0:c1], relb[:, :], ohv[:, c0:c1], True, True,
                          waits=[tk_relb, tk_oh]))
    tk_erev = act(erev[:, :], psum[0:H, 0:TBL], AF.Exp, waits=erev_mm)
    ohv_stack.close()
    gtab = sbufA("gtab", [128, 2 * GW], BF16)

    tk_vones = vop("memset", vtm[:, :], 1.0)

    pieces = []
    for nm in WSCR_SHAPES:
        rows, cols = WSCR_SHAPES[nm]
        srct, dsth = hb[nm], wscr[nm]
        nrb = (rows + 127) // 128
        for rb in range(nrb):
            r0 = rb * 128
            rr = min(128, rows - r0)
            for cc0 in range(0, cols, 1024):
                cc1 = min(cc0 + 1024, cols)
                pieces.append((srct, dsth, r0, rr, cc0, cc1))
    cast_lds = []
    cast_sts = [None, None]

    def emit_cast_load(ci):
        srct, dsth, r0, rr, cc0, cc1 = pieces[ci]
        sl = ci % 2
        w = [cast_sts[sl]] if cast_sts[sl] is not None else []
        if ci == 0:
            w.append(tk_erev)
        cast_lds.append(dma(gtab[0:rr, sl * 1024: sl * 1024 + cc1 - cc0],
                            srct[r0:r0 + rr, cc0:cc1], "gpsimd", waits=w))

    emit_cast_load(0)
    for ci in range(len(pieces)):
        srct, dsth, r0, rr, cc0, cc1 = pieces[ci]
        sl = ci % 2
        if ci + 1 < len(pieces):
            emit_cast_load(ci + 1)
            settle = cast_lds[ci + 1]
        else:
            settle = dma(scol[0:1, 8:16], hb["bo_row"][0:1, 0:8], "gpsimd")
        cast_last = dma(dsth[r0:r0 + rr, cc0:cc1],
                        gtab[0:rr, sl * 1024: sl * 1024 + cc1 - cc0], "gpsimd",
                        waits=[cast_lds[ci], settle])
        cast_sts[sl] = cast_last

    y_toks = []
    y1_toks = []

    def weight_transpose(name, first_wait, bank_waits=(None, None), wT_wait=None):
        xr = hb[name].rearrange("(a p) c -> a p c", p=128)
        evs = []
        slot_free = [None] * 2
        bank_free = list(bank_waits)
        lts = []

        def emit_load(rb):
            sl = rb % 2
            w = []
            if first_wait is not None and rb == 0:
                w.append(first_wait)
            if slot_free[sl] is not None:
                w.append(slot_free[sl])
            lts.append(dma(wtm[:, sl * D:(sl + 1) * D], xr[rb], "sync", waits=w))

        emit_load(0)
        for rb in range(8):
            sl = rb % 2
            if rb + 1 < 8:
                emit_load(rb + 1)
                settle = lts[rb + 1]
            else:
                settle = dma(scol[0:1, 0:8], hb["bo_row"][0:1, 0:8], "sync")
            trs = []
            for cb in range(8):
                bk = cb // 4
                tw = [lts[rb], tk_id32]
                if cb == 0:
                    tw.append(settle)
                if cb % 4 == 0 and bank_free[bk] is not None:
                    tw.append(bank_free[bk])
                trs.append(tr(psum[:, (6 + bk) * 512 + (cb % 4) * 128:
                                   (6 + bk) * 512 + (cb % 4) * 128 + 128],
                              wtm[:, sl * D + cb * 128: sl * D + (cb + 1) * 128],
                              idn32[:, :], waits=tw))
            slot_free[sl] = trs[-1]
            wTt = wT[:, :]
            for bk in range(2):
                dst = AP(wTt.tensor, (4 * bk) * D + rb * 128, [[8 * D, 128], [D, 4], [1, 128]])
                srcp = psum[:, (6 + bk) * 512:(6 + bk) * 512 + 512].rearrange(
                    "p (a b) -> p a b", a=4)
                ew = list(trs[bk * 4:(bk + 1) * 4])
                if rb == 0 and wT_wait is not None:
                    ew.append(wT_wait)
                ev = acopy(dst, srcp, waits=ew)
                evs.append(ev)
                bank_free[bk] = ev
        return evs[-1], evs

    prev_seq_tok = None
    for s in range(SEQS):
        xr = x_in.rearrange("(t p) c -> t p c", p=128)
        x_ev = []
        bank_free = [tk_erev, tk_erev] if s == 0 else [None, None]
        xslots = [xbt[:, :], xf32[:, 0:1024], xf32[:, 1024:2048]]
        xslot_free = [None, None, None]
        xlts = []

        def emit_xload(tt):
            sl3 = tt % 3
            w = []
            if tt == 0 and prev_seq_tok is not None:
                w.append(prev_seq_tok)
            if xslot_free[sl3] is not None:
                w.append(xslot_free[sl3])
            xlts.append(dma(xslots[sl3], xr[s * 16 + tt], "sync", waits=w))

        emit_xload(0)
        for tt in range(16):
            sl3 = tt % 3
            if tt + 1 < 16:
                emit_xload(tt + 1)
                settle = xlts[tt + 1]
            else:
                settle = dma(scol[0:1, 0:8], hb["bo_row"][0:1, 0:8], "sync")
            trs = []
            for cb in range(8):
                bk = cb // 4
                tw = [xlts[tt], tk_id32]
                if cb == 0:
                    tw.append(settle)
                if cb % 4 == 0 and bank_free[bk] is not None:
                    tw.append(bank_free[bk])
                trs.append(tr(psum[:, bk * 512 + (cb % 4) * 128:
                                   bk * 512 + (cb % 4) * 128 + 128],
                              xslots[sl3][:, cb * 128:(cb + 1) * 128],
                              idn32[:, :], waits=tw))
            xslot_free[sl3] = trs[-1]
            xfmt = xfm[:, :]
            for bk in range(2):
                dst = AP(xfmt.tensor, (4 * bk) * L + tt * 128, [[8 * L, 128], [L, 4], [1, 128]])
                srcp = psum[:, bk * 512:bk * 512 + 512].rearrange("p (a b) -> p a b", a=4)
                ev = acopy(dst, srcp, waits=trs[bk * 4:(bk + 1) * 4])
                x_ev.append(ev)
                bank_free[bk] = ev
        xf32_guard = [xslot_free[1], xslot_free[2]]

        wq_tok, _ = weight_transpose("wq", prev_seq_tok,
            bank_waits=((tk_erev, tk_erev) if s == 0 else (prev_seq_tok, prev_seq_tok)),
            wT_wait=prev_seq_tok)
        q_ev = []
        ev_par = [None, None]
        for db in range(8):
            for t4 in range(4):
                par = (4 * db + t4) % 2
                pb = pbank(2 + 2 * par)
                ms = []
                for cb in range(8):
                    w = []
                    if cb == 0:
                        w = [wq_tok] + (x_ev if db == 0 and t4 == 0 else [])
                        if ev_par[par] is not None:
                            w.append(ev_par[par])
                    ms.append(mm(pb, wT[:, cb * D + db * 128: cb * D + (db + 1) * 128],
                                 xfm[:, cb * L + t4 * 512: cb * L + (t4 + 1) * 512],
                                 cb == 0, cb == 7, waits=w))
                ev = act(qfm[:, db * L + t4 * 512: db * L + (t4 + 1) * 512], pb, AF.Identity,
                         bias=bias8[:, 16 + db:17 + db], scale=0.125,
                         waits=[ms[-1], tk_bqs])
                q_ev.append(ev)
                ev_par[par] = ev
        wk_tok, _ = weight_transpose("wk", q_ev[-1],
            bank_waits=(q_ev[-1], q_ev[-1]), wT_wait=q_ev[-1])
        k_ev = []
        ev_par = [None, None]
        for db in range(8):
            for t4 in range(4):
                par = (4 * db + t4) % 2
                pb = pbank(2 + 2 * par)
                ms = []
                for cb in range(8):
                    w = []
                    if cb == 0:
                        w = [wk_tok]
                        if ev_par[par] is not None:
                            w.append(ev_par[par])
                        elif par == (4 * 7 + 3) % 2:
                            w.append(q_ev[-1])
                        else:
                            w.append(q_ev[-2])
                    ms.append(mm(pb, wT[:, cb * D + db * 128: cb * D + (db + 1) * 128],
                                 xfm[:, cb * L + t4 * 512: cb * L + (t4 + 1) * 512],
                                 cb == 0, cb == 7, waits=w))
                ev = act(kfm[:, db * L + t4 * 512: db * L + (t4 + 1) * 512], pb, AF.Identity,
                         bias=bias8[:, 8 + db:9 + db], scale=1.0,
                         waits=[ms[-1], tk_b8b])
                k_ev.append(ev)
                ev_par[par] = ev
        wv_tok, _ = weight_transpose("wv", k_ev[-1],
            bank_waits=(k_ev[-1], k_ev[-1]), wT_wait=k_ev[-1])
        v_ev = []
        ev_par = [None, None]
        gi = 0
        for tt in range(16):
            for dvh in range(2):
                par = gi % 2
                gi += 1
                pb = pbank(2 + 2 * par)
                ms = []
                for cb in range(8):
                    w = []
                    if cb == 0:
                        w = [wv_tok]
                        if ev_par[par] is not None:
                            w.append(ev_par[par])
                        else:
                            w.append(k_ev[-1])
                    ms.append(mm(pb, xfm[:, cb * L + tt * 128: cb * L + (tt + 1) * 128],
                                 wT[:, cb * D + dvh * 512: cb * D + (dvh + 1) * 512],
                                 cb == 0, False, waits=w))
                ms.append(mm(pb, onesb[0:1, 0:128],
                             brow[0:1, dvh * 512:(dvh + 1) * 512], False, True,
                             waits=[tk_ones, tk_brow]))
                vt = vtm[:, :]
                dst = AP(vt.tensor, tt * HL + dvh * 8 * 65,
                         [[16 * HL, 128], [65, 8], [1, 64]])
                src = pb.rearrange("p (a b) -> p a b", a=8)
                ev = acopy(dst, src, waits=[ms[-1], tk_vones])
                v_ev.append(ev)
                ev_par[par] = ev

        ctx_done = []
        g_free = [None, None]
        st_free = [None, None]
        p_free = [None, None]
        pv_free = [None, None]
        craw_free = [None, None]
        glue_free = tk_srow_free if s == 0 else None
        first_st = True
        epi = None
        m = 0

        def emit_epilogue_a(e):
            nonlocal glue_free
            hh_, qh_, pvb_, par_, pvl_ = e
            cr = craw[:, 0:1024]
            w = list(pvl_)
            if craw_free[0] is not None:
                w.append(craw_free[0])
            evt = acopy(cr[0:65, :], psum[0:65, pvb_ * 512: pvb_ * 512 + 1024], waits=w)
            return e + (evt,)

        def emit_sums(e):
            nonlocal glue_free
            hh_, qh_, pvb_, par_, pvl_, evt = e
            w = list(pvl_) + [evt]
            if glue_free is not None:
                w.append(glue_free)
            t3 = vop("reciprocal", srow[0:1, 0:1024],
                     psum[64:65, pvb_ * 512: pvb_ * 512 + 1024], waits=w)
            return e + (t3,)

        def emit_epilogue_b(e):
            nonlocal glue_free
            hh_, qh_, pvb_, par_, pvl_, evt, t3 = e
            hc_, hr_ = hh_ // 2, (hh_ % 2) * 64
            cr = craw[:, 0:1024]
            rep = []
            for jj in range(2):
                rep.append(mm(psum[0:64, (pvb_ + jj) * 512:(pvb_ + jj) * 512 + 512],
                              ones32[0:1, 0:64], srow[0:1, jj * 512:(jj + 1) * 512],
                              True, True, waits=[t3, tk_ones32, evt]))
            glue_free = rep[-1]
            nrm = []
            for jj in range(2):
                dst = xfm[hr_:hr_ + 64, hc_ * L + qh_ * 1024 + jj * 512:
                          hc_ * L + qh_ * 1024 + (jj + 1) * 512]
                w = [rep[jj], evt]
                if hh_ == 0 and qh_ == 0 and jj == 0:
                    w += v_ev
                nrm.append(vop("tensor_tensor", dst, cr[0:64, jj * 512:(jj + 1) * 512],
                               psum[0:64, (pvb_ + jj) * 512:(pvb_ + jj) * 512 + 512],
                               ALU.mult, waits=w))
            craw_free[0] = nrm[-1]
            pv_free[(pvb_ - 4) // 2] = nrm[-1]
            ctx_done.extend(nrm)

        for h in range(H):
            gpar = h % 2
            w = [tk_erev]
            if h < 2:
                w.append(q_ev[0])
                if s == 0:
                    w.append(cast_last)
            if g_free[gpar] is not None:
                w.append(g_free[gpar])
            ert = erev[:, :]
            gsrc = AP(ert.tensor, h * TBL, [[TBL, 1], [1, 128], [1, GW]])
            gtok = dma(gtab[:, gpar * GW:(gpar + 1) * GW], gsrc, "sync", waits=w)
            hc, hr = h // 2, (h % 2) * 64
            last_pm = None
            for qh in range(2):
                if epi is not None:
                    epi = emit_epilogue_a(epi)
                pvpair = m % 2
                pvb = 4 + 2 * pvpair
                pv_ms = []
                ets = {}
                pms = {}
                def emit_st(kt):
                    spar = kt % 2
                    ms = []
                    nonlocal first_st
                    for jj in range(2):
                        w = []
                        if jj == 0:
                            if first_st:
                                w = q_ev + k_ev + v_ev
                                first_st = False
                            if st_free[spar] is not None:
                                w.append(st_free[spar])
                        ms.append(mm(psum[:, (2 * spar) * 512 + jj * 512:
                                          (2 * spar) * 512 + (jj + 1) * 512],
                                     kfm[hr:hr + 64, hc * L + kt * 128: hc * L + (kt + 1) * 128],
                                     qfm[hr:hr + 64, hc * L + qh * 1024 + jj * 512:
                                         hc * L + qh * 1024 + (jj + 1) * 512],
                                     True, True, waits=w))
                    return ms
                def emit_exp_mult(kt, ms):
                    spar = kt % 2
                    ppar = kt % 2
                    w = ms + ([p_free[ppar]] if p_free[ppar] is not None else [])
                    et = act(ptile[:, ppar * 1024:(ppar + 1) * 1024],
                             psum[:, (2 * spar) * 512:(2 * spar) * 512 + 1024],
                             AF.Exp, bias=cstf[:, 0:1], waits=w)
                    st_free[spar] = et
                    off = 1024 - 1024 * qh + 128 * kt
                    gt_t = gtab[:, :]
                    grev = AP(gt_t.tensor, gpar * GW + off + 1023, [[2 * GW, 128], [-1, 1024]])
                    pm = vop("tensor_tensor",
                             ptile[:, ppar * 1024:(ppar + 1) * 1024],
                             ptile[:, ppar * 1024:(ppar + 1) * 1024],
                             grev, ALU.mult, waits=[et, gtok])
                    return pm
                def emit_pv(kt, pm):
                    ppar = kt % 2
                    for jj in range(2):
                        w = [pm]
                        if kt == 0 and pv_free[pvpair] is not None:
                            w.append(pv_free[pvpair])
                        pv_ms.append(mm(psum[0:65, (pvb + jj) * 512:(pvb + jj + 1) * 512],
                                        vtm[:, kt * HL + h * 65: kt * HL + h * 65 + 65],
                                        ptile[:, ppar * 1024 + jj * 512:
                                              ppar * 1024 + (jj + 1) * 512],
                                        kt == 0, kt == 15, waits=w))
                    p_free[ppar] = pv_ms[-1]
                ms0 = emit_st(0)
                pms[0] = emit_exp_mult(0, ms0)
                for kt in range(16):
                    if kt + 1 < 16:
                        msn = emit_st(kt + 1)
                        pms[kt + 1] = emit_exp_mult(kt + 1, msn)
                    emit_pv(kt, pms[kt])
                    if kt == 8 and epi is not None and len(epi) == 6:
                        epi = emit_sums(epi)
                last_pm = pms[15]
                if epi is not None:
                    emit_epilogue_b(epi)
                epi = (h, qh, pvb, m % 2, [pv_ms[-1], pv_ms[-2]])
                m += 1
            g_free[gpar] = last_pm
        epi = emit_epilogue_a(epi)
        epi = emit_sums(epi)
        emit_epilogue_b(epi)

        wo_tok, _ = weight_transpose("wo", v_ev[-1],
            bank_waits=(pv_free[1], pv_free[1]), wT_wait=v_ev[-1])
        xl_free = [None, None]
        ev_par = [None, None]
        lnst_free = None
        lnst_act_free = None
        first_o = True
        for tt in range(16):
            par = tt % 2
            w = []
            if xl_free[par] is not None:
                w.append(xl_free[par])
            else:
                w.append(xf32_guard[par])
            xl = dma(xf32[:, par * 1024:(par + 1) * 1024], xr[s * 16 + tt], "sync", waits=w)
            s1_toks = []
            for oh2 in range(2):
                mpar = (2 * tt + oh2) % 2
                pb = pbank(2 + 2 * mpar)
                ms = []
                for cb in range(8):
                    w = []
                    if cb == 0:
                        w = [wo_tok]
                        if first_o:
                            w += ctx_done
                            if BARRIER_OPROJ:
                                w += [pv_free, glue_free] + rep_free + \
                                     [t for t in p_free if t is not None] + \
                                     [t for t in craw_free if t is not None]
                            first_o = False
                        if ev_par[mpar] is not None:
                            w.append(ev_par[mpar])
                        else:
                            w += [t for t in st_free if t is not None]
                            w += [t for t in pv_free if t is not None]
                    ms.append(mm(pb, xfm[:, cb * L + tt * 128: cb * L + (tt + 1) * 128],
                                 wT[:, cb * D + oh2 * 512: cb * D + (oh2 + 1) * 512],
                                 cb == 0, False, waits=w))
                ms.append(mm(pb, onesb[0:1, 0:128],
                             brow[0:1, D + oh2 * 512: D + (oh2 + 1) * 512],
                             False, True, waits=[tk_brow2]))
                sl = xf32[:, par * 1024 + oh2 * 512: par * 1024 + (oh2 + 1) * 512]
                stk = vop("tensor_tensor", sl, pb, sl, ALU.add, waits=[ms[-1], xl])
                s1_toks.append(stk)
                ev_par[mpar] = stk
            s1ap = xf32[:, par * 1024:(par + 1) * 1024]
            if DEBUG_S1:
                sdr = s1dbg.rearrange("(t p) c -> t p c", p=128)
                s1_toks.append(dma(sdr[s * 16 + tt], s1ap, "sync", waits=list(s1_toks)))
                y_toks.append(s1_toks[-1])
            w = s1_toks + ([lnst_free] if lnst_free is not None else [])
            b1 = vop("bn_stats", lnst[:, 0:6], s1ap[:, 0:512], waits=w, mark=True)
            b2 = vop("bn_stats", lnst[:, 6:12], s1ap[:, 512:1024], waits=w, mark=True)
            b3 = vop("bn_aggr", lnst[:, 12:14], lnst[:, 0:12],
                     waits=[b1, b2] + ([lnst_act_free] if lnst_act_free is not None else []),
                     keep_same_engine=True)
            l1 = act(lnst[:, 14:15], lnst[:, 13:14], AF.Ln, bias=cstf[:, 1:2], waits=[b3, tk_cst])
            l2 = act(lnst[:, 15:16], lnst[:, 14:15], AF.Exp, scale=-0.5, waits=[l1])
            lnst_act_free = l1
            if DEBUG_S1:
                lnt = dma(lndbg[s * 16 + tt], lnst[:, :], "sync", waits=[l2, b3])
                y_toks.append(lnt)
            yn = vop("tensor_scalar", s1ap, s1ap, lnst[:, 12:13], lnst[:, 15:16],
                     ALU.subtract, ALU.mult,
                     waits=[l2] + ([s1_toks[-1], lnt] if DEBUG_S1 else []))
            lnst_free = yn
            ysr = y1scr.rearrange("(t p) c -> t p c", p=128)
            yst = dma(ysr[s * 16 + tt], s1ap, "sync", waits=[yn])
            y1_toks.append(yst)
            if DEBUG_Y1:
                ydr = y1dbg.rearrange("(t p) c -> t p c", p=128)
                y_toks.append(dma(ydr[s * 16 + tt], s1ap, "sync", waits=[yn]))
            xl_free[par] = yst
        prev_seq_tok = y1_toks[-1]

    stackA.close()
    if PHASE_B:
        env = dict(nc=nc, P=P, stack=stack, sbuf=sbuf, sb=sb, psum=psum, psb=psb,
                   hb=hb, wscr=wscr, y1scr=y1scr, y_out=y_out, y1_toks=y1_toks,
                   dma=dma, mm=mm, tr=tr, act=act, acopy=acopy, vop=vop,
                   onesb=onesb, idnb_=idnb_, idn32=idn32, cstf=cstf, lnst=lnst,
                   prev_seq_tok=prev_seq_tok, cast_last=cast_last, APc=AP)
        y_toks += _emit_ffn(env)

    with nc.Block() as block:
        P.emit(block, stack, y_toks if y_toks else [y1_toks[-1]])
    stack.close()
    return nc


def _emit_ffn(env):
    nc, P, sbuf = env["nc"], env["P"], env["sbuf"]
    psum, psb = env["psum"], env["psb"]
    hb, wscr, y1scr, y_out = env["hb"], env["wscr"], env["y1scr"], env["y_out"]
    y1_toks = env["y1_toks"]
    dma, mm, tr, act, acopy, vop = (env["dma"], env["mm"], env["tr"], env["act"],
                                    env["acopy"], env["vop"])
    onesb, idnb_, idn32, cstf, lnst = (env["onesb"], env["idnb_"], env["idn32"],
                                       env["cstf"], env["lnst"])
    gate = env["prev_seq_tok"]
    AP_ = env["APc"]

    y1t  = sbuf("y1t",  [128, 4096], F32)
    itm  = sbuf("itm",  [128, 4096], F32)
    vtm2 = sbuf("vtm2", [128, 4096], F32)
    ctxf = sbuf("ctxf", [128, 16 * 512], BF16)
    hf   = sbuf("hf",   [128, 4 * 512], BF16)
    h1f  = sbuf("h1f",  [128, 4 * 512], BF16)
    h2f  = sbuf("h2f",  [128, 4 * 512], BF16)
    m1r  = sbuf("m1r",  [128, 4 * 512], BF16)
    m2r  = sbuf("m2r",  [128, 4 * 512], BF16)
    hsh  = sbuf("hsh",  [65, 512], BF16)
    rhid = sbuf("rhid", [128, 2 * 512], BF16)
    hhid = sbuf("hhid", [128, 2 * 512], BF16)
    m1f  = sbuf("m1f",  [8, 512], BF16)
    m2f  = sbuf("m2f",  [8, 512], BF16)
    m1t  = sbuf("m1t",  [128, 64], BF16)
    mq   = sbuf("mq",   [128, 32], F32)
    cols = sbuf("cols", [128, 32], F32)
    ew   = sbuf("ew",   [128, 16 * 512], F32)
    wa   = sbuf("wa",   [128, 32768], BF16)
    rha  = sbuf("rha",  [128, 4116], BF16)
    murep = sbuf("murep", [128, D], F32)
    iwrep = sbuf("iwrep", [128, D], F32)
    bct  = sbuf("bct",  [128, 32], F32)
    sbt  = sbuf("sbt",  [64, 1], F32)
    rows = sbuf("rows", [1, D + 16], BF16)
    selb = sbuf("selb", [8, HC], BF16)

    W1O, W2O, S1O, S2O, B2O = 0, 8192, 20480, 21504, 24576
    R1O, R2O = 0, 16384
    RWO, HWO, RW2O, HW2O = 0, 2048, 4096, 4112

    toks_y = []
    dbgB = {}
    if DEBUG_B:
        for nm, sh in (("d_m1t", (128, 64)), ("d_cols", (128, 32)), ("d_mq", (128, 32)),
                       ("d_rhid", (128, 1024)), ("d_hf", (128, 2048)), ("d_hsh", (65, 512)),
                       ("d_h1f", (128, 2048)), ("d_m1r", (128, 2048)),
                       ("d_i0", (128, 4096)), ("d_v0", (128, 4096)),
                       ("d_i1", (128, 4096)), ("d_v1", (128, 4096)),
                       ("d_ln2in", (128, 4096)), ("d_ctxf", (128, 8192)),
                       ("d_sa", (128, 512)), ("d_sb", (128, 512)), ("d_sg", (128, 512)), ("d_vn", (128, 512)), ("d_g", (128, 512)), ("d_cols2", (128, 32)), ("d_sb1", (128, 512)), ("d_sb2", (128, 512)), ("d_refiw", (128, 512)), ("d_vn0", (128, 512)), ("d_vn1", (128, 512)), ("d_vn2", (128, 512))):
            dbgB[nm] = nc.dram_tensor(nm, list(sh), F32 if nm not in ("d_m1t", "d_hf", "d_h1f", "d_m1r", "d_ctxf", "d_rhid", "d_hsh") else BF16, kind="ExternalOutput").ap()
        dbgB["d_hidr"] = nc.dram_tensor("d_hidr", [128, 8192], BF16, kind="ExternalOutput").ap()

    ld = []
    ld.append(dma(murep[:, :], hb["mu_rep"][:, :], "sync", waits=[gate]))
    ld.append(dma(iwrep[:, :], hb["iw_rep"][:, :], "sync", waits=[gate]))
    ld.append(dma(bct[:, 0:2], hb["rb1T"][:, :], "sync", waits=[gate]))
    ld.append(dma(bct[:, 2:4], hb["hb1T"][:, :], "sync", waits=[gate]))
    ld.append(dma(bct[:, 4:8], hb["b1catT"][:, :], "sync", waits=[gate]))
    ld.append(dma(bct[:, 8:24], hb["refb1T"][:, :], "sync", waits=[gate]))
    ld.append(dma(bct[:, 24:25], hb["hb2rep"][:, :], "sync", waits=[gate]))
    ld.append(dma(bct[:, 25:26], hb["swrep"][:, :], "sync", waits=[gate]))
    ld.append(dma(sbt[:, :], hb["sb1T"][:, :], "sync", waits=[gate]))
    tk_b = ld[-1]
    t1 = vop("tensor_scalar_mul", bct[:, 26:27], bct[:, 24:25], 0.5, waits=[ld[6]])
    t2 = act(bct[:, 27:28], bct[:, 25:26], AF.Tanh, scale=0.5, waits=[ld[7]])
    t3 = vop("tensor_scalar", bct[:, 28:29], bct[:, 27:28], 0.5, 0.5,
             ALU.mult, ALU.add, waits=[t2])
    t4 = vop("tensor_scalar", bct[:, 29:30], bct[:, 28:29], -1.0, 1.0,
             ALU.mult, ALU.add, waits=[t3], keep_same_engine=True, mark=True)
    t5 = vop("tensor_scalar_mul", bct[:, 30:31], bct[:, 29:30], 0.5,
             waits=[t4], keep_same_engine=True, mark=True)
    tk_cols = t5
    tmpr = ew[0:1, 0:D]
    r1 = dma(tmpr, hb["refb2_row"][:, :], "sync", waits=[gate])
    r2 = acopy(rows[0:1, 0:D], tmpr, waits=[r1])
    r3 = dma(ew[0:1, 0:E], hb["rb2_row"][:, :], "sync", waits=[r2])
    r4 = acopy(rows[0:1, D:D + E], ew[0:1, 0:E], waits=[r3])
    tk_rows = r4
    tk_hones = vop("memset", hsh[64:65, :], 1.0, waits=[gate])
    tk_sel = dma(selb[:, :], hb["sel"][:, :], "sync", waits=[gate])
    def arena_load(dst_t, dst_off, scr_ap, rows_, cols_, planes, waits):
        srcv = scr_ap.rearrange("(a p) c -> p a c", p=128) if rows_ > 128 else scr_ap
        if rows_ > 128:
            dstap = AP_(dst_t[:, :].tensor, dst_off,
                        [[dst_t.shape[1], 128], [cols_, planes], [1, cols_]])
        else:
            dstap = AP_(dst_t[:, :].tensor, dst_off, [[dst_t.shape[1], rows_], [1, cols_]])
        return dma(dstap, srcv, "sync", waits=waits, slow=(cols_ < 8))
    lw = []
    gate_c = [gate] + ([env["cast_last"]] if env.get("cast_last") is not None else [])
    lw.append(arena_load(rha, RWO, wscr["rw1"], D, 256, 8, gate_c))
    lw.append(arena_load(rha, HWO, wscr["hw1"], D, 256, 8, gate_c))
    lw.append(arena_load(rha, RW2O, wscr["rw2"], 256, 8, 2, gate_c))
    lw.append(arena_load(rha, HW2O, wscr["hw2"], 256, 1, 2, gate_c))
    tk_rh = lw[-1]

    bank_rd = [gate] * 8

    def mmg(outbank, outslice, parts, waits_first=()):
        t = None
        n = len(parts)
        for i, (lh, rh) in enumerate(parts):
            w = list(waits_first) + [bank_rd[outbank]] if i == 0 else []
            t = mm(outslice, lh, rh, i == 0, i == n - 1, waits=w)
        return t

    prev_tile_done = gate
    warena_free = gate
    ew_war = [None]
    for ft in range(NFT):
        tbase = ft * FT
        ysr = y1scr
        ysv = AP_(ysr.tensor, tbase * D, [[D, 128], [128 * D, 4], [1, D]])
        wv = [y1_toks[tbase // 128 + k] for k in range(4)] + [prev_tile_done]
        ly1 = dma(y1t[:, :], ysv, "sync", waits=wv)
        icp = vop("tensor_copy", itm[:, :], y1t[:, :], waits=[ly1])

        def to_fm(srctile, planes0, ctoks):
            evs = []
            for tc in range(4):
                trs = []
                for cb in range(8):
                    bk = 6 + cb // 4
                    w = list(ctoks) if (tc == 0 and cb == 0) else []
                    if cb % 4 == 0:
                        w.append(bank_rd[bk])
                    trs.append(tr(psum[:, bk * 512 + (cb % 4) * 128:
                                       bk * 512 + (cb % 4) * 128 + 128],
                                  srctile[:, tc * 1024 + cb * 128: tc * 1024 + (cb + 1) * 128],
                                  idn32[:, :], waits=w))
                ctxt = ctxf[:, :]
                for bk in range(2):
                    dst = AP_(ctxt.tensor, (planes0 + 4 * bk) * 512 + tc * 128,
                              [[16 * 512, 128], [512, 4], [1, 128]])
                    srcp = psum[:, (6 + bk) * 512:(6 + bk) * 512 + 512].rearrange(
                        "p (a b) -> p a b", a=4)
                    ev = acopy(dst, srcp, waits=trs[bk * 4:(bk + 1) * 4])
                    evs.append(ev)
                    bank_rd[6 + bk] = ev
            return evs
        y1fm_ev = to_fm(y1t, 0, [ly1])

        rh_ev = []
        for hc in range(2):
            bk = hc % 2
            parts = [(rha[:, RWO + cb * 256 + hc * 128: RWO + cb * 256 + (hc + 1) * 128],
                      ctxf[:, cb * 512:(cb + 1) * 512]) for cb in range(8)]
            t = mmg(bk, psum[:, bk * 512:(bk + 1) * 512], parts,
                    waits_first=[tk_rh] + y1fm_ev)
            ev = act(rhid[:, hc * 512:(hc + 1) * 512], psum[:, bk * 512:(bk + 1) * 512],
                     AF.Gelu, bias=bct[:, 0 + hc:1 + hc], scale=1.0, waits=[t, tk_b])
            rh_ev.append(ev)
            bank_rd[bk] = ev
        lg_toks = []
        for tc in range(4):
            sl = psum[:, 2 * 512 + tc * 8: 2 * 512 + (tc + 1) * 8]
            parts = [(rhid[:, hc * 512 + tc * 128: hc * 512 + (tc + 1) * 128],
                      rha[:, RW2O + hc * 8: RW2O + (hc + 1) * 8]) for hc in range(2)]
            parts.append((onesb[0:1, 0:128], rows[0:1, D:D + E]))
            lg_toks.append(mmg(2, sl, parts, waits_first=rh_ev + [tk_rows]))
        mt_toks = []
        alpha_toks = []
        for tc in range(4):
            sl = psum[:, 2 * 512 + tc * 8: 2 * 512 + (tc + 1) * 8]
            c0 = tc * 8
            q1 = vop("reduce_max", cols[:, c0:c0 + 1], sl, axis=AX.X, waits=lg_toks)
            q2 = vop("tensor_single_scalar", mq[:, 0:8], sl, cols[:, c0:c0 + 1],
                     ALU.is_equal, waits=[q1], keep_same_engine=True, mark=True)
            q3 = vop("scalar_tensor_tensor", mq[:, 8:16], mq[:, 0:8], -1e30, sl,
                     ALU.mult, ALU.add, waits=[q2], keep_same_engine=True, mark=True)
            q4 = vop("reduce_max", cols[:, c0 + 1:c0 + 2], mq[:, 8:16], axis=AX.X,
                     waits=[q3], keep_same_engine=True, mark=True)
            q5 = vop("tensor_single_scalar", mq[:, 16:24], mq[:, 8:16],
                     cols[:, c0 + 1:c0 + 2], ALU.is_equal, waits=[q4],
                     keep_same_engine=True, mark=True)
            q6 = vop("tensor_copy", m1t[:, tc * 8: tc * 8 + 8], mq[:, 0:8], waits=[q2])
            q7 = vop("tensor_copy", m1t[:, 32 + tc * 8: 32 + tc * 8 + 8], mq[:, 16:24],
                     waits=[q5], keep_same_engine=True, mark=True)
            mt_toks.append((q6, q7))
            q8 = vop("tensor_sub", cols[:, c0 + 2:c0 + 3], cols[:, c0:c0 + 1],
                     cols[:, c0 + 1:c0 + 2], waits=[q4], keep_same_engine=True, mark=True)
            q9 = act(cols[:, c0 + 3:c0 + 4], cols[:, c0 + 2:c0 + 3], AF.Tanh, scale=0.5,
                     waits=[q8])
            qa = vop("tensor_scalar", cols[:, c0 + 4:c0 + 5], cols[:, c0 + 3:c0 + 4],
                     bct[:, 30:31], bct[:, 30:31], ALU.mult, ALU.add,
                     waits=[q9, tk_cols], mark=True)
            qb = vop("tensor_sub", cols[:, c0 + 5:c0 + 6], bct[:, 29:30],
                     cols[:, c0 + 4:c0 + 5], waits=[qa], keep_same_engine=True,
                     mark=True)
            alpha_toks.append(qb)
        bank_rd[2] = alpha_toks[-1]
        mtr = [[], []]
        for mi, (mtile_off, dstf) in enumerate(((0, m1f), (32, m2f))):
            bk = 6 + mi
            for tc in range(4):
                w = [mt_toks[tc][mi], env["idnb_"] and None] if False else [mt_toks[tc][mi]]
                if tc == 0:
                    w.append(bank_rd[bk])
                mtr[mi].append(tr(psb[0:8, bk * 1024 + tc * 128: bk * 1024 + (tc + 1) * 128],
                                  m1t[:, mtile_off + tc * 8: mtile_off + (tc + 1) * 8],
                                  idnb_[:, :], waits=w))
            ev = acopy(dstf[0:8, :], psb[0:8, bk * 1024: bk * 1024 + 512],
                       waits=mtr[mi])
            bank_rd[bk] = ev
            mtr[mi] = ev
        mrep_ev = []
        for mi, (mf, mr) in enumerate(((m1f, m1r), (m2f, m2r))):
            for hc in range(4):
                bk = hc % 2
                t = mmg(bk, psum[:, bk * 512:(bk + 1) * 512],
                        [(selb[0:8, hc * 128:(hc + 1) * 128], mf[0:8, :])],
                        waits_first=[mtr[mi], tk_sel])
                ev = acopy(mr[:, hc * 512:(hc + 1) * 512],
                           psum[:, bk * 512:(bk + 1) * 512], waits=[t])
                mrep_ev.append(ev)
                bank_rd[bk] = ev

        if DEBUG_B and ft == 0:
            toks_y.append(dma(dbgB["d_m1t"][:, :], m1t[:, :], "sync",
                              waits=[t for pair in mt_toks for t in pair]))
            toks_y.append(dma(dbgB["d_cols"][:, :], cols[:, :], "sync", waits=alpha_toks))
            toks_y.append(dma(dbgB["d_rhid"][:, :], rhid[:, :], "sync", waits=rh_ev))
            toks_y.append(dma(dbgB["d_m1r"][:, :], m1r[:, :], "sync", waits=mrep_ev))

        wvt = [warena_free]
        if ft == 0 and env.get("cast_last") is not None:
            wvt = wvt + [env["cast_last"]]
        l1 = arena_load(wa, W1O, wscr["w1cat"], C2, HC, 16, wvt)
        l2 = arena_load(wa, W2O, wscr["w2cat"], HC, O3, 4, wvt)
        l3 = arena_load(wa, S1O, wscr["sw1"], C2, EH, 16, wvt)
        l4 = arena_load(wa, S2O, wscr["sw2p"], EH + 1, O3, 1, wvt)
        l5 = arena_load(wa, B2O, wscr["b2cat"], E, O3, 1, wvt)
        lwt = [l1, l2, l3, l4, l5]

        ctrl_last = None
        for it in range(2):
            ncc = 8 if it == 0 else 16
            if it == 1:
                w_iv = [iup_last, vup_last]
                if DEBUG_B and ft == 0:
                    t_i0 = dma(dbgB["d_i0"][:, :], itm[:, :], "sync", waits=[iup_last])
                    t_v0 = dma(dbgB["d_v0"][:, :], vtm2[:, :], "sync", waits=[vup_last])
                    toks_y += [t_i0, t_v0]
                    w_iv += [t_i0, t_v0]
                ifm_ev = to_fm(itm, 0, w_iv)
                vfm_ev = to_fm(vtm2, 8, w_iv)
                ctx_ev = ifm_ev + vfm_ev
            else:
                ctx_ev = y1fm_ev
            h_ev = []
            for hc in range(4):
                bk = hc % 2
                parts = [(wa[:, W1O + cb * 512 + hc * 128: W1O + cb * 512 + (hc + 1) * 128],
                          ctxf[:, cb * 512:(cb + 1) * 512]) for cb in range(ncc)]
                t = mmg(bk, psum[:, bk * 512:(bk + 1) * 512], parts,
                        waits_first=[l1] + ctx_ev)
                ev = act(hf[:, hc * 512:(hc + 1) * 512], psum[:, bk * 512:(bk + 1) * 512],
                         AF.Gelu, bias=bct[:, 4 + hc:5 + hc], waits=[t, tk_b])
                h_ev.append(ev)
                bank_rd[bk] = ev
            parts = [(wa[0:128, S1O + cb * 64: S1O + (cb + 1) * 64],
                      ctxf[:, cb * 512:(cb + 1) * 512]) for cb in range(ncc)]
            t = mmg(2, psum[0:64, 2 * 512: 2 * 512 + 512], parts,
                    waits_first=[l3] + ctx_ev)
            hsh_ev = act(hsh[0:64, :], psum[0:64, 2 * 512: 2 * 512 + 512], AF.Gelu,
                         bias=sbt[:, 0:1], waits=[t, tk_b, tk_hones])
            bank_rd[2] = hsh_ev
            hm1 = vop("tensor_mul", h1f[:, :], hf[:, :], m1r[:, :],
                      waits=h_ev + mrep_ev)
            hm2 = vop("tensor_mul", h2f[:, :], hf[:, :], m2r[:, :],
                      waits=h_ev + mrep_ev, keep_same_engine=True, mark=True)
            hh_ev = []
            for hc in range(2):
                parts = [(rha[:, HWO + cb * 256 + hc * 128: HWO + cb * 256 + (hc + 1) * 128],
                          ctxf[:, cb * 512:(cb + 1) * 512]) for cb in range(8)]
                t = mmg(3, psum[:, 3 * 512: 3 * 512 + 512], parts,
                        waits_first=[tk_rh] + ctx_ev)
                ev = act(hhid[:, hc * 512:(hc + 1) * 512], psum[:, 3 * 512: 3 * 512 + 512],
                         AF.Gelu, bias=bct[:, 2 + hc:3 + hc], waits=[t, tk_b])
                hh_ev.append(ev)
                bank_rd[3] = ev
            if DEBUG_B and ft == 0 and it == 0:
                toks_y.append(dma(dbgB["d_hf"][:, :], hf[:, :], "sync", waits=h_ev))
                toks_y.append(dma(dbgB["d_hsh"][:, :], hsh[:, :], "sync", waits=[hsh_ev]))
                toks_y.append(dma(dbgB["d_h1f"][:, :], h1f[:, :], "sync", waits=[hm1]))
            hc_mm = []
            for tc in range(4):
                sl = psum[:, 2 * 512 + 16 + tc: 2 * 512 + 17 + tc]
                parts = [(hhid[:, hc * 512 + tc * 128: hc * 512 + (tc + 1) * 128],
                          rha[:, HW2O + hc: HW2O + hc + 1]) for hc in range(2)]
                hc_mm.append(mmg(2, sl, parts, waits_first=hh_ev))
            hcol_toks = []
            for tc in range(4):
                sl = psum[:, 2 * 512 + 16 + tc: 2 * 512 + 17 + tc]
                th = act(cols[:, tc * 8 + 6: tc * 8 + 7], sl, AF.Tanh, scale=0.5,
                         bias=bct[:, 26:27], waits=hc_mm + [t1])
                u1 = vop("tensor_scalar", cols[:, tc * 8 + 7: tc * 8 + 8],
                         cols[:, tc * 8 + 6: tc * 8 + 7], 0.5, 0.5, ALU.mult, ALU.add,
                         waits=[th], mark=True)
                u2 = vop("tensor_scalar", mq[:, 24 + tc: 25 + tc],
                         cols[:, tc * 8 + 6: tc * 8 + 7], 0.05, 0.05, ALU.mult, ALU.add,
                         waits=[th], mark=True)
                hcol_toks.append((u1, u2))
            bank_rd[2] = hcol_toks[-1][0]
            if DEBUG_B and ft == 0 and it == 0:
                toks_y.append(dma(dbgB["d_mq"][:, :], mq[:, :], "sync",
                                  waits=[t for p_ in hcol_toks for t in p_]))
                toks_y.append(dma(dbgB["d_ctxf"][0:128, 0:8192], ctxf[:, 0:8192],
                                  "sync", waits=ctx_ev))
                toks_y.append(dma(dbgB["d_cols2"][:, :], cols[:, :], "sync",
                                  waits=[t for p_ in hcol_toks for t in p_]))
            for tc in range(4):
                def ewb(i):
                    return ew[:, i * 512:(i + 1) * 512]
                for ch in range(2):
                    c0 = tc * 1024 + ch * 512
                    dtok = vop("tensor_sub", ewb(0), itm[:, c0:c0 + 512],
                               murep[:, ch * 512:(ch + 1) * 512],
                               waits=[icp if it == 0 else iup_last, ld[0]])
                    ev_parts = {}
                    s_wave = [("sh", "a"), ("sh", "g"), ("r1", "a"), ("r1", "g"),
                              ("r2", "a"), ("r2", "g")]
                    e_wave = [("sh", "b"), ("r1", "b"), ("r2", "b")]
                    order = (s_wave + e_wave) if (tc * 2 + ch) % 2 == 0 else                             (e_wave + s_wave)
                    for gi, (br, pname) in enumerate(order):
                        bi = ("sh", "r1", "r2").index(br)
                        pi = ("a", "b", "g").index(pname)
                        bk = 4 + gi % 2
                        war = ([ew_war[0][(br, pname)]]
                               if (ew_war[0] is not None and (br, pname) in ew_war[0])
                               else [])
                        off = pi * 1024 + ch * 512
                        if br == "sh":
                            parts = [(hsh[0:65, tc * 128:(tc + 1) * 128],
                                      wa[0:65, S2O + off: S2O + off + 512])]
                            wf = [l4, hsh_ev]
                        else:
                            mfm = m1f if br == "r1" else m2f
                            hxf = h1f if br == "r1" else h2f
                            hmt = hm1 if br == "r1" else hm2
                            parts = [(hxf[:, hc2 * 512 + tc * 128: hc2 * 512 + (tc + 1) * 128],
                                      wa[:, W2O + hc2 * 3072 + off: W2O + hc2 * 3072 + off + 512])
                                     for hc2 in range(4)]
                            parts.append((mfm[0:8, tc * 128:(tc + 1) * 128],
                                          wa[0:8, B2O + off: B2O + off + 512]))
                            wf = [l2, l5, hmt, mtr[0 if br == "r1" else 1]]
                        t = mmg(bk, psum[:, bk * 512:(bk + 1) * 512], parts,
                                waits_first=wf)
                        bufi = {"a": 1, "b": 7, "g": 4}[pname] + bi
                        if pname == "b":
                            e1 = act(ewb(bufi), psum[:, bk * 512:(bk + 1) * 512],
                                     AF.Exp, waits=[t] + war)
                            e2 = act(ewb(bufi), ewb(bufi), AF.Ln, bias=1.0,
                                     waits=[e1])
                            ev_parts[(br, pname)] = e2
                            bank_rd[bk] = e1
                        else:
                            e1 = act(ewb(bufi), psum[:, bk * 512:(bk + 1) * 512],
                                     AF.Sigmoid, waits=[t] + war)
                            ev_parts[(br, pname)] = e1
                            bank_rd[bk] = e1
                    if DEBUG_B and ft == 0 and it == 0 and tc == 0 and ch == 0:
                        evp = list(ev_parts.values())
                        for dn, bi_ in (("d_sa", 1), ("d_sb", 7), ("d_sg", 4),
                                        ("d_sb1", 8), ("d_sb2", 9)):
                            toks_y.append(dma(dbgB[dn][:, :], ewb(bi_), "sync", waits=evp))
                        dbg_ev = toks_y[-5:]
                        acopy(ew[0:1, 0:1], ew[0:1, 0:1], waits=dbg_ev)
                    else:
                        dbg_ev = []
                    wv_ = [dtok] + dbg_ev + list(ev_parts.values()) + [alpha_toks[tc],
                                                              hcol_toks[tc][0], hcol_toks[tc][1]]
                    ch_toks = []
                    readers = {}
                    first = True
                    for bi, br in enumerate(("sh", "r1", "r2")):
                        acol = (bct[:, 28:29] if br == "sh" else
                                cols[:, tc * 8 + 4 + (0 if br == "r1" else 1):
                                     tc * 8 + 5 + (0 if br == "r1" else 1)])
                        SA, SB, SG = ewb(1 + bi), ewb(7 + bi), ewb(4 + bi)
                        if it == 0:
                            na = mq[:, 28 + bi:29 + bi]
                            natok = vop("tensor_scalar_mul", na, acol, -1.0,
                                        waits=wv_ if first else [],
                                        keep_same_engine=True, mark=True)
                            ch_toks.append(natok)
                            vnp = ewb(12 + bi)
                            ch_toks.append(vop("scalar_tensor_tensor", vnp, ewb(0),
                                               na, SB, ALU.mult, ALU.mult,
                                               waits=[natok], keep_same_engine=True,
                                               mark=True))
                            readers[(br, "b")] = ch_toks[-1]
                        else:
                            A, B = ewb(10), ewb(11)
                            ch_toks.append(vop("scalar_tensor_tensor", A,
                                               vtm2[:, c0:c0 + 512], acol, SA,
                                               ALU.mult, ALU.mult,
                                               waits=(wv_ + [vup_last]) if first else [],
                                               keep_same_engine=True, mark=True))
                            readers[(br, "a")] = ch_toks[-1]
                            ch_toks.append(vop("scalar_tensor_tensor", B, ewb(0),
                                               acol, SB, ALU.mult, ALU.mult, waits=[],
                                               keep_same_engine=True, mark=True))
                            readers[(br, "b")] = ch_toks[-1]
                            vnp = ewb(12 + bi)
                            ch_toks.append(vop("tensor_sub", vnp, A, B, waits=[],
                                               keep_same_engine=True, mark=True))
                        first = False
                        if bi == 0:
                            ch_toks.append(vop("tensor_mul", ewb(15), SG, vnp, waits=[],
                                               keep_same_engine=True, mark=True))
                            readers[(br, "g")] = ch_toks[-1]
                        else:
                            ch_toks.append(vop("tensor_mul", ewb(10), SG, vnp, waits=[],
                                               keep_same_engine=True, mark=True))
                            readers[(br, "g")] = ch_toks[-1]
                            ch_toks.append(vop("tensor_add", ewb(15), ewb(15), ewb(10),
                                               waits=[], keep_same_engine=True, mark=True))
                    ch_toks.append(vop("tensor_add", ewb(11), ewb(12), ewb(13), waits=[],
                                       keep_same_engine=True, mark=True))
                    ch_toks.append(vop("tensor_add", ewb(11), ewb(11), ewb(14), waits=[],
                                       keep_same_engine=True, mark=True))
                    if DEBUG_B and ft == 0 and it == 0 and tc == 0 and ch == 0:
                        tvn = dma(dbgB["d_vn"][:, :], ewb(11), "sync", waits=[ch_toks[-1]])
                        tgg = dma(dbgB["d_g"][:, :], ewb(15), "sync", waits=[ch_toks[-1]])
                        for dn, bi_ in (("d_vn0", 12), ("d_vn1", 13), ("d_vn2", 14)):
                            toks_y.append(dma(dbgB[dn][:, :], ewb(bi_), "sync",
                                              waits=[ch_toks[-1]]))
                        tvn2 = toks_y[-3:]
                        toks_y += [tvn, tgg]
                        tvn = tvn; tgg = tgg
                        ch_toks.append(vop("tensor_copy", ew[0:1, 0:1], ew[0:1, 0:1],
                                           waits=[tvn, tgg] + tvn2))
                    iup = vop("scalar_tensor_tensor", itm[:, c0:c0 + 512], ewb(15),
                              mq[:, 24 + tc: 25 + tc], itm[:, c0:c0 + 512],
                              ALU.mult, ALU.add, waits=[hcol_toks[tc][1]],
                              keep_same_engine=True, mark=True)
                    if it == 0:
                        vup = vop("tensor_scalar_mul", vtm2[:, c0:c0 + 512], ewb(11),
                                  cols[:, tc * 8 + 7: tc * 8 + 8],
                                  waits=[hcol_toks[tc][0]],
                                  keep_same_engine=True, mark=True)
                    else:
                        ch_toks.append(vop("tensor_sub", ewb(10), ewb(11),
                                           vtm2[:, c0:c0 + 512], waits=[],
                                           keep_same_engine=True, mark=True))
                        vup = vop("scalar_tensor_tensor", vtm2[:, c0:c0 + 512], ewb(10),
                                  cols[:, tc * 8 + 7: tc * 8 + 8], vtm2[:, c0:c0 + 512],
                                  ALU.mult, ALU.add, waits=[hcol_toks[tc][0]],
                                  keep_same_engine=True, mark=True)
                    iup_last, vup_last = iup, vup
                    ew_war[0] = readers
            ctrl_last = iup_last

        if DEBUG_B and ft == 0:
            t_i1 = dma(dbgB["d_i1"][:, :], itm[:, :], "sync", waits=[iup_last])
            t_v1 = dma(dbgB["d_v1"][:, :], vtm2[:, :], "sync", waits=[vup_last])
            toks_y += [t_i1, t_v1]
            dbg_iv = [t_i1, t_v1]
        else:
            dbg_iv = []

        lr1 = arena_load(wa, R1O, wscr["refw1"], D, C2, 8, [ctrl_last] + dbg_iv)
        lr2 = arena_load(wa, R2O, wscr["refw2"], C2, D, 16, [ctrl_last])
        rfm_ev = to_fm(itm, 0, [iup_last] + dbg_iv)
        def hidr_plane(hc):
            tens = (hf, h1f, h2f, m1r)[hc // 4]
            return tens[:, (hc % 4) * 512:((hc % 4) + 1) * 512]
        hr_ev = []
        for hc in range(16):
            bk = hc % 2
            parts = [(wa[:, R1O + cb * 2048 + hc * 128: R1O + cb * 2048 + (hc + 1) * 128],
                      ctxf[:, cb * 512:(cb + 1) * 512]) for cb in range(8)]
            t = mmg(bk, psum[:, bk * 512:(bk + 1) * 512], parts,
                    waits_first=[lr1] + rfm_ev + ([hm1, hm2] if hc == 4 else []) +
                                (mrep_ev if hc == 12 else []))
            ev = act(hidr_plane(hc), psum[:, bk * 512:(bk + 1) * 512], AF.Gelu,
                     bias=bct[:, 8 + hc:9 + hc],
                     waits=[t, tk_b] + ([hm1, hm2] if hc in (4, 8, 12) else []))
            hr_ev.append(ev)
            bank_rd[bk] = ev
        if DEBUG_B and ft == 0:
            hr_t = dma(dbgB["d_hidr"][:, 0:2048], hf[:, :], "sync", waits=hr_ev)
            hr_t2 = dma(dbgB["d_hidr"][:, 2048:4096], h1f[:, :], "sync", waits=hr_ev)
            hr_t3 = dma(dbgB["d_hidr"][:, 4096:6144], h2f[:, :], "sync", waits=hr_ev)
            hr_t4 = dma(dbgB["d_hidr"][:, 6144:8192], m1r[:, :], "sync", waits=hr_ev)
            toks_y += [hr_t, hr_t2, hr_t3, hr_t4]
        ln_in_toks = []
        for tc in range(4):
            for chf in range(2):
                bk = 2 + (2 * tc + chf) % 2
                c0 = tc * 1024 + chf * 512
                parts = [(hidr_plane(hc)[:, tc * 128:(tc + 1) * 128],
                          wa[:, R2O + hc * 1024 + chf * 512: R2O + hc * 1024 + chf * 512 + 512])
                         for hc in range(16)]
                parts.append((onesb[0:1, 0:128], rows[0:1, chf * 512: (chf + 1) * 512]))
                t = mmg(bk, psum[:, bk * 512:(bk + 1) * 512], parts,
                        waits_first=hr_ev + [lr2, tk_rows])
                u1 = vop("tensor_mul", ew[:, 0:512], psum[:, bk * 512:(bk + 1) * 512],
                         iwrep[:, (c0 % 1024):(c0 % 1024) + 512], waits=[t, ld[1]])
                if DEBUG_B and ft == 0 and tc == 0 and chf == 0:
                    t_ri = dma(dbgB["d_refiw"][:, :], ew[:, 0:512], "sync", waits=[u1])
                    toks_y.append(t_ri)
                    u1 = t_ri
                u2 = vop("scalar_tensor_tensor", y1t[:, c0:c0 + 512], y1t[:, c0:c0 + 512],
                         2.0, ew[:, 0:512], ALU.mult, ALU.add, waits=[u1],
                         keep_same_engine=True, mark=True)
                ln_in_toks.append(u2)
                bank_rd[bk] = u2
        if DEBUG_B and ft == 0:
            t_li = dma(dbgB["d_ln2in"][:, :], y1t[:, :], "sync", waits=ln_in_toks)
            toks_y.append(t_li)
            ln_in_toks = ln_in_toks + [t_li]
        lnf = None
        lnaf = None
        for tc in range(4):
            s1ap = y1t[:, tc * 1024:(tc + 1) * 1024]
            w = [ln_in_toks[2 * tc], ln_in_toks[2 * tc + 1]] + ([lnf] if lnf else [])
            b1 = vop("bn_stats", lnst[:, 0:6], s1ap[:, 0:512], waits=w, mark=True)
            b2 = vop("bn_stats", lnst[:, 6:12], s1ap[:, 512:1024], waits=w, mark=True)
            b3 = vop("bn_aggr", lnst[:, 12:14], lnst[:, 0:12],
                     waits=[b1, b2] + ([lnaf] if lnaf else []), keep_same_engine=True)
            l1_ = act(lnst[:, 14:15], lnst[:, 13:14], AF.Ln, bias=cstf[:, 1:2],
                      waits=[b3])
            l2_ = act(lnst[:, 15:16], lnst[:, 14:15], AF.Exp, scale=-0.5, waits=[l1_])
            lnaf = l1_
            yn = vop("tensor_scalar", s1ap, s1ap, lnst[:, 12:13], lnst[:, 15:16],
                     ALU.subtract, ALU.mult, waits=[l2_])
            lnf = yn
            yr = y_out.rearrange("(t p) c -> t p c", p=128)
            yst = dma(yr[(tbase // 128) + tc], s1ap, "sync", waits=[yn])
            toks_y.append(yst)
        prev_tile_done = toks_y[-1]
        warena_free = ln_in_toks[-1]
    return toks_y


def _run(inputs):
    consts = _host_consts()
    lay = _host_layout(inputs)
    x = np.asarray(inputs["L0_attn_x"])
    nc = build_nc()
    in_maps = []
    for c in range(N_CORES):
        m = dict(lay)
        m.update(consts)
        m["x"] = np.ascontiguousarray(x[2 * c:2 * c + 2].reshape(TOK, D))
        in_maps.append(m)
    res = run_bass_kernel_spmd(nc, in_maps, core_ids=list(range(N_CORES)))
    return res


def kernel(**inputs):
    res = _run(inputs)
    return np.concatenate(
        [res.results[c]["y"].reshape(2, L, D) for c in range(N_CORES)], 0)


# version 09
# speedup vs baseline: 1.1626x; 1.0302x over previous
import numpy as np
import ml_dtypes
import contextlib
import concourse.bass as bass
import concourse.mybir as mybir
from concourse.bass_utils import run_bass_kernel_spmd
from concourse.ap import AP

F32 = mybir.dt.float32
BF16 = mybir.dt.bfloat16
AF = mybir.ActivationFunctionType
ALU = mybir.AluOpType
AX = mybir.AxisListType
BF = ml_dtypes.bfloat16

N_CORES = 8
SEQS = 2
L = 2048
D = 1024
H = 16
HD = 64
TOK = SEQS * L
CSH = 24.0
EPS = 1e-5
NBK = 32
TBL = 4095
GW = 3968
E = 8
EH = 64
C2 = 2048
HC = 512
O3 = 3072
FT = 512
NFT = TOK // FT
DT_ = 0.1

PHASE_B = True
DEBUG_Y1 = False
DEBUG_ATT = False
DEBUG_S1 = False
BARRIER_OPROJ = False
DEBUG_B = False

HL = H * 65



def _bucket_of_delta(delta):
    n = np.asarray(delta, np.int64)
    out = np.zeros_like(n)
    small = (n >= 1) & (n < 16)
    out[small] = n[small]
    big = n >= 16
    nf = n[big].astype(np.float32)
    v = np.log(nf / np.float32(16.0)) / np.float32(np.log(128.0 / 16.0)) * np.float32(16.0)
    out[big] = np.minimum(16 + v.astype(np.int32), 31)
    return out


def _host_consts():
    c = {}
    c["idn32"] = np.eye(128, dtype=np.float32)
    c["idnb"] = np.eye(128, dtype=np.float32).astype(BF)
    m = np.arange(TBL)
    bk = _bucket_of_delta(2047 - m)
    oh = np.zeros((NBK, TBL), np.float32)
    oh[bk, m] = 1.0
    c["ohrev"] = oh.astype(BF)
    sel = np.zeros((E, HC), np.float32)
    for e in range(E):
        sel[e, e * EH:(e + 1) * EH] = 1.0
    c["sel"] = sel.astype(BF)
    c["cstf"] = np.ascontiguousarray(
        np.broadcast_to(np.array([[-CSH, EPS]], np.float32), (128, 2)))
    return c


def _host_layout(inp):
    a = {}
    g = lambda k: np.asarray(inp[k])
    for nm in ("wq", "wk", "wv", "wo"):
        a[nm] = np.ascontiguousarray(g("L0_attn_" + nm))
    a["bqT"] = np.ascontiguousarray(g("L0_attn_bq").reshape(8, 128).T)
    a["bkT"] = np.ascontiguousarray(g("L0_attn_bk").reshape(8, 128).T)
    a["bv_row"] = np.ascontiguousarray(g("L0_attn_bv").reshape(1, D))
    a["bo_row"] = np.ascontiguousarray(g("L0_attn_bo").reshape(1, D))
    a["rel_bias"] = np.ascontiguousarray(g("L0_attn_rel_bias"))
    a["rw1"] = np.ascontiguousarray(g("L0_ffn_router_w1"))
    a["rb1T"] = np.ascontiguousarray(g("L0_ffn_router_b1").reshape(2, 128).T)
    a["rw2"] = np.ascontiguousarray(g("L0_ffn_router_w2"))
    a["rb2_row"] = np.ascontiguousarray(g("L0_ffn_router_b2").reshape(1, E))
    a["hw1"] = np.ascontiguousarray(g("L0_ffn_halt_w1"))
    a["hb1T"] = np.ascontiguousarray(g("L0_ffn_halt_b1").reshape(2, 128).T)
    a["hw2"] = np.ascontiguousarray(g("L0_ffn_halt_w2"))
    a["hb2rep"] = np.ascontiguousarray(
        np.broadcast_to(g("L0_ffn_halt_b2").reshape(1, 1), (128, 1)))
    ew1 = g("L0_ffn_expert_w1")
    a["w1cat"] = np.ascontiguousarray(np.transpose(ew1, (1, 0, 2)).reshape(C2, HC))
    a["b1catT"] = np.ascontiguousarray(g("L0_ffn_expert_b1").reshape(-1).reshape(4, 128).T)
    a["w2cat"] = np.ascontiguousarray(g("L0_ffn_expert_w2").reshape(HC, O3))
    a["b2cat"] = np.ascontiguousarray(g("L0_ffn_expert_b2"))
    a["sw1"] = np.ascontiguousarray(g("L0_ffn_shared_w1"))
    a["sb1T"] = np.ascontiguousarray(g("L0_ffn_shared_b1").reshape(64, 1))
    a["sw2p"] = np.ascontiguousarray(
        np.concatenate([g("L0_ffn_shared_w2"), g("L0_ffn_shared_b2").reshape(1, O3)], 0))
    a["swrep"] = np.ascontiguousarray(
        np.broadcast_to(np.asarray(g("L0_ffn_shared_weight"), np.float32).reshape(1, 1), (128, 1)))
    a["refw1"] = np.ascontiguousarray(g("L0_ffn_refine_w1"))
    a["refb1T"] = np.ascontiguousarray(g("L0_ffn_refine_b1").reshape(16, 128).T)
    a["refw2"] = np.ascontiguousarray(g("L0_ffn_refine_w2"))
    a["refb2_row"] = np.ascontiguousarray(g("L0_ffn_refine_b2").reshape(1, D))
    a["mu_rep"] = np.ascontiguousarray(np.broadcast_to(g("L0_ffn_mu").reshape(1, D), (128, D)))
    a["iw_rep"] = np.ascontiguousarray(
        np.broadcast_to(g("L0_ffn_integration_weight").reshape(1, D), (128, D)))
    return a



ENGS = ("tensor", "vector", "scalar", "gpsimd", "sync")
DMA_ENGS = ("gpsimd", "sync")


class Tok:
    __slots__ = ("eng", "idx")
    def __init__(self, eng, idx):
        self.eng, self.idx = eng, idx


class Prog:
    def __init__(self, nc):
        self.nc = nc
        self.ops = {e: [] for e in ENGS}

    def op(self, eng, fn, waits=(), mark=False, keep_same_engine=False):
        ws = []
        for w in waits:
            if w is None or (w.eng == eng and eng not in DMA_ENGS
                             and not keep_same_engine):
                continue
            ws.append(w)
        self.ops[eng].append([fn, ws, mark, 0])
        return Tok(eng, len(self.ops[eng]) - 1)

    def emit(self, block, stack, final_toks):
        sems = {e: stack.enter_context(self.nc.semaphore("sem_" + e)) for e in ENGS}
        for e in DMA_ENGS:
            for rec in self.ops[e]:
                rec[2] = True
        for e in ENGS:
            for rec in self.ops[e]:
                for w in rec[1]:
                    self.ops[w.eng][w.idx][2] = True
        for w in final_toks:
            self.ops[w.eng][w.idx][2] = True
        for e in ENGS:
            acc = 0
            step = 16 if e in DMA_ENGS else 1
            for rec in self.ops[e]:
                if rec[2]:
                    acc += step
                rec[3] = acc

        def mk_stream(e):
            ops = self.ops[e]
            step = 16 if e in DMA_ENGS else 1
            def body(eng):
                for fn, ws, marked, val in ops:
                    byeng = {}
                    for w in ws:
                        wv = self.ops[w.eng][w.idx][3]
                        byeng[w.eng] = max(byeng.get(w.eng, 0), wv)
                    for we, wv in sorted(byeng.items()):
                        eng.wait_ge(sems[we], wv)
                    inst = fn(eng)
                    if marked:
                        inst.then_inc(sems[e], step)
            return body

        for e in ENGS:
            if self.ops[e]:
                getattr(block, e)(mk_stream(e))
        byeng = {}
        for w in final_toks:
            byeng[w.eng] = max(byeng.get(w.eng, 0), self.ops[w.eng][w.idx][3])
        def fin(eng):
            for we, wv in sorted(byeng.items()):
                if we != "sync":
                    eng.wait_ge(sems[we], wv)
            if "sync" in byeng:
                eng.wait_ge(sems["sync"], byeng["sync"])
        block.sync(fin)



def build_nc():
    nc = bass.Bass("TRN2", target_bir_lowering=False, debug=False)
    dt = nc.dram_tensor

    x_in = dt("x", [TOK, D], F32, kind="ExternalInput").ap()
    y_out = dt("y", [TOK, D], F32, kind="ExternalOutput").ap()
    y1scr = dt("y1scr", [TOK, D], F32, kind="Internal").ap()
    y1dbg = dt("y1dbg", [TOK, D], F32, kind="ExternalOutput").ap() if DEBUG_Y1 else None
    if DEBUG_ATT:
        dbg = {nm: dt(nm, [128, 8 * L], BF16, kind="ExternalOutput").ap()
               for nm in ("qdbg", "kdbg", "ctxdbg")}
        dbg["vdbg"] = dt("vdbg", [128, 16 * HL], BF16, kind="ExternalOutput").ap()
        dbg["crawdbg"] = dt("crawdbg", [128, 1024], F32, kind="ExternalOutput").ap()
        dbg["srowdbg"] = dt("srowdbg", [1, 1024], F32, kind="ExternalOutput").ap()
    s1dbg = dt("s1dbg", [TOK, D], F32, kind="ExternalOutput").ap() if DEBUG_S1 else None
    lndbg = dt("lndbg", [32, 128, 16], F32, kind="ExternalOutput").ap() if DEBUG_S1 else None

    hb = {}
    for nm, sh, dd in [
            ("wq", (D, D), F32), ("wk", (D, D), F32), ("wv", (D, D), F32), ("wo", (D, D), F32),
            ("bqT", (128, 8), F32), ("bkT", (128, 8), F32),
            ("bv_row", (1, D), F32), ("bo_row", (1, D), F32), ("rel_bias", (NBK, H), F32),
            ("rw1", (D, 256), F32), ("rb1T", (128, 2), F32), ("rw2", (256, E), F32),
            ("rb2_row", (1, E), F32),
            ("hw1", (D, 256), F32), ("hb1T", (128, 2), F32), ("hw2", (256, 1), F32),
            ("hb2rep", (128, 1), F32),
            ("w1cat", (C2, HC), F32), ("b1catT", (128, 4), F32), ("w2cat", (HC, O3), F32),
            ("b2cat", (E, O3), F32),
            ("sw1", (C2, EH), F32), ("sb1T", (EH, 1), F32), ("sw2p", (EH + 1, O3), F32),
            ("swrep", (128, 1), F32),
            ("refw1", (D, C2), F32), ("refb1T", (128, 16), F32), ("refw2", (C2, D), F32),
            ("refb2_row", (1, D), F32),
            ("mu_rep", (128, D), F32), ("iw_rep", (128, D), F32),
            ("idn32", (128, 128), F32), ("idnb", (128, 128), BF16),
            ("cstf", (128, 2), F32),
            ("ohrev", (NBK, TBL), BF16), ("sel", (E, HC), BF16)]:
        hb[nm] = dt(nm, list(sh), dd, kind="ExternalInput").ap()

    WSCR_SHAPES = {"w1cat": (C2, HC), "w2cat": (HC, O3), "sw1": (C2, EH),
                   "sw2p": (EH + 1, O3), "rw1": (D, 256), "hw1": (D, 256),
                   "refw1": (D, C2), "refw2": (C2, D), "rw2": (256, E),
                   "hw2": (256, 1), "b2cat": (E, O3)}
    wscr = {nm: dt("scr_" + nm, list(sh), BF16, kind="Internal").ap()
            for nm, sh in WSCR_SHAPES.items()}

    stack = contextlib.ExitStack()
    stackA = contextlib.ExitStack()
    sb = {}
    def sbuf(name, shape, dtype, st=None):
        sb[name] = (st or stack).enter_context(nc.sbuf_tensor(name, list(shape), dtype))
        return sb[name]
    def sbufA(name, shape, dtype):
        return sbuf(name, shape, dtype, st=stackA)

    onesb = sbuf("onesb", [1, 128], BF16)

    lnst  = sbuf("lnst",  [128, 16], F32)
    idn32 = sbuf("idn32s", [128, 128], F32)
    idnb_ = sbuf("idnbs", [128, 128], BF16)
    cstf  = sbuf("cstfs", [128, 2], F32)
    xbt   = sbufA("xbt",   [128, D], F32)
    xfm   = sbufA("xfm",   [128, 8 * L], BF16)
    qfm   = sbufA("qfm",   [128, 8 * L], BF16)
    kfm   = sbufA("kfm",   [128, 8 * L], BF16)
    vtm   = sbufA("vtm",   [128, 16 * HL], BF16)
    wtm   = sbufA("wtm",   [128, 2 * D], F32)
    wT    = sbufA("wT",    [128, 8 * D], BF16)
    erev  = sbufA("erev",  [H, TBL], BF16)
    ptile = sbufA("ptile", [128, 2 * 1024], BF16)
    craw  = sbufA("craw",  [128, 1024], F32)
    xf32  = sbufA("xf32",  [128, 2 * 1024], F32)
    srow  = sbufA("srow",  [1, 1024], F32)
    scol  = sbufA("scol",  [128, 16], F32)
    brow  = sbufA("brow",  [1, 2 * D], BF16)
    bias8 = sbufA("bias8", [128, 24], F32)
    relb  = sbufA("relb",  [NBK, H], BF16)
    relb32 = sbufA("relb32", [NBK, H], F32)
    ones32 = sbufA("ones32", [1, 64], F32)

    psum = stack.enter_context(nc.psum_tensor("PS", [128, 4096], F32))
    psb = psum[:, :].bitcast(BF16)

    def pbank(i, w=1):
        return psum[:, i * 512:(i + w) * 512]

    def pbankb(i, off, w):
        return psb[:, i * 1024 + off: i * 1024 + off + w]

    P = Prog(nc)

    def dma(dst, src, eng="sync", waits=(), slow=False):
        def fn(e, dst=dst, src=src, slow=slow):
            return e.dma_start(out=dst, in_=src, allow_slow_non_contiguous=slow)
        return P.op(eng, fn, waits)

    def mm(out, lhsT, rhs, start, stop, waits=()):
        def fn(e, out=out, lhsT=lhsT, rhs=rhs, start=start, stop=stop):
            return e.matmul(out, lhsT, rhs, start=start, stop=stop)
        return P.op("tensor", fn, waits)

    def tr(out, in_, ident, waits=()):
        def fn(e, out=out, in_=in_, ident=ident):
            return e.transpose(out, in_, ident)
        return P.op("tensor", fn, waits)

    def act(out, in_, func, bias=0.0, scale=1.0, waits=()):
        def fn(e, out=out, in_=in_, func=func, bias=bias, scale=scale):
            return e.activation(out, in_, func, bias=bias, scale=scale)
        return P.op("scalar", fn, waits)

    def acopy(out, in_, waits=()):
        def fn(e, out=out, in_=in_):
            return e.copy(out, in_)
        return P.op("scalar", fn, waits)

    def vop(name, *args, waits=(), mark=False, keep_same_engine=False, **kw):
        def fn(e, name=name, args=args, kw=kw):
            return getattr(e, name)(*args, **kw)
        return P.op("vector", fn, waits, mark=mark, keep_same_engine=keep_same_engine)

    tk_id32 = dma(idn32[:, :], hb["idn32"][:, :])
    tk_cst = dma(cstf[:, :], hb["cstf"][:, :])
    tk_idb = dma(idnb_[:, :], hb["idnb"][:, :])
    ohv_stack = contextlib.ExitStack()
    ohv = ohv_stack.enter_context(nc.sbuf_tensor("ohv", [NBK, TBL], BF16))
    tk_relb0 = dma(relb32[:, :], hb["rel_bias"][:, :], "sync")
    tk_relb = vop("tensor_copy", relb[:, :], relb32[:, :], waits=[tk_relb0])
    tk_oh = dma(ohv[:, :], hb["ohrev"][:, :], "sync")
    tk_ones = vop("memset", onesb[:, :], 1.0)
    tk_ones32 = vop("memset", ones32[:, :], 1.0)
    tk_srow1 = dma(srow[0:1, :], hb["bv_row"][:, :], "sync")
    tk_brow = acopy(brow[0:1, 0:D], srow[0:1, :], waits=[tk_srow1])
    tk_srow2 = dma(srow[0:1, :], hb["bo_row"][:, :], "sync", waits=[tk_brow])
    tk_brow2 = acopy(brow[0:1, D:2 * D], srow[0:1, :], waits=[tk_srow2])
    tk_srow_free = tk_brow2
    tk_b8 = dma(bias8[:, 0:8], hb["bqT"][:, :])
    tk_b8b = dma(bias8[:, 8:16], hb["bkT"][:, :])
    tk_bqs = vop("tensor_scalar_mul", bias8[:, 16:24], bias8[:, 0:8], 0.125,
                 waits=[tk_b8])

    erev_mm = []
    for j in range(8):
        c0, c1 = j * 512, min((j + 1) * 512, TBL)
        erev_mm.append(mm(psum[0:H, c0:c1], relb[:, :], ohv[:, c0:c1], True, True,
                          waits=[tk_relb, tk_oh]))
    tk_erev = act(erev[:, :], psum[0:H, 0:TBL], AF.Exp, waits=erev_mm)
    ohv_stack.close()
    gtab = sbufA("gtab", [128, 2 * GW], BF16)

    tk_vones = vop("memset", vtm[:, :], 1.0)

    pieces = []
    for nm in WSCR_SHAPES:
        rows, cols = WSCR_SHAPES[nm]
        srct, dsth = hb[nm], wscr[nm]
        nrb = (rows + 127) // 128
        for rb in range(nrb):
            r0 = rb * 128
            rr = min(128, rows - r0)
            for cc0 in range(0, cols, 1024):
                cc1 = min(cc0 + 1024, cols)
                pieces.append((srct, dsth, r0, rr, cc0, cc1))
    NSL = 6
    cast_lds = []
    cast_sts = []
    ncast = len(pieces)

    def _cast_load():
        ci = len(cast_lds)
        if ci >= ncast:
            return
        srct, dsth, r0, rr, cc0, cc1 = pieces[ci]
        sl = ci % NSL
        w = []
        if ci >= NSL:
            w.append(cast_sts[ci - NSL])
        if ci == 0:
            w.append(tk_erev)
        cast_lds.append(dma(gtab[0:rr, sl * 1024: sl * 1024 + cc1 - cc0],
                            srct[r0:r0 + rr, cc0:cc1], "gpsimd", waits=w))

    def _cast_store():
        ci = len(cast_sts)
        if ci >= ncast or len(cast_lds) < min(ci + 2, ncast):
            return
        srct, dsth, r0, rr, cc0, cc1 = pieces[ci]
        sl = ci % NSL
        if ci + 1 < ncast:
            settle = cast_lds[ci + 1]
        else:
            settle = dma(scol[0:1, 8:16], hb["bo_row"][0:1, 0:8], "gpsimd",
                         waits=[cast_lds[ci]])
        cast_sts.append(dma(dsth[r0:r0 + rr, cc0:cc1],
                            gtab[0:rr, sl * 1024: sl * 1024 + cc1 - cc0], "sync",
                            waits=[cast_lds[ci], settle]))

    def emit_cast_chunk(nl, ns):
        for _ in range(nl):
            _cast_load()
        for _ in range(ns):
            _cast_store()

    def drain_casts():
        while len(cast_lds) < ncast or len(cast_sts) < ncast:
            emit_cast_chunk(2, 2)
        return cast_sts[-1]

    emit_cast_chunk(3, 0)
    cast_last = None

    y_toks = []
    y1_toks = []

    def weight_transpose(name, first_wait, bank_waits=(None, None), wT_wait=None):
        xr = hb[name].rearrange("(a p) c -> a p c", p=128)
        evs = []
        slot_free = [None] * 2
        bank_free = list(bank_waits)
        lts = []

        def emit_load(rb):
            sl = rb % 2
            w = []
            if first_wait is not None and rb == 0:
                w.append(first_wait)
            if slot_free[sl] is not None:
                w.append(slot_free[sl])
            lts.append(dma(wtm[:, sl * D:(sl + 1) * D], xr[rb], "sync", waits=w))

        emit_load(0)
        for rb in range(8):
            sl = rb % 2
            if rb + 1 < 8:
                emit_load(rb + 1)
                settle = lts[rb + 1]
            else:
                settle = dma(scol[0:1, 0:8], hb["bo_row"][0:1, 0:8], "sync")
            trs = []
            for cb in range(8):
                bk = cb // 4
                tw = [lts[rb], tk_id32]
                if cb == 0:
                    tw.append(settle)
                if cb % 4 == 0 and bank_free[bk] is not None:
                    tw.append(bank_free[bk])
                trs.append(tr(psum[:, (6 + bk) * 512 + (cb % 4) * 128:
                                   (6 + bk) * 512 + (cb % 4) * 128 + 128],
                              wtm[:, sl * D + cb * 128: sl * D + (cb + 1) * 128],
                              idn32[:, :], waits=tw))
            slot_free[sl] = trs[-1]
            wTt = wT[:, :]
            for bk in range(2):
                dst = AP(wTt.tensor, (4 * bk) * D + rb * 128, [[8 * D, 128], [D, 4], [1, 128]])
                srcp = psum[:, (6 + bk) * 512:(6 + bk) * 512 + 512].rearrange(
                    "p (a b) -> p a b", a=4)
                ew = list(trs[bk * 4:(bk + 1) * 4])
                if rb == 0 and wT_wait is not None:
                    ew.append(wT_wait)
                ev = acopy(dst, srcp, waits=ew)
                evs.append(ev)
                bank_free[bk] = ev
        return evs[-1], evs

    prev_seq_tok = None
    warm_n = [0]
    for s in range(SEQS):
        def emit_warm(n):
            for _ in range(n):
                w = []
                if warm_n[0] == 0:
                    w = [tk_erev]
                elif warm_n[0] == 1000 * s:
                    w = [prev_seq_tok]
                mm(psum[:, 3 * 512: 4 * 512], idnb_[:, :], xfm[:, 0:512],
                   True, True, waits=w)
                warm_n[0] += 1
        if s == 1:
            warm_n[0] = 1000

        xr = x_in.rearrange("(t p) c -> t p c", p=128)
        x_ev = []
        bank_free = [tk_erev, tk_erev] if s == 0 else [None, None]
        xslots = [xbt[:, :], xf32[:, 0:1024], xf32[:, 1024:2048]]
        xslot_free = [None, None, None]
        xlts = []

        def emit_xload(tt):
            sl3 = tt % 3
            w = []
            if tt == 0 and prev_seq_tok is not None:
                w.append(prev_seq_tok)
            if xslot_free[sl3] is not None:
                w.append(xslot_free[sl3])
            xlts.append(dma(xslots[sl3], xr[s * 16 + tt], "sync", waits=w))

        emit_xload(0)
        for tt in range(16):
            sl3 = tt % 3
            if tt + 1 < 16:
                emit_xload(tt + 1)
                settle = xlts[tt + 1]
            else:
                settle = dma(scol[0:1, 0:8], hb["bo_row"][0:1, 0:8], "sync")
            trs = []
            for cb in range(8):
                bk = cb // 4
                tw = [xlts[tt], tk_id32]
                if cb == 0:
                    tw.append(settle)
                if cb % 4 == 0 and bank_free[bk] is not None:
                    tw.append(bank_free[bk])
                trs.append(tr(psum[:, bk * 512 + (cb % 4) * 128:
                                   bk * 512 + (cb % 4) * 128 + 128],
                              xslots[sl3][:, cb * 128:(cb + 1) * 128],
                              idn32[:, :], waits=tw))
            xslot_free[sl3] = trs[-1]
            emit_warm(3)
            if s == 0:
                emit_cast_chunk(3, 3)
            xfmt = xfm[:, :]
            for bk in range(2):
                dst = AP(xfmt.tensor, (4 * bk) * L + tt * 128, [[8 * L, 128], [L, 4], [1, 128]])
                srcp = psum[:, bk * 512:bk * 512 + 512].rearrange("p (a b) -> p a b", a=4)
                ev = acopy(dst, srcp, waits=trs[bk * 4:(bk + 1) * 4])
                x_ev.append(ev)
                bank_free[bk] = ev
        xf32_guard = [xslot_free[1], xslot_free[2]]

        wq_tok, _ = weight_transpose("wq", prev_seq_tok,
            bank_waits=((tk_erev, tk_erev) if s == 0 else (prev_seq_tok, prev_seq_tok)),
            wT_wait=prev_seq_tok)
        q_ev = []
        ev_par = [None, None]
        for db in range(8):
            for t4 in range(4):
                par = (4 * db + t4) % 2
                pb = pbank(2 + 2 * par)
                ms = []
                for cb in range(8):
                    w = []
                    if cb == 0:
                        w = [wq_tok] + (x_ev if db == 0 and t4 == 0 else [])
                        if ev_par[par] is not None:
                            w.append(ev_par[par])
                    ms.append(mm(pb, wT[:, cb * D + db * 128: cb * D + (db + 1) * 128],
                                 xfm[:, cb * L + t4 * 512: cb * L + (t4 + 1) * 512],
                                 cb == 0, cb == 7, waits=w))
                ev = act(qfm[:, db * L + t4 * 512: db * L + (t4 + 1) * 512], pb, AF.Identity,
                         bias=bias8[:, 16 + db:17 + db], scale=0.125,
                         waits=[ms[-1], tk_bqs])
                q_ev.append(ev)
                ev_par[par] = ev
                emit_warm(2)
                if s == 0:
                    emit_cast_chunk(1, 1)
        wk_tok, _ = weight_transpose("wk", q_ev[-1],
            bank_waits=(q_ev[-1], q_ev[-1]), wT_wait=q_ev[-1])
        k_ev = []
        ev_par = [None, None]
        for db in range(8):
            for t4 in range(4):
                par = (4 * db + t4) % 2
                pb = pbank(2 + 2 * par)
                ms = []
                for cb in range(8):
                    w = []
                    if cb == 0:
                        w = [wk_tok]
                        if ev_par[par] is not None:
                            w.append(ev_par[par])
                        elif par == (4 * 7 + 3) % 2:
                            w.append(q_ev[-1])
                        else:
                            w.append(q_ev[-2])
                    ms.append(mm(pb, wT[:, cb * D + db * 128: cb * D + (db + 1) * 128],
                                 xfm[:, cb * L + t4 * 512: cb * L + (t4 + 1) * 512],
                                 cb == 0, cb == 7, waits=w))
                ev = act(kfm[:, db * L + t4 * 512: db * L + (t4 + 1) * 512], pb, AF.Identity,
                         bias=bias8[:, 8 + db:9 + db], scale=1.0,
                         waits=[ms[-1], tk_b8b])
                k_ev.append(ev)
                ev_par[par] = ev
                emit_warm(2)
                if s == 0:
                    emit_cast_chunk(1, 1)
        wv_tok, _ = weight_transpose("wv", k_ev[-1],
            bank_waits=(k_ev[-1], k_ev[-1]), wT_wait=k_ev[-1])
        v_ev = []
        ev_par = [None, None]
        gi = 0
        for tt in range(16):
            for dvh in range(2):
                par = gi % 2
                gi += 1
                pb = pbank(2 + 2 * par)
                ms = []
                for cb in range(8):
                    w = []
                    if cb == 0:
                        w = [wv_tok]
                        if ev_par[par] is not None:
                            w.append(ev_par[par])
                        else:
                            w.append(k_ev[-1])
                    ms.append(mm(pb, xfm[:, cb * L + tt * 128: cb * L + (tt + 1) * 128],
                                 wT[:, cb * D + dvh * 512: cb * D + (dvh + 1) * 512],
                                 cb == 0, False, waits=w))
                ms.append(mm(pb, onesb[0:1, 0:128],
                             brow[0:1, dvh * 512:(dvh + 1) * 512], False, True,
                             waits=[tk_ones, tk_brow]))
                vt = vtm[:, :]
                dst = AP(vt.tensor, tt * HL + dvh * 8 * 65,
                         [[16 * HL, 128], [65, 8], [1, 64]])
                src = pb.rearrange("p (a b) -> p a b", a=8)
                ev = acopy(dst, src, waits=[ms[-1], tk_vones])
                v_ev.append(ev)
                ev_par[par] = ev
                emit_warm(2)
                if s == 0:
                    emit_cast_chunk(1, 1)

        if s == 0:
            cast_last = drain_casts()

        ctx_done = []
        g_free = [None, None]
        st_free = [None, None]
        p_free = [None, None]
        pv_free = [None, None]
        craw_free = [None, None]
        glue_free = tk_srow_free if s == 0 else None
        first_st = True
        epi = None
        m = 0

        def emit_epilogue_a(e):
            nonlocal glue_free
            hh_, qh_, pvb_, par_, pvl_ = e
            cr = craw[:, 0:1024]
            w = list(pvl_)
            if craw_free[0] is not None:
                w.append(craw_free[0])
            evt = acopy(cr[0:65, :], psum[0:65, pvb_ * 512: pvb_ * 512 + 1024], waits=w)
            return e + (evt,)

        def emit_sums(e):
            nonlocal glue_free
            hh_, qh_, pvb_, par_, pvl_, evt = e
            w = list(pvl_) + [evt]
            if glue_free is not None:
                w.append(glue_free)
            t3 = vop("reciprocal", srow[0:1, 0:1024],
                     psum[64:65, pvb_ * 512: pvb_ * 512 + 1024], waits=w)
            return e + (t3,)

        def emit_epilogue_b(e):
            nonlocal glue_free
            hh_, qh_, pvb_, par_, pvl_, evt, t3 = e
            hc_, hr_ = hh_ // 2, (hh_ % 2) * 64
            cr = craw[:, 0:1024]
            rep = []
            for jj in range(2):
                rep.append(mm(psum[0:64, (pvb_ + jj) * 512:(pvb_ + jj) * 512 + 512],
                              ones32[0:1, 0:64], srow[0:1, jj * 512:(jj + 1) * 512],
                              True, True, waits=[t3, tk_ones32, evt]))
            glue_free = rep[-1]
            nrm = []
            for jj in range(2):
                dst = xfm[hr_:hr_ + 64, hc_ * L + qh_ * 1024 + jj * 512:
                          hc_ * L + qh_ * 1024 + (jj + 1) * 512]
                w = [rep[jj], evt]
                if hh_ == 0 and qh_ == 0 and jj == 0:
                    w += v_ev
                nrm.append(vop("tensor_tensor", dst, cr[0:64, jj * 512:(jj + 1) * 512],
                               psum[0:64, (pvb_ + jj) * 512:(pvb_ + jj) * 512 + 512],
                               ALU.mult, waits=w))
            craw_free[0] = nrm[-1]
            pv_free[(pvb_ - 4) // 2] = nrm[-1]
            ctx_done.extend(nrm)

        for h in range(H):
            gpar = h % 2
            w = [tk_erev]
            if h < 2:
                w.append(q_ev[0])
                if s == 0:
                    w.append(cast_last)
            if g_free[gpar] is not None:
                w.append(g_free[gpar])
            ert = erev[:, :]
            gsrc = AP(ert.tensor, h * TBL, [[TBL, 1], [1, 128], [1, GW]])
            gtok = dma(gtab[:, gpar * GW:(gpar + 1) * GW], gsrc, "sync", waits=w)
            hc, hr = h // 2, (h % 2) * 64
            last_pm = None
            for qh in range(2):
                if epi is not None:
                    epi = emit_epilogue_a(epi)
                pvpair = m % 2
                pvb = 4 + 2 * pvpair
                pv_ms = []
                ets = {}
                pms = {}
                def emit_st(kt):
                    spar = kt % 2
                    ms = []
                    nonlocal first_st
                    for jj in range(2):
                        w = []
                        if jj == 0:
                            if first_st:
                                w = q_ev + k_ev + v_ev
                                first_st = False
                            if st_free[spar] is not None:
                                w.append(st_free[spar])
                        ms.append(mm(psum[:, (2 * spar) * 512 + jj * 512:
                                          (2 * spar) * 512 + (jj + 1) * 512],
                                     kfm[hr:hr + 64, hc * L + kt * 128: hc * L + (kt + 1) * 128],
                                     qfm[hr:hr + 64, hc * L + qh * 1024 + jj * 512:
                                         hc * L + qh * 1024 + (jj + 1) * 512],
                                     True, True, waits=w))
                    return ms
                def emit_exp_mult(kt, ms):
                    spar = kt % 2
                    ppar = kt % 2
                    w = ms + ([p_free[ppar]] if p_free[ppar] is not None else [])
                    et = act(ptile[:, ppar * 1024:(ppar + 1) * 1024],
                             psum[:, (2 * spar) * 512:(2 * spar) * 512 + 1024],
                             AF.Exp, bias=cstf[:, 0:1], waits=w)
                    st_free[spar] = et
                    off = 1024 - 1024 * qh + 128 * kt
                    gt_t = gtab[:, :]
                    grev = AP(gt_t.tensor, gpar * GW + off + 1023, [[2 * GW, 128], [-1, 1024]])
                    pm = vop("tensor_tensor",
                             ptile[:, ppar * 1024:(ppar + 1) * 1024],
                             ptile[:, ppar * 1024:(ppar + 1) * 1024],
                             grev, ALU.mult, waits=[et, gtok])
                    return pm
                def emit_pv(kt, pm):
                    ppar = kt % 2
                    for jj in range(2):
                        w = [pm]
                        if kt == 0 and pv_free[pvpair] is not None:
                            w.append(pv_free[pvpair])
                        pv_ms.append(mm(psum[0:65, (pvb + jj) * 512:(pvb + jj + 1) * 512],
                                        vtm[:, kt * HL + h * 65: kt * HL + h * 65 + 65],
                                        ptile[:, ppar * 1024 + jj * 512:
                                              ppar * 1024 + (jj + 1) * 512],
                                        kt == 0, kt == 15, waits=w))
                    p_free[ppar] = pv_ms[-1]
                ms0 = emit_st(0)
                pms[0] = emit_exp_mult(0, ms0)
                for kt in range(16):
                    if kt + 1 < 16:
                        msn = emit_st(kt + 1)
                        pms[kt + 1] = emit_exp_mult(kt + 1, msn)
                    emit_pv(kt, pms[kt])
                    if kt == 8 and epi is not None and len(epi) == 6:
                        epi = emit_sums(epi)
                last_pm = pms[15]
                if epi is not None:
                    emit_epilogue_b(epi)
                epi = (h, qh, pvb, m % 2, [pv_ms[-1], pv_ms[-2]])
                m += 1
            g_free[gpar] = last_pm
        epi = emit_epilogue_a(epi)
        epi = emit_sums(epi)
        emit_epilogue_b(epi)

        wo_tok, _ = weight_transpose("wo", v_ev[-1],
            bank_waits=(pv_free[1], pv_free[1]), wT_wait=v_ev[-1])
        xl_free = [None, None]
        ev_par = [None, None]
        lnst_free = None
        lnst_act_free = None
        first_o = True
        for tt in range(16):
            par = tt % 2
            w = []
            if xl_free[par] is not None:
                w.append(xl_free[par])
            else:
                w.append(xf32_guard[par])
            xl = dma(xf32[:, par * 1024:(par + 1) * 1024], xr[s * 16 + tt], "sync", waits=w)
            s1_toks = []
            for oh2 in range(2):
                mpar = (2 * tt + oh2) % 2
                pb = pbank(2 + 2 * mpar)
                ms = []
                for cb in range(8):
                    w = []
                    if cb == 0:
                        w = [wo_tok]
                        if first_o:
                            w += ctx_done
                            if BARRIER_OPROJ:
                                w += [pv_free, glue_free] + rep_free + \
                                     [t for t in p_free if t is not None] + \
                                     [t for t in craw_free if t is not None]
                            first_o = False
                        if ev_par[mpar] is not None:
                            w.append(ev_par[mpar])
                        else:
                            w += [t for t in st_free if t is not None]
                            w += [t for t in pv_free if t is not None]
                    ms.append(mm(pb, xfm[:, cb * L + tt * 128: cb * L + (tt + 1) * 128],
                                 wT[:, cb * D + oh2 * 512: cb * D + (oh2 + 1) * 512],
                                 cb == 0, False, waits=w))
                ms.append(mm(pb, onesb[0:1, 0:128],
                             brow[0:1, D + oh2 * 512: D + (oh2 + 1) * 512],
                             False, True, waits=[tk_brow2]))
                sl = xf32[:, par * 1024 + oh2 * 512: par * 1024 + (oh2 + 1) * 512]
                stk = vop("tensor_tensor", sl, pb, sl, ALU.add, waits=[ms[-1], xl])
                s1_toks.append(stk)
                ev_par[mpar] = stk
            s1ap = xf32[:, par * 1024:(par + 1) * 1024]
            if DEBUG_S1:
                sdr = s1dbg.rearrange("(t p) c -> t p c", p=128)
                s1_toks.append(dma(sdr[s * 16 + tt], s1ap, "sync", waits=list(s1_toks)))
                y_toks.append(s1_toks[-1])
            w = s1_toks + ([lnst_free] if lnst_free is not None else [])
            b1 = vop("bn_stats", lnst[:, 0:6], s1ap[:, 0:512], waits=w, mark=True)
            b2 = vop("bn_stats", lnst[:, 6:12], s1ap[:, 512:1024], waits=w, mark=True)
            b3 = vop("bn_aggr", lnst[:, 12:14], lnst[:, 0:12],
                     waits=[b1, b2] + ([lnst_act_free] if lnst_act_free is not None else []),
                     keep_same_engine=True)
            l1 = act(lnst[:, 14:15], lnst[:, 13:14], AF.Ln, bias=cstf[:, 1:2], waits=[b3, tk_cst])
            l2 = act(lnst[:, 15:16], lnst[:, 14:15], AF.Exp, scale=-0.5, waits=[l1])
            lnst_act_free = l1
            if DEBUG_S1:
                lnt = dma(lndbg[s * 16 + tt], lnst[:, :], "sync", waits=[l2, b3])
                y_toks.append(lnt)
            yn = vop("tensor_scalar", s1ap, s1ap, lnst[:, 12:13], lnst[:, 15:16],
                     ALU.subtract, ALU.mult,
                     waits=[l2] + ([s1_toks[-1], lnt] if DEBUG_S1 else []))
            lnst_free = yn
            ysr = y1scr.rearrange("(t p) c -> t p c", p=128)
            yst = dma(ysr[s * 16 + tt], s1ap, "sync", waits=[yn])
            y1_toks.append(yst)
            if DEBUG_Y1:
                ydr = y1dbg.rearrange("(t p) c -> t p c", p=128)
                y_toks.append(dma(ydr[s * 16 + tt], s1ap, "sync", waits=[yn]))
            xl_free[par] = yst
        prev_seq_tok = y1_toks[-1]

    stackA.close()
    if PHASE_B:
        env = dict(nc=nc, P=P, stack=stack, sbuf=sbuf, sb=sb, psum=psum, psb=psb,
                   hb=hb, wscr=wscr, y1scr=y1scr, y_out=y_out, y1_toks=y1_toks,
                   dma=dma, mm=mm, tr=tr, act=act, acopy=acopy, vop=vop,
                   onesb=onesb, idnb_=idnb_, idn32=idn32, cstf=cstf, lnst=lnst,
                   prev_seq_tok=prev_seq_tok, cast_last=cast_last, APc=AP)
        y_toks += _emit_ffn(env)

    with nc.Block() as block:
        P.emit(block, stack, y_toks if y_toks else [y1_toks[-1]])
    stack.close()
    return nc


def _emit_ffn(env):
    nc, P, sbuf = env["nc"], env["P"], env["sbuf"]
    psum, psb = env["psum"], env["psb"]
    hb, wscr, y1scr, y_out = env["hb"], env["wscr"], env["y1scr"], env["y_out"]
    y1_toks = env["y1_toks"]
    dma, mm, tr, act, acopy, vop = (env["dma"], env["mm"], env["tr"], env["act"],
                                    env["acopy"], env["vop"])
    onesb, idnb_, idn32, cstf, lnst = (env["onesb"], env["idnb_"], env["idn32"],
                                       env["cstf"], env["lnst"])
    gate = env["prev_seq_tok"]
    AP_ = env["APc"]

    y1t  = sbuf("y1t",  [128, 4096], F32)
    itm  = sbuf("itm",  [128, 4096], F32)
    vtm2 = sbuf("vtm2", [128, 4096], F32)
    ctxf = sbuf("ctxf", [128, 16 * 512], BF16)
    hf   = sbuf("hf",   [128, 4 * 512], BF16)
    h1f  = sbuf("h1f",  [128, 4 * 512], BF16)
    h2f  = sbuf("h2f",  [128, 4 * 512], BF16)
    m1r  = sbuf("m1r",  [128, 4 * 512], BF16)
    m2r  = sbuf("m2r",  [128, 4 * 512], BF16)
    hsh  = sbuf("hsh",  [65, 512], BF16)
    rhid = sbuf("rhid", [128, 2 * 512], BF16)
    hhid = sbuf("hhid", [128, 2 * 512], BF16)
    m1f  = sbuf("m1f",  [8, 512], BF16)
    m2f  = sbuf("m2f",  [8, 512], BF16)
    m1t  = sbuf("m1t",  [128, 64], BF16)
    mq   = sbuf("mq",   [128, 32], F32)
    cols = sbuf("cols", [128, 32], F32)
    ew   = sbuf("ew",   [128, 16 * 512], F32)
    wa   = sbuf("wa",   [128, 32768], BF16)
    rha  = sbuf("rha",  [128, 4116], BF16)
    murep = sbuf("murep", [128, D], F32)
    iwrep = sbuf("iwrep", [128, D], F32)
    bct  = sbuf("bct",  [128, 32], F32)
    sbt  = sbuf("sbt",  [64, 1], F32)
    rows = sbuf("rows", [1, D + 16], BF16)
    selb = sbuf("selb", [8, HC], BF16)

    W1O, W2O, S1O, S2O, B2O = 0, 8192, 20480, 21504, 24576
    R1O, R2O = 0, 16384
    RWO, HWO, RW2O, HW2O = 0, 2048, 4096, 4112

    toks_y = []
    dbgB = {}
    if DEBUG_B:
        for nm, sh in (("d_m1t", (128, 64)), ("d_cols", (128, 32)), ("d_mq", (128, 32)),
                       ("d_rhid", (128, 1024)), ("d_hf", (128, 2048)), ("d_hsh", (65, 512)),
                       ("d_h1f", (128, 2048)), ("d_m1r", (128, 2048)),
                       ("d_i0", (128, 4096)), ("d_v0", (128, 4096)),
                       ("d_i1", (128, 4096)), ("d_v1", (128, 4096)),
                       ("d_ln2in", (128, 4096)), ("d_ctxf", (128, 8192)),
                       ("d_sa", (128, 512)), ("d_sb", (128, 512)), ("d_sg", (128, 512)), ("d_vn", (128, 512)), ("d_g", (128, 512)), ("d_cols2", (128, 32)), ("d_sb1", (128, 512)), ("d_sb2", (128, 512)), ("d_refiw", (128, 512)), ("d_vn0", (128, 512)), ("d_vn1", (128, 512)), ("d_vn2", (128, 512))):
            dbgB[nm] = nc.dram_tensor(nm, list(sh), F32 if nm not in ("d_m1t", "d_hf", "d_h1f", "d_m1r", "d_ctxf", "d_rhid", "d_hsh") else BF16, kind="ExternalOutput").ap()
        dbgB["d_hidr"] = nc.dram_tensor("d_hidr", [128, 8192], BF16, kind="ExternalOutput").ap()

    ld = []
    ld.append(dma(murep[:, :], hb["mu_rep"][:, :], "sync", waits=[gate]))
    ld.append(dma(iwrep[:, :], hb["iw_rep"][:, :], "sync", waits=[gate]))
    ld.append(dma(bct[:, 0:2], hb["rb1T"][:, :], "sync", waits=[gate]))
    ld.append(dma(bct[:, 2:4], hb["hb1T"][:, :], "sync", waits=[gate]))
    ld.append(dma(bct[:, 4:8], hb["b1catT"][:, :], "sync", waits=[gate]))
    ld.append(dma(bct[:, 8:24], hb["refb1T"][:, :], "sync", waits=[gate]))
    ld.append(dma(bct[:, 24:25], hb["hb2rep"][:, :], "sync", waits=[gate]))
    ld.append(dma(bct[:, 25:26], hb["swrep"][:, :], "sync", waits=[gate]))
    ld.append(dma(sbt[:, :], hb["sb1T"][:, :], "sync", waits=[gate]))
    tk_b = ld[-1]
    t1 = vop("tensor_scalar_mul", bct[:, 26:27], bct[:, 24:25], 0.5, waits=[ld[6]])
    t2 = act(bct[:, 27:28], bct[:, 25:26], AF.Tanh, scale=0.5, waits=[ld[7]])
    t3 = vop("tensor_scalar", bct[:, 28:29], bct[:, 27:28], 0.5, 0.5,
             ALU.mult, ALU.add, waits=[t2])
    t4 = vop("tensor_scalar", bct[:, 29:30], bct[:, 28:29], -1.0, 1.0,
             ALU.mult, ALU.add, waits=[t3], keep_same_engine=True, mark=True)
    t5 = vop("tensor_scalar_mul", bct[:, 30:31], bct[:, 29:30], 0.5,
             waits=[t4], keep_same_engine=True, mark=True)
    tk_cols = t5
    tmpr = ew[0:1, 0:D]
    r1 = dma(tmpr, hb["refb2_row"][:, :], "sync", waits=[gate])
    r2 = acopy(rows[0:1, 0:D], tmpr, waits=[r1])
    r3 = dma(ew[0:1, 0:E], hb["rb2_row"][:, :], "sync", waits=[r2])
    r4 = acopy(rows[0:1, D:D + E], ew[0:1, 0:E], waits=[r3])
    tk_rows = r4
    tk_hones = vop("memset", hsh[64:65, :], 1.0, waits=[gate])
    tk_sel = dma(selb[:, :], hb["sel"][:, :], "sync", waits=[gate])
    def arena_load(dst_t, dst_off, scr_ap, rows_, cols_, planes, waits):
        srcv = scr_ap.rearrange("(a p) c -> p a c", p=128) if rows_ > 128 else scr_ap
        if rows_ > 128:
            dstap = AP_(dst_t[:, :].tensor, dst_off,
                        [[dst_t.shape[1], 128], [cols_, planes], [1, cols_]])
        else:
            dstap = AP_(dst_t[:, :].tensor, dst_off, [[dst_t.shape[1], rows_], [1, cols_]])
        return dma(dstap, srcv, "sync", waits=waits, slow=(cols_ < 8))
    lw = []
    gate_c = [gate] + ([env["cast_last"]] if env.get("cast_last") is not None else [])
    lw.append(arena_load(rha, RWO, wscr["rw1"], D, 256, 8, gate_c))
    lw.append(arena_load(rha, HWO, wscr["hw1"], D, 256, 8, gate_c))
    lw.append(arena_load(rha, RW2O, wscr["rw2"], 256, 8, 2, gate_c))
    lw.append(arena_load(rha, HW2O, wscr["hw2"], 256, 1, 2, gate_c))
    tk_rh = lw[-1]

    bank_rd = [gate] * 8

    def mmg(outbank, outslice, parts, waits_first=()):
        t = None
        n = len(parts)
        for i, (lh, rh) in enumerate(parts):
            w = list(waits_first) + [bank_rd[outbank]] if i == 0 else []
            t = mm(outslice, lh, rh, i == 0, i == n - 1, waits=w)
        return t

    prev_tile_done = gate
    warena_free = gate
    ew_war = [None]
    for ft in range(NFT):
        tbase = ft * FT
        ysr = y1scr
        ysv = AP_(ysr.tensor, tbase * D, [[D, 128], [128 * D, 4], [1, D]])
        wv = [y1_toks[tbase // 128 + k] for k in range(4)] + [prev_tile_done]
        ly1 = dma(y1t[:, :], ysv, "sync", waits=wv)
        icp = vop("tensor_copy", itm[:, :], y1t[:, :], waits=[ly1])

        def to_fm(srctile, planes0, ctoks):
            evs = []
            for tc in range(4):
                trs = []
                for cb in range(8):
                    bk = 6 + cb // 4
                    w = list(ctoks) if (tc == 0 and cb == 0) else []
                    if cb % 4 == 0:
                        w.append(bank_rd[bk])
                    trs.append(tr(psum[:, bk * 512 + (cb % 4) * 128:
                                       bk * 512 + (cb % 4) * 128 + 128],
                                  srctile[:, tc * 1024 + cb * 128: tc * 1024 + (cb + 1) * 128],
                                  idn32[:, :], waits=w))
                ctxt = ctxf[:, :]
                for bk in range(2):
                    dst = AP_(ctxt.tensor, (planes0 + 4 * bk) * 512 + tc * 128,
                              [[16 * 512, 128], [512, 4], [1, 128]])
                    srcp = psum[:, (6 + bk) * 512:(6 + bk) * 512 + 512].rearrange(
                        "p (a b) -> p a b", a=4)
                    ev = acopy(dst, srcp, waits=trs[bk * 4:(bk + 1) * 4])
                    evs.append(ev)
                    bank_rd[6 + bk] = ev
            return evs
        y1fm_ev = to_fm(y1t, 0, [ly1])

        rh_ev = []
        for hc in range(2):
            bk = hc % 2
            parts = [(rha[:, RWO + cb * 256 + hc * 128: RWO + cb * 256 + (hc + 1) * 128],
                      ctxf[:, cb * 512:(cb + 1) * 512]) for cb in range(8)]
            t = mmg(bk, psum[:, bk * 512:(bk + 1) * 512], parts,
                    waits_first=[tk_rh] + y1fm_ev)
            ev = act(rhid[:, hc * 512:(hc + 1) * 512], psum[:, bk * 512:(bk + 1) * 512],
                     AF.Gelu, bias=bct[:, 0 + hc:1 + hc], scale=1.0, waits=[t, tk_b])
            rh_ev.append(ev)
            bank_rd[bk] = ev
        lg_toks = []
        for tc in range(4):
            sl = psum[:, 2 * 512 + tc * 8: 2 * 512 + (tc + 1) * 8]
            parts = [(rhid[:, hc * 512 + tc * 128: hc * 512 + (tc + 1) * 128],
                      rha[:, RW2O + hc * 8: RW2O + (hc + 1) * 8]) for hc in range(2)]
            parts.append((onesb[0:1, 0:128], rows[0:1, D:D + E]))
            lg_toks.append(mmg(2, sl, parts, waits_first=rh_ev + [tk_rows]))
        mt_toks = []
        alpha_toks = []
        for tc in range(4):
            sl = psum[:, 2 * 512 + tc * 8: 2 * 512 + (tc + 1) * 8]
            c0 = tc * 8
            q1 = vop("reduce_max", cols[:, c0:c0 + 1], sl, axis=AX.X, waits=lg_toks)
            q2 = vop("tensor_single_scalar", mq[:, 0:8], sl, cols[:, c0:c0 + 1],
                     ALU.is_equal, waits=[q1], keep_same_engine=True, mark=True)
            q3 = vop("scalar_tensor_tensor", mq[:, 8:16], mq[:, 0:8], -1e30, sl,
                     ALU.mult, ALU.add, waits=[q2], keep_same_engine=True, mark=True)
            q4 = vop("reduce_max", cols[:, c0 + 1:c0 + 2], mq[:, 8:16], axis=AX.X,
                     waits=[q3], keep_same_engine=True, mark=True)
            q5 = vop("tensor_single_scalar", mq[:, 16:24], mq[:, 8:16],
                     cols[:, c0 + 1:c0 + 2], ALU.is_equal, waits=[q4],
                     keep_same_engine=True, mark=True)
            q6 = vop("tensor_copy", m1t[:, tc * 8: tc * 8 + 8], mq[:, 0:8], waits=[q2])
            q7 = vop("tensor_copy", m1t[:, 32 + tc * 8: 32 + tc * 8 + 8], mq[:, 16:24],
                     waits=[q5], keep_same_engine=True, mark=True)
            mt_toks.append((q6, q7))
            q8 = vop("tensor_sub", cols[:, c0 + 2:c0 + 3], cols[:, c0:c0 + 1],
                     cols[:, c0 + 1:c0 + 2], waits=[q4], keep_same_engine=True, mark=True)
            q9 = act(cols[:, c0 + 3:c0 + 4], cols[:, c0 + 2:c0 + 3], AF.Tanh, scale=0.5,
                     waits=[q8])
            qa = vop("tensor_scalar", cols[:, c0 + 4:c0 + 5], cols[:, c0 + 3:c0 + 4],
                     bct[:, 30:31], bct[:, 30:31], ALU.mult, ALU.add,
                     waits=[q9, tk_cols], mark=True)
            qb = vop("tensor_sub", cols[:, c0 + 5:c0 + 6], bct[:, 29:30],
                     cols[:, c0 + 4:c0 + 5], waits=[qa], keep_same_engine=True,
                     mark=True)
            alpha_toks.append(qb)
        bank_rd[2] = alpha_toks[-1]
        mtr = [[], []]
        for mi, (mtile_off, dstf) in enumerate(((0, m1f), (32, m2f))):
            bk = 6 + mi
            for tc in range(4):
                w = [mt_toks[tc][mi], env["idnb_"] and None] if False else [mt_toks[tc][mi]]
                if tc == 0:
                    w.append(bank_rd[bk])
                mtr[mi].append(tr(psb[0:8, bk * 1024 + tc * 128: bk * 1024 + (tc + 1) * 128],
                                  m1t[:, mtile_off + tc * 8: mtile_off + (tc + 1) * 8],
                                  idnb_[:, :], waits=w))
            ev = acopy(dstf[0:8, :], psb[0:8, bk * 1024: bk * 1024 + 512],
                       waits=mtr[mi])
            bank_rd[bk] = ev
            mtr[mi] = ev
        mrep_ev = []
        for mi, (mf, mr) in enumerate(((m1f, m1r), (m2f, m2r))):
            for hc in range(4):
                bk = hc % 2
                t = mmg(bk, psum[:, bk * 512:(bk + 1) * 512],
                        [(selb[0:8, hc * 128:(hc + 1) * 128], mf[0:8, :])],
                        waits_first=[mtr[mi], tk_sel])
                ev = acopy(mr[:, hc * 512:(hc + 1) * 512],
                           psum[:, bk * 512:(bk + 1) * 512], waits=[t])
                mrep_ev.append(ev)
                bank_rd[bk] = ev

        if DEBUG_B and ft == 0:
            toks_y.append(dma(dbgB["d_m1t"][:, :], m1t[:, :], "sync",
                              waits=[t for pair in mt_toks for t in pair]))
            toks_y.append(dma(dbgB["d_cols"][:, :], cols[:, :], "sync", waits=alpha_toks))
            toks_y.append(dma(dbgB["d_rhid"][:, :], rhid[:, :], "sync", waits=rh_ev))
            toks_y.append(dma(dbgB["d_m1r"][:, :], m1r[:, :], "sync", waits=mrep_ev))

        wvt = [warena_free]
        if ft == 0 and env.get("cast_last") is not None:
            wvt = wvt + [env["cast_last"]]
        l1 = arena_load(wa, W1O, wscr["w1cat"], C2, HC, 16, wvt)
        l2 = arena_load(wa, W2O, wscr["w2cat"], HC, O3, 4, wvt)
        l3 = arena_load(wa, S1O, wscr["sw1"], C2, EH, 16, wvt)
        l4 = arena_load(wa, S2O, wscr["sw2p"], EH + 1, O3, 1, wvt)
        l5 = arena_load(wa, B2O, wscr["b2cat"], E, O3, 1, wvt)
        lwt = [l1, l2, l3, l4, l5]

        ctrl_last = None
        for it in range(2):
            ncc = 8 if it == 0 else 16
            if it == 1:
                w_iv = [iup_last, vup_last]
                if DEBUG_B and ft == 0:
                    t_i0 = dma(dbgB["d_i0"][:, :], itm[:, :], "sync", waits=[iup_last])
                    t_v0 = dma(dbgB["d_v0"][:, :], vtm2[:, :], "sync", waits=[vup_last])
                    toks_y += [t_i0, t_v0]
                    w_iv += [t_i0, t_v0]
                ifm_ev = to_fm(itm, 0, w_iv)
                vfm_ev = to_fm(vtm2, 8, w_iv)
                ctx_ev = ifm_ev + vfm_ev
            else:
                ctx_ev = y1fm_ev
            h_ev = []
            for hc in range(4):
                bk = hc % 2
                parts = [(wa[:, W1O + cb * 512 + hc * 128: W1O + cb * 512 + (hc + 1) * 128],
                          ctxf[:, cb * 512:(cb + 1) * 512]) for cb in range(ncc)]
                t = mmg(bk, psum[:, bk * 512:(bk + 1) * 512], parts,
                        waits_first=[l1] + ctx_ev)
                ev = act(hf[:, hc * 512:(hc + 1) * 512], psum[:, bk * 512:(bk + 1) * 512],
                         AF.Gelu, bias=bct[:, 4 + hc:5 + hc], waits=[t, tk_b])
                h_ev.append(ev)
                bank_rd[bk] = ev
            parts = [(wa[0:128, S1O + cb * 64: S1O + (cb + 1) * 64],
                      ctxf[:, cb * 512:(cb + 1) * 512]) for cb in range(ncc)]
            t = mmg(2, psum[0:64, 2 * 512: 2 * 512 + 512], parts,
                    waits_first=[l3] + ctx_ev)
            hsh_ev = act(hsh[0:64, :], psum[0:64, 2 * 512: 2 * 512 + 512], AF.Gelu,
                         bias=sbt[:, 0:1], waits=[t, tk_b, tk_hones])
            bank_rd[2] = hsh_ev
            hm1 = vop("tensor_mul", h1f[:, :], hf[:, :], m1r[:, :],
                      waits=h_ev + mrep_ev)
            hm2 = vop("tensor_mul", h2f[:, :], hf[:, :], m2r[:, :],
                      waits=h_ev + mrep_ev, keep_same_engine=True, mark=True)
            hh_ev = []
            for hc in range(2):
                parts = [(rha[:, HWO + cb * 256 + hc * 128: HWO + cb * 256 + (hc + 1) * 128],
                          ctxf[:, cb * 512:(cb + 1) * 512]) for cb in range(8)]
                t = mmg(3, psum[:, 3 * 512: 3 * 512 + 512], parts,
                        waits_first=[tk_rh] + ctx_ev)
                ev = act(hhid[:, hc * 512:(hc + 1) * 512], psum[:, 3 * 512: 3 * 512 + 512],
                         AF.Gelu, bias=bct[:, 2 + hc:3 + hc], waits=[t, tk_b])
                hh_ev.append(ev)
                bank_rd[3] = ev
            if DEBUG_B and ft == 0 and it == 0:
                toks_y.append(dma(dbgB["d_hf"][:, :], hf[:, :], "sync", waits=h_ev))
                toks_y.append(dma(dbgB["d_hsh"][:, :], hsh[:, :], "sync", waits=[hsh_ev]))
                toks_y.append(dma(dbgB["d_h1f"][:, :], h1f[:, :], "sync", waits=[hm1]))
            hc_mm = []
            for tc in range(4):
                sl = psum[:, 2 * 512 + 16 + tc: 2 * 512 + 17 + tc]
                parts = [(hhid[:, hc * 512 + tc * 128: hc * 512 + (tc + 1) * 128],
                          rha[:, HW2O + hc: HW2O + hc + 1]) for hc in range(2)]
                hc_mm.append(mmg(2, sl, parts, waits_first=hh_ev))
            hcol_toks = []
            for tc in range(4):
                sl = psum[:, 2 * 512 + 16 + tc: 2 * 512 + 17 + tc]
                th = act(cols[:, tc * 8 + 6: tc * 8 + 7], sl, AF.Tanh, scale=0.5,
                         bias=bct[:, 26:27], waits=hc_mm + [t1])
                u1 = vop("tensor_scalar", cols[:, tc * 8 + 7: tc * 8 + 8],
                         cols[:, tc * 8 + 6: tc * 8 + 7], 0.5, 0.5, ALU.mult, ALU.add,
                         waits=[th], mark=True)
                u2 = vop("tensor_scalar", mq[:, 24 + tc: 25 + tc],
                         cols[:, tc * 8 + 6: tc * 8 + 7], 0.05, 0.05, ALU.mult, ALU.add,
                         waits=[th], mark=True)
                hcol_toks.append((u1, u2))
            bank_rd[2] = hcol_toks[-1][0]
            if DEBUG_B and ft == 0 and it == 0:
                toks_y.append(dma(dbgB["d_mq"][:, :], mq[:, :], "sync",
                                  waits=[t for p_ in hcol_toks for t in p_]))
                toks_y.append(dma(dbgB["d_ctxf"][0:128, 0:8192], ctxf[:, 0:8192],
                                  "sync", waits=ctx_ev))
                toks_y.append(dma(dbgB["d_cols2"][:, :], cols[:, :], "sync",
                                  waits=[t for p_ in hcol_toks for t in p_]))
            for tc in range(4):
                def ewb(i):
                    return ew[:, i * 512:(i + 1) * 512]
                for ch in range(2):
                    c0 = tc * 1024 + ch * 512
                    dtok = vop("tensor_sub", ewb(0), itm[:, c0:c0 + 512],
                               murep[:, ch * 512:(ch + 1) * 512],
                               waits=[icp if it == 0 else iup_last, ld[0]])
                    ev_parts = {}
                    s_wave = [("sh", "a"), ("sh", "g"), ("r1", "a"), ("r1", "g"),
                              ("r2", "a"), ("r2", "g")]
                    e_wave = [("sh", "b"), ("r1", "b"), ("r2", "b")]
                    order = (s_wave + e_wave) if (tc * 2 + ch) % 2 == 0 else                             (e_wave + s_wave)
                    for gi, (br, pname) in enumerate(order):
                        bi = ("sh", "r1", "r2").index(br)
                        pi = ("a", "b", "g").index(pname)
                        bk = 4 + gi % 2
                        war = ([ew_war[0][(br, pname)]]
                               if (ew_war[0] is not None and (br, pname) in ew_war[0])
                               else [])
                        off = pi * 1024 + ch * 512
                        if br == "sh":
                            parts = [(hsh[0:65, tc * 128:(tc + 1) * 128],
                                      wa[0:65, S2O + off: S2O + off + 512])]
                            wf = [l4, hsh_ev]
                        else:
                            mfm = m1f if br == "r1" else m2f
                            hxf = h1f if br == "r1" else h2f
                            hmt = hm1 if br == "r1" else hm2
                            parts = [(hxf[:, hc2 * 512 + tc * 128: hc2 * 512 + (tc + 1) * 128],
                                      wa[:, W2O + hc2 * 3072 + off: W2O + hc2 * 3072 + off + 512])
                                     for hc2 in range(4)]
                            parts.append((mfm[0:8, tc * 128:(tc + 1) * 128],
                                          wa[0:8, B2O + off: B2O + off + 512]))
                            wf = [l2, l5, hmt, mtr[0 if br == "r1" else 1]]
                        t = mmg(bk, psum[:, bk * 512:(bk + 1) * 512], parts,
                                waits_first=wf)
                        bufi = {"a": 1, "b": 7, "g": 4}[pname] + bi
                        if pname == "b":
                            e1 = act(ewb(bufi), psum[:, bk * 512:(bk + 1) * 512],
                                     AF.Exp, waits=[t] + war)
                            e2 = act(ewb(bufi), ewb(bufi), AF.Ln, bias=1.0,
                                     waits=[e1])
                            ev_parts[(br, pname)] = e2
                            bank_rd[bk] = e1
                        else:
                            e1 = act(ewb(bufi), psum[:, bk * 512:(bk + 1) * 512],
                                     AF.Sigmoid, waits=[t] + war)
                            ev_parts[(br, pname)] = e1
                            bank_rd[bk] = e1
                    if DEBUG_B and ft == 0 and it == 0 and tc == 0 and ch == 0:
                        evp = list(ev_parts.values())
                        for dn, bi_ in (("d_sa", 1), ("d_sb", 7), ("d_sg", 4),
                                        ("d_sb1", 8), ("d_sb2", 9)):
                            toks_y.append(dma(dbgB[dn][:, :], ewb(bi_), "sync", waits=evp))
                        dbg_ev = toks_y[-5:]
                        acopy(ew[0:1, 0:1], ew[0:1, 0:1], waits=dbg_ev)
                    else:
                        dbg_ev = []
                    wv_ = [dtok] + dbg_ev + list(ev_parts.values()) + [alpha_toks[tc],
                                                              hcol_toks[tc][0], hcol_toks[tc][1]]
                    ch_toks = []
                    readers = {}
                    first = True
                    for bi, br in enumerate(("sh", "r1", "r2")):
                        acol = (bct[:, 28:29] if br == "sh" else
                                cols[:, tc * 8 + 4 + (0 if br == "r1" else 1):
                                     tc * 8 + 5 + (0 if br == "r1" else 1)])
                        SA, SB, SG = ewb(1 + bi), ewb(7 + bi), ewb(4 + bi)
                        if it == 0:
                            na = mq[:, 28 + bi:29 + bi]
                            natok = vop("tensor_scalar_mul", na, acol, -1.0,
                                        waits=wv_ if first else [],
                                        keep_same_engine=True, mark=True)
                            ch_toks.append(natok)
                            vnp = ewb(12 + bi)
                            ch_toks.append(vop("scalar_tensor_tensor", vnp, ewb(0),
                                               na, SB, ALU.mult, ALU.mult,
                                               waits=[natok], keep_same_engine=True,
                                               mark=True))
                            readers[(br, "b")] = ch_toks[-1]
                        else:
                            A, B = ewb(10), ewb(11)
                            ch_toks.append(vop("scalar_tensor_tensor", A,
                                               vtm2[:, c0:c0 + 512], acol, SA,
                                               ALU.mult, ALU.mult,
                                               waits=(wv_ + [vup_last]) if first else [],
                                               keep_same_engine=True, mark=True))
                            readers[(br, "a")] = ch_toks[-1]
                            ch_toks.append(vop("scalar_tensor_tensor", B, ewb(0),
                                               acol, SB, ALU.mult, ALU.mult, waits=[],
                                               keep_same_engine=True, mark=True))
                            readers[(br, "b")] = ch_toks[-1]
                            vnp = ewb(12 + bi)
                            ch_toks.append(vop("tensor_sub", vnp, A, B, waits=[],
                                               keep_same_engine=True, mark=True))
                        first = False
                        if bi == 0:
                            ch_toks.append(vop("tensor_mul", ewb(15), SG, vnp, waits=[],
                                               keep_same_engine=True, mark=True))
                            readers[(br, "g")] = ch_toks[-1]
                        else:
                            ch_toks.append(vop("tensor_mul", ewb(10), SG, vnp, waits=[],
                                               keep_same_engine=True, mark=True))
                            readers[(br, "g")] = ch_toks[-1]
                            ch_toks.append(vop("tensor_add", ewb(15), ewb(15), ewb(10),
                                               waits=[], keep_same_engine=True, mark=True))
                    ch_toks.append(vop("tensor_add", ewb(11), ewb(12), ewb(13), waits=[],
                                       keep_same_engine=True, mark=True))
                    ch_toks.append(vop("tensor_add", ewb(11), ewb(11), ewb(14), waits=[],
                                       keep_same_engine=True, mark=True))
                    if DEBUG_B and ft == 0 and it == 0 and tc == 0 and ch == 0:
                        tvn = dma(dbgB["d_vn"][:, :], ewb(11), "sync", waits=[ch_toks[-1]])
                        tgg = dma(dbgB["d_g"][:, :], ewb(15), "sync", waits=[ch_toks[-1]])
                        for dn, bi_ in (("d_vn0", 12), ("d_vn1", 13), ("d_vn2", 14)):
                            toks_y.append(dma(dbgB[dn][:, :], ewb(bi_), "sync",
                                              waits=[ch_toks[-1]]))
                        tvn2 = toks_y[-3:]
                        toks_y += [tvn, tgg]
                        tvn = tvn; tgg = tgg
                        ch_toks.append(vop("tensor_copy", ew[0:1, 0:1], ew[0:1, 0:1],
                                           waits=[tvn, tgg] + tvn2))
                    iup = vop("scalar_tensor_tensor", itm[:, c0:c0 + 512], ewb(15),
                              mq[:, 24 + tc: 25 + tc], itm[:, c0:c0 + 512],
                              ALU.mult, ALU.add, waits=[hcol_toks[tc][1]],
                              keep_same_engine=True, mark=True)
                    if it == 0:
                        vup = vop("tensor_scalar_mul", vtm2[:, c0:c0 + 512], ewb(11),
                                  cols[:, tc * 8 + 7: tc * 8 + 8],
                                  waits=[hcol_toks[tc][0]],
                                  keep_same_engine=True, mark=True)
                    else:
                        ch_toks.append(vop("tensor_sub", ewb(10), ewb(11),
                                           vtm2[:, c0:c0 + 512], waits=[],
                                           keep_same_engine=True, mark=True))
                        vup = vop("scalar_tensor_tensor", vtm2[:, c0:c0 + 512], ewb(10),
                                  cols[:, tc * 8 + 7: tc * 8 + 8], vtm2[:, c0:c0 + 512],
                                  ALU.mult, ALU.add, waits=[hcol_toks[tc][0]],
                                  keep_same_engine=True, mark=True)
                    iup_last, vup_last = iup, vup
                    ew_war[0] = readers
            ctrl_last = iup_last

        if DEBUG_B and ft == 0:
            t_i1 = dma(dbgB["d_i1"][:, :], itm[:, :], "sync", waits=[iup_last])
            t_v1 = dma(dbgB["d_v1"][:, :], vtm2[:, :], "sync", waits=[vup_last])
            toks_y += [t_i1, t_v1]
            dbg_iv = [t_i1, t_v1]
        else:
            dbg_iv = []

        lr1 = arena_load(wa, R1O, wscr["refw1"], D, C2, 8, [ctrl_last] + dbg_iv)
        lr2 = arena_load(wa, R2O, wscr["refw2"], C2, D, 16, [ctrl_last])
        rfm_ev = to_fm(itm, 0, [iup_last] + dbg_iv)
        def hidr_plane(hc):
            tens = (hf, h1f, h2f, m1r)[hc // 4]
            return tens[:, (hc % 4) * 512:((hc % 4) + 1) * 512]
        hr_ev = []
        for hc in range(16):
            bk = hc % 2
            parts = [(wa[:, R1O + cb * 2048 + hc * 128: R1O + cb * 2048 + (hc + 1) * 128],
                      ctxf[:, cb * 512:(cb + 1) * 512]) for cb in range(8)]
            t = mmg(bk, psum[:, bk * 512:(bk + 1) * 512], parts,
                    waits_first=[lr1] + rfm_ev + ([hm1, hm2] if hc == 4 else []) +
                                (mrep_ev if hc == 12 else []))
            ev = act(hidr_plane(hc), psum[:, bk * 512:(bk + 1) * 512], AF.Gelu,
                     bias=bct[:, 8 + hc:9 + hc],
                     waits=[t, tk_b] + ([hm1, hm2] if hc in (4, 8, 12) else []))
            hr_ev.append(ev)
            bank_rd[bk] = ev
        if DEBUG_B and ft == 0:
            hr_t = dma(dbgB["d_hidr"][:, 0:2048], hf[:, :], "sync", waits=hr_ev)
            hr_t2 = dma(dbgB["d_hidr"][:, 2048:4096], h1f[:, :], "sync", waits=hr_ev)
            hr_t3 = dma(dbgB["d_hidr"][:, 4096:6144], h2f[:, :], "sync", waits=hr_ev)
            hr_t4 = dma(dbgB["d_hidr"][:, 6144:8192], m1r[:, :], "sync", waits=hr_ev)
            toks_y += [hr_t, hr_t2, hr_t3, hr_t4]
        ln_in_toks = []
        for tc in range(4):
            for chf in range(2):
                bk = 2 + (2 * tc + chf) % 2
                c0 = tc * 1024 + chf * 512
                parts = [(hidr_plane(hc)[:, tc * 128:(tc + 1) * 128],
                          wa[:, R2O + hc * 1024 + chf * 512: R2O + hc * 1024 + chf * 512 + 512])
                         for hc in range(16)]
                parts.append((onesb[0:1, 0:128], rows[0:1, chf * 512: (chf + 1) * 512]))
                t = mmg(bk, psum[:, bk * 512:(bk + 1) * 512], parts,
                        waits_first=hr_ev + [lr2, tk_rows])
                u1 = vop("tensor_mul", ew[:, 0:512], psum[:, bk * 512:(bk + 1) * 512],
                         iwrep[:, (c0 % 1024):(c0 % 1024) + 512], waits=[t, ld[1]])
                if DEBUG_B and ft == 0 and tc == 0 and chf == 0:
                    t_ri = dma(dbgB["d_refiw"][:, :], ew[:, 0:512], "sync", waits=[u1])
                    toks_y.append(t_ri)
                    u1 = t_ri
                u2 = vop("scalar_tensor_tensor", y1t[:, c0:c0 + 512], y1t[:, c0:c0 + 512],
                         2.0, ew[:, 0:512], ALU.mult, ALU.add, waits=[u1],
                         keep_same_engine=True, mark=True)
                ln_in_toks.append(u2)
                bank_rd[bk] = u2
        if DEBUG_B and ft == 0:
            t_li = dma(dbgB["d_ln2in"][:, :], y1t[:, :], "sync", waits=ln_in_toks)
            toks_y.append(t_li)
            ln_in_toks = ln_in_toks + [t_li]
        lnf = None
        lnaf = None
        for tc in range(4):
            s1ap = y1t[:, tc * 1024:(tc + 1) * 1024]
            w = [ln_in_toks[2 * tc], ln_in_toks[2 * tc + 1]] + ([lnf] if lnf else [])
            b1 = vop("bn_stats", lnst[:, 0:6], s1ap[:, 0:512], waits=w, mark=True)
            b2 = vop("bn_stats", lnst[:, 6:12], s1ap[:, 512:1024], waits=w, mark=True)
            b3 = vop("bn_aggr", lnst[:, 12:14], lnst[:, 0:12],
                     waits=[b1, b2] + ([lnaf] if lnaf else []), keep_same_engine=True)
            l1_ = act(lnst[:, 14:15], lnst[:, 13:14], AF.Ln, bias=cstf[:, 1:2],
                      waits=[b3])
            l2_ = act(lnst[:, 15:16], lnst[:, 14:15], AF.Exp, scale=-0.5, waits=[l1_])
            lnaf = l1_
            yn = vop("tensor_scalar", s1ap, s1ap, lnst[:, 12:13], lnst[:, 15:16],
                     ALU.subtract, ALU.mult, waits=[l2_])
            lnf = yn
            yr = y_out.rearrange("(t p) c -> t p c", p=128)
            yst = dma(yr[(tbase // 128) + tc], s1ap, "sync", waits=[yn])
            toks_y.append(yst)
        prev_tile_done = toks_y[-1]
        warena_free = ln_in_toks[-1]
    return toks_y


def _run(inputs):
    consts = _host_consts()
    lay = _host_layout(inputs)
    x = np.asarray(inputs["L0_attn_x"])
    nc = build_nc()
    in_maps = []
    for c in range(N_CORES):
        m = dict(lay)
        m.update(consts)
        m["x"] = np.ascontiguousarray(x[2 * c:2 * c + 2].reshape(TOK, D))
        in_maps.append(m)
    res = run_bass_kernel_spmd(nc, in_maps, core_ids=list(range(N_CORES)))
    return res


def kernel(**inputs):
    res = _run(inputs)
    return np.concatenate(
        [res.results[c]["y"].reshape(2, L, D) for c in range(N_CORES)], 0)


# version 10
# speedup vs baseline: 1.1799x; 1.0149x over previous
import numpy as np
import ml_dtypes
import contextlib
import concourse.bass as bass
import concourse.mybir as mybir
from concourse.bass_utils import run_bass_kernel_spmd
from concourse.ap import AP

F32 = mybir.dt.float32
BF16 = mybir.dt.bfloat16
AF = mybir.ActivationFunctionType
ALU = mybir.AluOpType
AX = mybir.AxisListType
BF = ml_dtypes.bfloat16

N_CORES = 8
SEQS = 2
L = 2048
D = 1024
H = 16
HD = 64
TOK = SEQS * L
CSH = 24.0
EPS = 1e-5
NBK = 32
TBL = 4095
GW = 3968
E = 8
EH = 64
C2 = 2048
HC = 512
O3 = 3072
FT = 512
NFT = TOK // FT
DT_ = 0.1

PHASE_B = True
DEBUG_Y1 = False
DEBUG_ATT = False
DEBUG_S1 = False
BARRIER_OPROJ = False
DEBUG_B = False

HL = H * 65



def _bucket_of_delta(delta):
    n = np.asarray(delta, np.int64)
    out = np.zeros_like(n)
    small = (n >= 1) & (n < 16)
    out[small] = n[small]
    big = n >= 16
    nf = n[big].astype(np.float32)
    v = np.log(nf / np.float32(16.0)) / np.float32(np.log(128.0 / 16.0)) * np.float32(16.0)
    out[big] = np.minimum(16 + v.astype(np.int32), 31)
    return out


def _host_consts():
    c = {}
    c["idn32"] = np.eye(128, dtype=np.float32)
    c["idnb"] = np.eye(128, dtype=np.float32).astype(BF)
    m = np.arange(TBL)
    bk = _bucket_of_delta(2047 - m)
    oh = np.zeros((NBK, TBL), np.float32)
    oh[bk, m] = 1.0
    c["ohrev"] = oh.astype(BF)
    sel = np.zeros((E, HC), np.float32)
    for e in range(E):
        sel[e, e * EH:(e + 1) * EH] = 1.0
    c["sel"] = sel.astype(BF)
    c["cstf"] = np.ascontiguousarray(
        np.broadcast_to(np.array([[-CSH, EPS]], np.float32), (128, 2)))
    return c


def _host_layout(inp):
    a = {}
    g = lambda k: np.asarray(inp[k])
    for nm in ("wq", "wk", "wv", "wo"):
        a[nm] = np.ascontiguousarray(g("L0_attn_" + nm))
    a["bqT"] = np.ascontiguousarray(g("L0_attn_bq").reshape(8, 128).T)
    a["bkT"] = np.ascontiguousarray(g("L0_attn_bk").reshape(8, 128).T)
    a["bv_row"] = np.ascontiguousarray(g("L0_attn_bv").reshape(1, D))
    a["bo_row"] = np.ascontiguousarray(g("L0_attn_bo").reshape(1, D))
    a["rel_bias"] = np.ascontiguousarray(g("L0_attn_rel_bias"))
    a["rw1"] = np.ascontiguousarray(g("L0_ffn_router_w1"))
    a["rb1T"] = np.ascontiguousarray(g("L0_ffn_router_b1").reshape(2, 128).T)
    a["rw2"] = np.ascontiguousarray(g("L0_ffn_router_w2"))
    a["rb2_row"] = np.ascontiguousarray(g("L0_ffn_router_b2").reshape(1, E))
    a["hw1"] = np.ascontiguousarray(g("L0_ffn_halt_w1"))
    a["hb1T"] = np.ascontiguousarray(g("L0_ffn_halt_b1").reshape(2, 128).T)
    a["hw2"] = np.ascontiguousarray(g("L0_ffn_halt_w2"))
    a["hb2rep"] = np.ascontiguousarray(
        np.broadcast_to(g("L0_ffn_halt_b2").reshape(1, 1), (128, 1)))
    ew1 = g("L0_ffn_expert_w1")
    a["w1cat"] = np.ascontiguousarray(np.transpose(ew1, (1, 0, 2)).reshape(C2, HC))
    a["b1catT"] = np.ascontiguousarray(g("L0_ffn_expert_b1").reshape(-1).reshape(4, 128).T)
    a["w2cat"] = np.ascontiguousarray(g("L0_ffn_expert_w2").reshape(HC, O3))
    a["b2cat"] = np.ascontiguousarray(g("L0_ffn_expert_b2"))
    a["sw1"] = np.ascontiguousarray(g("L0_ffn_shared_w1"))
    a["sb1T"] = np.ascontiguousarray(g("L0_ffn_shared_b1").reshape(64, 1))
    a["sw2p"] = np.ascontiguousarray(
        np.concatenate([g("L0_ffn_shared_w2"), g("L0_ffn_shared_b2").reshape(1, O3)], 0))
    a["swrep"] = np.ascontiguousarray(
        np.broadcast_to(np.asarray(g("L0_ffn_shared_weight"), np.float32).reshape(1, 1), (128, 1)))
    a["refw1"] = np.ascontiguousarray(g("L0_ffn_refine_w1"))
    a["refb1T"] = np.ascontiguousarray(g("L0_ffn_refine_b1").reshape(16, 128).T)
    a["refw2"] = np.ascontiguousarray(g("L0_ffn_refine_w2"))
    a["refb2_row"] = np.ascontiguousarray(g("L0_ffn_refine_b2").reshape(1, D))
    a["mu_rep"] = np.ascontiguousarray(np.broadcast_to(g("L0_ffn_mu").reshape(1, D), (128, D)))
    a["iw_rep"] = np.ascontiguousarray(
        np.broadcast_to(g("L0_ffn_integration_weight").reshape(1, D), (128, D)))
    return a



ENGS = ("tensor", "vector", "scalar", "gpsimd", "sync")
DMA_ENGS = ("gpsimd", "sync")


class Tok:
    __slots__ = ("eng", "idx")
    def __init__(self, eng, idx):
        self.eng, self.idx = eng, idx


class Prog:
    def __init__(self, nc):
        self.nc = nc
        self.ops = {e: [] for e in ENGS}

    def op(self, eng, fn, waits=(), mark=False, keep_same_engine=False):
        ws = []
        for w in waits:
            if w is None or (w.eng == eng and eng not in DMA_ENGS
                             and not keep_same_engine):
                continue
            ws.append(w)
        self.ops[eng].append([fn, ws, mark, 0])
        return Tok(eng, len(self.ops[eng]) - 1)

    def emit(self, block, stack, final_toks):
        sems = {e: stack.enter_context(self.nc.semaphore("sem_" + e)) for e in ENGS}
        for e in DMA_ENGS:
            for rec in self.ops[e]:
                rec[2] = True
        for e in ENGS:
            for rec in self.ops[e]:
                for w in rec[1]:
                    self.ops[w.eng][w.idx][2] = True
        for w in final_toks:
            self.ops[w.eng][w.idx][2] = True
        for e in ENGS:
            acc = 0
            step = 16 if e in DMA_ENGS else 1
            for rec in self.ops[e]:
                if rec[2]:
                    acc += step
                rec[3] = acc

        def mk_stream(e):
            ops = self.ops[e]
            step = 16 if e in DMA_ENGS else 1
            def body(eng):
                for fn, ws, marked, val in ops:
                    byeng = {}
                    for w in ws:
                        wv = self.ops[w.eng][w.idx][3]
                        byeng[w.eng] = max(byeng.get(w.eng, 0), wv)
                    for we, wv in sorted(byeng.items()):
                        eng.wait_ge(sems[we], wv)
                    inst = fn(eng)
                    if marked:
                        inst.then_inc(sems[e], step)
            return body

        for e in ENGS:
            if self.ops[e]:
                getattr(block, e)(mk_stream(e))
        byeng = {}
        for w in final_toks:
            byeng[w.eng] = max(byeng.get(w.eng, 0), self.ops[w.eng][w.idx][3])
        def fin(eng):
            for we, wv in sorted(byeng.items()):
                if we != "sync":
                    eng.wait_ge(sems[we], wv)
            if "sync" in byeng:
                eng.wait_ge(sems["sync"], byeng["sync"])
        block.sync(fin)



def build_nc():
    nc = bass.Bass("TRN2", target_bir_lowering=False, debug=False)
    dt = nc.dram_tensor

    x_in = dt("x", [TOK, D], F32, kind="ExternalInput").ap()
    y_out = dt("y", [TOK, D], F32, kind="ExternalOutput").ap()
    y1scr = dt("y1scr", [TOK, D], F32, kind="Internal").ap()
    y1dbg = dt("y1dbg", [TOK, D], F32, kind="ExternalOutput").ap() if DEBUG_Y1 else None
    if DEBUG_ATT:
        dbg = {nm: dt(nm, [128, 8 * L], BF16, kind="ExternalOutput").ap()
               for nm in ("qdbg", "kdbg", "ctxdbg")}
        dbg["vdbg"] = dt("vdbg", [128, 16 * HL], BF16, kind="ExternalOutput").ap()
        dbg["crawdbg"] = dt("crawdbg", [128, 1024], F32, kind="ExternalOutput").ap()
        dbg["srowdbg"] = dt("srowdbg", [1, 1024], F32, kind="ExternalOutput").ap()
    s1dbg = dt("s1dbg", [TOK, D], F32, kind="ExternalOutput").ap() if DEBUG_S1 else None
    lndbg = dt("lndbg", [32, 128, 16], F32, kind="ExternalOutput").ap() if DEBUG_S1 else None

    hb = {}
    for nm, sh, dd in [
            ("wq", (D, D), F32), ("wk", (D, D), F32), ("wv", (D, D), F32), ("wo", (D, D), F32),
            ("bqT", (128, 8), F32), ("bkT", (128, 8), F32),
            ("bv_row", (1, D), F32), ("bo_row", (1, D), F32), ("rel_bias", (NBK, H), F32),
            ("rw1", (D, 256), F32), ("rb1T", (128, 2), F32), ("rw2", (256, E), F32),
            ("rb2_row", (1, E), F32),
            ("hw1", (D, 256), F32), ("hb1T", (128, 2), F32), ("hw2", (256, 1), F32),
            ("hb2rep", (128, 1), F32),
            ("w1cat", (C2, HC), F32), ("b1catT", (128, 4), F32), ("w2cat", (HC, O3), F32),
            ("b2cat", (E, O3), F32),
            ("sw1", (C2, EH), F32), ("sb1T", (EH, 1), F32), ("sw2p", (EH + 1, O3), F32),
            ("swrep", (128, 1), F32),
            ("refw1", (D, C2), F32), ("refb1T", (128, 16), F32), ("refw2", (C2, D), F32),
            ("refb2_row", (1, D), F32),
            ("mu_rep", (128, D), F32), ("iw_rep", (128, D), F32),
            ("idn32", (128, 128), F32), ("idnb", (128, 128), BF16),
            ("cstf", (128, 2), F32),
            ("ohrev", (NBK, TBL), BF16), ("sel", (E, HC), BF16)]:
        hb[nm] = dt(nm, list(sh), dd, kind="ExternalInput").ap()

    WSCR_SHAPES = {"w1cat": (C2, HC), "w2cat": (HC, O3), "sw1": (C2, EH),
                   "sw2p": (EH + 1, O3), "rw1": (D, 256), "hw1": (D, 256),
                   "refw1": (D, C2), "refw2": (C2, D), "rw2": (256, E),
                   "hw2": (256, 1), "b2cat": (E, O3)}
    wscr = {nm: dt("scr_" + nm, list(sh), BF16, kind="Internal").ap()
            for nm, sh in WSCR_SHAPES.items()}

    stack = contextlib.ExitStack()
    stackA = contextlib.ExitStack()
    sb = {}
    def sbuf(name, shape, dtype, st=None):
        sb[name] = (st or stack).enter_context(nc.sbuf_tensor(name, list(shape), dtype))
        return sb[name]
    def sbufA(name, shape, dtype):
        return sbuf(name, shape, dtype, st=stackA)

    onesb = sbuf("onesb", [1, 128], BF16)

    lnst  = sbuf("lnst",  [128, 16], F32)
    idn32 = sbuf("idn32s", [128, 128], F32)
    idnb_ = sbuf("idnbs", [128, 128], BF16)
    cstf  = sbuf("cstfs", [128, 2], F32)
    xbt   = sbufA("xbt",   [128, D], F32)
    xfm   = sbufA("xfm",   [128, 8 * L], BF16)
    qfm   = sbufA("qfm",   [128, 8 * L], BF16)
    kfm   = sbufA("kfm",   [128, 8 * L], BF16)
    vtm   = sbufA("vtm",   [128, 16 * HL], BF16)
    wtm   = sbufA("wtm",   [128, 2 * D], F32)
    wT    = sbufA("wT",    [128, 8 * D], BF16)
    erev  = sbufA("erev",  [H, TBL], BF16)
    ptile = sbufA("ptile", [128, 2 * 1024], BF16)
    craw  = sbufA("craw",  [128, 1024], F32)
    xf32  = sbufA("xf32",  [128, 2 * 1024], F32)
    srow  = sbufA("srow",  [1, 1024], F32)
    scol  = sbufA("scol",  [128, 16], F32)
    brow  = sbufA("brow",  [1, 2 * D], BF16)
    bias8 = sbufA("bias8", [128, 24], F32)
    relb  = sbufA("relb",  [NBK, H], BF16)
    relb32 = sbufA("relb32", [NBK, H], F32)
    ones32 = sbufA("ones32", [1, 64], F32)

    psum = stack.enter_context(nc.psum_tensor("PS", [128, 4096], F32))
    psb = psum[:, :].bitcast(BF16)

    def pbank(i, w=1):
        return psum[:, i * 512:(i + w) * 512]

    def pbankb(i, off, w):
        return psb[:, i * 1024 + off: i * 1024 + off + w]

    P = Prog(nc)

    def dma(dst, src, eng="sync", waits=(), slow=False):
        def fn(e, dst=dst, src=src, slow=slow):
            return e.dma_start(out=dst, in_=src, allow_slow_non_contiguous=slow)
        return P.op(eng, fn, waits)

    def mm(out, lhsT, rhs, start, stop, waits=()):
        def fn(e, out=out, lhsT=lhsT, rhs=rhs, start=start, stop=stop):
            return e.matmul(out, lhsT, rhs, start=start, stop=stop)
        return P.op("tensor", fn, waits)

    def tr(out, in_, ident, waits=()):
        def fn(e, out=out, in_=in_, ident=ident):
            return e.transpose(out, in_, ident)
        return P.op("tensor", fn, waits)

    def act(out, in_, func, bias=0.0, scale=1.0, waits=()):
        def fn(e, out=out, in_=in_, func=func, bias=bias, scale=scale):
            return e.activation(out, in_, func, bias=bias, scale=scale)
        return P.op("scalar", fn, waits)

    def acopy(out, in_, waits=()):
        def fn(e, out=out, in_=in_):
            return e.copy(out, in_)
        return P.op("scalar", fn, waits)

    def vop(name, *args, waits=(), mark=False, keep_same_engine=False, **kw):
        def fn(e, name=name, args=args, kw=kw):
            return getattr(e, name)(*args, **kw)
        return P.op("vector", fn, waits, mark=mark, keep_same_engine=keep_same_engine)

    tk_id32 = dma(idn32[:, :], hb["idn32"][:, :])
    tk_cst = dma(cstf[:, :], hb["cstf"][:, :])
    tk_idb = dma(idnb_[:, :], hb["idnb"][:, :])
    ohv_stack = contextlib.ExitStack()
    ohv = ohv_stack.enter_context(nc.sbuf_tensor("ohv", [NBK, TBL], BF16))
    tk_relb0 = dma(relb32[:, :], hb["rel_bias"][:, :], "sync")
    tk_relb = vop("tensor_copy", relb[:, :], relb32[:, :], waits=[tk_relb0])
    tk_oh = dma(ohv[:, :], hb["ohrev"][:, :], "sync")
    tk_ones = vop("memset", onesb[:, :], 1.0)
    tk_ones32 = vop("memset", ones32[:, :], 1.0)
    tk_srow1 = dma(srow[0:1, :], hb["bv_row"][:, :], "sync")
    tk_brow = acopy(brow[0:1, 0:D], srow[0:1, :], waits=[tk_srow1])
    tk_srow2 = dma(srow[0:1, :], hb["bo_row"][:, :], "sync", waits=[tk_brow])
    tk_brow2 = acopy(brow[0:1, D:2 * D], srow[0:1, :], waits=[tk_srow2])
    tk_srow_free = tk_brow2
    tk_b8 = dma(bias8[:, 0:8], hb["bqT"][:, :])
    tk_b8b = dma(bias8[:, 8:16], hb["bkT"][:, :])
    tk_bqs = vop("tensor_scalar_mul", bias8[:, 16:24], bias8[:, 0:8], 0.125,
                 waits=[tk_b8])

    erev_mm = []
    for j in range(8):
        c0, c1 = j * 512, min((j + 1) * 512, TBL)
        erev_mm.append(mm(psum[0:H, c0:c1], relb[:, :], ohv[:, c0:c1], True, True,
                          waits=[tk_relb, tk_oh]))
    tk_erev = act(erev[:, :], psum[0:H, 0:TBL], AF.Exp, waits=erev_mm)
    ohv_stack.close()
    gtab = sbufA("gtab", [128, 2 * GW], BF16)

    tk_vones = vop("memset", vtm[:, :], 1.0)

    pieces = []
    for nm in WSCR_SHAPES:
        rows, cols = WSCR_SHAPES[nm]
        srct, dsth = hb[nm], wscr[nm]
        nrb = (rows + 127) // 128
        for rb in range(nrb):
            r0 = rb * 128
            rr = min(128, rows - r0)
            for cc0 in range(0, cols, 1024):
                cc1 = min(cc0 + 1024, cols)
                pieces.append((srct, dsth, r0, rr, cc0, cc1))
    NSL = 6
    cast_lds = []
    cast_sts = []
    ncast = len(pieces)

    def _cast_load():
        ci = len(cast_lds)
        if ci >= ncast:
            return
        srct, dsth, r0, rr, cc0, cc1 = pieces[ci]
        sl = ci % NSL
        w = []
        if ci >= NSL:
            w.append(cast_sts[ci - NSL])
        if ci == 0:
            w.append(tk_erev)
        cast_lds.append(dma(gtab[0:rr, sl * 1024: sl * 1024 + cc1 - cc0],
                            srct[r0:r0 + rr, cc0:cc1], "gpsimd", waits=w))

    def _cast_store():
        ci = len(cast_sts)
        if ci >= ncast or len(cast_lds) < min(ci + 2, ncast):
            return
        srct, dsth, r0, rr, cc0, cc1 = pieces[ci]
        sl = ci % NSL
        if ci + 1 < ncast:
            settle = cast_lds[ci + 1]
        else:
            settle = dma(scol[0:1, 8:16], hb["bo_row"][0:1, 0:8], "gpsimd",
                         waits=[cast_lds[ci]])
        cast_sts.append(dma(dsth[r0:r0 + rr, cc0:cc1],
                            gtab[0:rr, sl * 1024: sl * 1024 + cc1 - cc0], "sync",
                            waits=[cast_lds[ci], settle]))

    def emit_cast_chunk(nl, ns):
        for _ in range(nl):
            _cast_load()
        for _ in range(ns):
            _cast_store()

    def drain_casts():
        while len(cast_lds) < ncast or len(cast_sts) < ncast:
            emit_cast_chunk(2, 2)
        return cast_sts[-1]

    emit_cast_chunk(3, 0)
    cast_last = None

    y_toks = []
    y1_toks = []

    def weight_transpose(name, first_wait, bank_waits=(None, None), wT_wait=None):
        xr = hb[name].rearrange("(a p) c -> a p c", p=128)
        evs = []
        slot_free = [None] * 2
        bank_free = list(bank_waits)
        lts = []

        def emit_load(rb):
            sl = rb % 2
            w = []
            if first_wait is not None and rb == 0:
                w.append(first_wait)
            if slot_free[sl] is not None:
                w.append(slot_free[sl])
            lts.append(dma(wtm[:, sl * D:(sl + 1) * D], xr[rb], "sync", waits=w))

        emit_load(0)
        for rb in range(8):
            sl = rb % 2
            if rb + 1 < 8:
                emit_load(rb + 1)
                settle = lts[rb + 1]
            else:
                settle = dma(scol[0:1, 0:8], hb["bo_row"][0:1, 0:8], "sync")
            trs = []
            for cb in range(8):
                bk = cb // 4
                tw = [lts[rb], tk_id32]
                if cb == 0:
                    tw.append(settle)
                if cb % 4 == 0 and bank_free[bk] is not None:
                    tw.append(bank_free[bk])
                trs.append(tr(psum[:, (6 + bk) * 512 + (cb % 4) * 128:
                                   (6 + bk) * 512 + (cb % 4) * 128 + 128],
                              wtm[:, sl * D + cb * 128: sl * D + (cb + 1) * 128],
                              idn32[:, :], waits=tw))
            slot_free[sl] = trs[-1]
            wTt = wT[:, :]
            for bk in range(2):
                dst = AP(wTt.tensor, (4 * bk) * D + rb * 128, [[8 * D, 128], [D, 4], [1, 128]])
                srcp = psum[:, (6 + bk) * 512:(6 + bk) * 512 + 512].rearrange(
                    "p (a b) -> p a b", a=4)
                ew = list(trs[bk * 4:(bk + 1) * 4])
                if rb == 0 and wT_wait is not None:
                    ew.append(wT_wait)
                ev = acopy(dst, srcp, waits=ew)
                evs.append(ev)
                bank_free[bk] = ev
        return evs[-1], evs

    prev_seq_tok = None
    for s in range(SEQS):
        xr = x_in.rearrange("(t p) c -> t p c", p=128)
        x_ev = []
        bank_free = [tk_erev, tk_erev] if s == 0 else [None, None]
        xslots = [xbt[:, :], xf32[:, 0:1024], xf32[:, 1024:2048]]
        xslot_free = [None, None, None]
        xlts = []

        def emit_xload(tt):
            sl3 = tt % 3
            w = []
            if tt == 0 and prev_seq_tok is not None:
                w.append(prev_seq_tok)
            if xslot_free[sl3] is not None:
                w.append(xslot_free[sl3])
            xlts.append(dma(xslots[sl3], xr[s * 16 + tt], "sync", waits=w))

        emit_xload(0)
        for tt in range(16):
            sl3 = tt % 3
            if tt + 1 < 16:
                emit_xload(tt + 1)
                settle = xlts[tt + 1]
            else:
                settle = dma(scol[0:1, 0:8], hb["bo_row"][0:1, 0:8], "sync")
            trs = []
            for cb in range(8):
                bk = cb // 4
                tw = [xlts[tt], tk_id32]
                if cb == 0:
                    tw.append(settle)
                if cb % 4 == 0 and bank_free[bk] is not None:
                    tw.append(bank_free[bk])
                trs.append(tr(psum[:, bk * 512 + (cb % 4) * 128:
                                   bk * 512 + (cb % 4) * 128 + 128],
                              xslots[sl3][:, cb * 128:(cb + 1) * 128],
                              idn32[:, :], waits=tw))
            xslot_free[sl3] = trs[-1]
            if s == 0:
                emit_cast_chunk(3, 3)
            xfmt = xfm[:, :]
            for bk in range(2):
                dst = AP(xfmt.tensor, (4 * bk) * L + tt * 128, [[8 * L, 128], [L, 4], [1, 128]])
                srcp = psum[:, bk * 512:bk * 512 + 512].rearrange("p (a b) -> p a b", a=4)
                ev = acopy(dst, srcp, waits=trs[bk * 4:(bk + 1) * 4])
                x_ev.append(ev)
                bank_free[bk] = ev
        xf32_guard = [xslot_free[1], xslot_free[2]]

        wq_tok, _ = weight_transpose("wq", prev_seq_tok,
            bank_waits=((tk_erev, tk_erev) if s == 0 else (prev_seq_tok, prev_seq_tok)),
            wT_wait=prev_seq_tok)
        q_ev = []
        ev_par = [None, None]
        for db in range(8):
            for t4 in range(4):
                par = (4 * db + t4) % 2
                pb = pbank(2 + 2 * par)
                ms = []
                for cb in range(8):
                    w = []
                    if cb == 0:
                        w = [wq_tok] + (x_ev if db == 0 and t4 == 0 else [])
                        if ev_par[par] is not None:
                            w.append(ev_par[par])
                    ms.append(mm(pb, wT[:, cb * D + db * 128: cb * D + (db + 1) * 128],
                                 xfm[:, cb * L + t4 * 512: cb * L + (t4 + 1) * 512],
                                 cb == 0, cb == 7, waits=w))
                ev = act(qfm[:, db * L + t4 * 512: db * L + (t4 + 1) * 512], pb, AF.Identity,
                         bias=bias8[:, 16 + db:17 + db], scale=0.125,
                         waits=[ms[-1], tk_bqs])
                q_ev.append(ev)
                ev_par[par] = ev
                if s == 0:
                    emit_cast_chunk(1, 1)
        wk_tok, _ = weight_transpose("wk", q_ev[-1],
            bank_waits=(q_ev[-1], q_ev[-1]), wT_wait=q_ev[-1])
        k_ev = []
        ev_par = [None, None]
        for db in range(8):
            for t4 in range(4):
                par = (4 * db + t4) % 2
                pb = pbank(2 + 2 * par)
                ms = []
                for cb in range(8):
                    w = []
                    if cb == 0:
                        w = [wk_tok]
                        if ev_par[par] is not None:
                            w.append(ev_par[par])
                        elif par == (4 * 7 + 3) % 2:
                            w.append(q_ev[-1])
                        else:
                            w.append(q_ev[-2])
                    ms.append(mm(pb, wT[:, cb * D + db * 128: cb * D + (db + 1) * 128],
                                 xfm[:, cb * L + t4 * 512: cb * L + (t4 + 1) * 512],
                                 cb == 0, cb == 7, waits=w))
                ev = act(kfm[:, db * L + t4 * 512: db * L + (t4 + 1) * 512], pb, AF.Identity,
                         bias=bias8[:, 8 + db:9 + db], scale=1.0,
                         waits=[ms[-1], tk_b8b])
                k_ev.append(ev)
                ev_par[par] = ev
                if s == 0:
                    emit_cast_chunk(1, 1)
        wv_tok, _ = weight_transpose("wv", k_ev[-1],
            bank_waits=(k_ev[-1], k_ev[-1]), wT_wait=k_ev[-1])
        v_ev = []
        ev_par = [None, None]
        gi = 0
        for tt in range(16):
            for dvh in range(2):
                par = gi % 2
                gi += 1
                pb = pbank(2 + 2 * par)
                ms = []
                for cb in range(8):
                    w = []
                    if cb == 0:
                        w = [wv_tok]
                        if ev_par[par] is not None:
                            w.append(ev_par[par])
                        else:
                            w.append(k_ev[-1])
                    ms.append(mm(pb, xfm[:, cb * L + tt * 128: cb * L + (tt + 1) * 128],
                                 wT[:, cb * D + dvh * 512: cb * D + (dvh + 1) * 512],
                                 cb == 0, False, waits=w))
                ms.append(mm(pb, onesb[0:1, 0:128],
                             brow[0:1, dvh * 512:(dvh + 1) * 512], False, True,
                             waits=[tk_ones, tk_brow]))
                vt = vtm[:, :]
                dst = AP(vt.tensor, tt * HL + dvh * 8 * 65,
                         [[16 * HL, 128], [65, 8], [1, 64]])
                src = pb.rearrange("p (a b) -> p a b", a=8)
                ev = acopy(dst, src, waits=[ms[-1], tk_vones])
                v_ev.append(ev)
                ev_par[par] = ev
                if s == 0:
                    emit_cast_chunk(1, 1)

        if s == 0:
            cast_last = drain_casts()

        ctx_done = []
        g_free = [None, None]
        st_free = [None, None]
        p_free = [None, None]
        pv_free = [None, None]
        craw_free = [None, None]
        glue_free = tk_srow_free if s == 0 else None
        first_st = True
        epi = None
        m = 0

        def emit_epilogue_a(e):
            nonlocal glue_free
            hh_, qh_, pvb_, par_, pvl_ = e
            cr = craw[:, 0:1024]
            w = list(pvl_)
            if craw_free[0] is not None:
                w.append(craw_free[0])
            evt = acopy(cr[0:65, :], psum[0:65, pvb_ * 512: pvb_ * 512 + 1024], waits=w)
            return e + (evt,)

        def emit_sums(e, jj):
            nonlocal glue_free
            hh_, qh_, pvb_, par_, pvl_, evt = e[:6]
            w = list(pvl_) + [evt]
            if glue_free is not None:
                w.append(glue_free)
            t3 = vop("reciprocal", srow[0:1, jj * 512:(jj + 1) * 512],
                     psum[64:65, (pvb_ + jj) * 512:(pvb_ + jj) * 512 + 512],
                     waits=w)
            return e + (t3,)

        def emit_epilogue_b(e):
            nonlocal glue_free
            hh_, qh_, pvb_, par_, pvl_, evt, t3a, t3b = e
            hc_, hr_ = hh_ // 2, (hh_ % 2) * 64
            cr = craw[:, 0:1024]
            rep = []
            for jj in range(2):
                rep.append(mm(psum[0:64, (pvb_ + jj) * 512:(pvb_ + jj) * 512 + 512],
                              ones32[0:1, 0:64], srow[0:1, jj * 512:(jj + 1) * 512],
                              True, True, waits=[t3a if jj == 0 else t3b,
                                                 tk_ones32, evt]))
            glue_free = rep[-1]
            nrm = []
            for jj in range(2):
                dst = xfm[hr_:hr_ + 64, hc_ * L + qh_ * 1024 + jj * 512:
                          hc_ * L + qh_ * 1024 + (jj + 1) * 512]
                w = [rep[jj], evt]
                if hh_ == 0 and qh_ == 0 and jj == 0:
                    w += v_ev
                nrm.append(vop("tensor_tensor", dst, cr[0:64, jj * 512:(jj + 1) * 512],
                               psum[0:64, (pvb_ + jj) * 512:(pvb_ + jj) * 512 + 512],
                               ALU.mult, waits=w))
            craw_free[0] = nrm[-1]
            pv_free[(pvb_ - 4) // 2] = nrm[-1]
            ctx_done.extend(nrm)

        for h in range(H):
            gpar = h % 2
            w = [tk_erev]
            if h < 2:
                w.append(q_ev[0])
                if s == 0:
                    w.append(cast_last)
            if g_free[gpar] is not None:
                w.append(g_free[gpar])
            ert = erev[:, :]
            gsrc = AP(ert.tensor, h * TBL, [[TBL, 1], [1, 128], [1, GW]])
            gtok = dma(gtab[:, gpar * GW:(gpar + 1) * GW], gsrc, "sync", waits=w)
            hc, hr = h // 2, (h % 2) * 64
            last_pm = None
            for qh in range(2):
                if epi is not None:
                    epi = emit_epilogue_a(epi)
                pvpair = m % 2
                pvb = 4 + 2 * pvpair
                pv_ms = []
                ets = {}
                pms = {}
                def emit_st(kt):
                    spar = kt % 2
                    ms = []
                    nonlocal first_st
                    for jj in range(2):
                        w = []
                        if jj == 0:
                            if first_st:
                                w = q_ev + k_ev + v_ev
                                first_st = False
                            if st_free[spar] is not None:
                                w.append(st_free[spar])
                        ms.append(mm(psum[:, (2 * spar) * 512 + jj * 512:
                                          (2 * spar) * 512 + (jj + 1) * 512],
                                     kfm[hr:hr + 64, hc * L + kt * 128: hc * L + (kt + 1) * 128],
                                     qfm[hr:hr + 64, hc * L + qh * 1024 + jj * 512:
                                         hc * L + qh * 1024 + (jj + 1) * 512],
                                     True, True, waits=w))
                    return ms
                def emit_exp_mult(kt, ms):
                    spar = kt % 2
                    ppar = kt % 2
                    w = ms + ([p_free[ppar]] if p_free[ppar] is not None else [])
                    et = act(ptile[:, ppar * 1024:(ppar + 1) * 1024],
                             psum[:, (2 * spar) * 512:(2 * spar) * 512 + 1024],
                             AF.Exp, bias=cstf[:, 0:1], waits=w)
                    st_free[spar] = et
                    off = 1024 - 1024 * qh + 128 * kt
                    gt_t = gtab[:, :]
                    grev = AP(gt_t.tensor, gpar * GW + off + 1023, [[2 * GW, 128], [-1, 1024]])
                    pm = vop("tensor_tensor",
                             ptile[:, ppar * 1024:(ppar + 1) * 1024],
                             ptile[:, ppar * 1024:(ppar + 1) * 1024],
                             grev, ALU.mult, waits=[et, gtok])
                    return pm
                def emit_pv(kt, pm):
                    ppar = kt % 2
                    for jj in range(2):
                        w = [pm]
                        if kt == 0 and pv_free[pvpair] is not None:
                            w.append(pv_free[pvpair])
                        pv_ms.append(mm(psum[0:65, (pvb + jj) * 512:(pvb + jj + 1) * 512],
                                        vtm[:, kt * HL + h * 65: kt * HL + h * 65 + 65],
                                        ptile[:, ppar * 1024 + jj * 512:
                                              ppar * 1024 + (jj + 1) * 512],
                                        kt == 0, kt == 15, waits=w))
                    p_free[ppar] = pv_ms[-1]
                ms0 = emit_st(0)
                pms[0] = emit_exp_mult(0, ms0)
                for kt in range(16):
                    if kt + 1 < 16:
                        msn = emit_st(kt + 1)
                        pms[kt + 1] = emit_exp_mult(kt + 1, msn)
                    emit_pv(kt, pms[kt])
                    if kt == 5 and epi is not None and len(epi) == 6:
                        epi = emit_sums(epi, 0)
                    if kt == 11 and epi is not None and len(epi) == 7:
                        epi = emit_sums(epi, 1)
                last_pm = pms[15]
                if epi is not None:
                    emit_epilogue_b(epi)
                epi = (h, qh, pvb, m % 2, [pv_ms[-1], pv_ms[-2]])
                m += 1
            g_free[gpar] = last_pm
        epi = emit_epilogue_a(epi)
        epi = emit_sums(epi, 0)
        epi = emit_sums(epi, 1)
        emit_epilogue_b(epi)

        wo_tok, _ = weight_transpose("wo", v_ev[-1],
            bank_waits=(pv_free[1], pv_free[1]), wT_wait=v_ev[-1])
        xl_free = [None, None]
        ev_par = [None, None]
        lnst_free = None
        lnst_act_free = None
        first_o = True
        for tt in range(16):
            par = tt % 2
            w = []
            if xl_free[par] is not None:
                w.append(xl_free[par])
            else:
                w.append(xf32_guard[par])
            xl = dma(xf32[:, par * 1024:(par + 1) * 1024], xr[s * 16 + tt], "sync", waits=w)
            s1_toks = []
            for oh2 in range(2):
                mpar = (2 * tt + oh2) % 2
                pb = pbank(2 + 2 * mpar)
                ms = []
                for cb in range(8):
                    w = []
                    if cb == 0:
                        w = [wo_tok]
                        if first_o:
                            w += ctx_done
                            if BARRIER_OPROJ:
                                w += [pv_free, glue_free] + rep_free + \
                                     [t for t in p_free if t is not None] + \
                                     [t for t in craw_free if t is not None]
                            first_o = False
                        if ev_par[mpar] is not None:
                            w.append(ev_par[mpar])
                        else:
                            w += [t for t in st_free if t is not None]
                            w += [t for t in pv_free if t is not None]
                    ms.append(mm(pb, xfm[:, cb * L + tt * 128: cb * L + (tt + 1) * 128],
                                 wT[:, cb * D + oh2 * 512: cb * D + (oh2 + 1) * 512],
                                 cb == 0, False, waits=w))
                ms.append(mm(pb, onesb[0:1, 0:128],
                             brow[0:1, D + oh2 * 512: D + (oh2 + 1) * 512],
                             False, True, waits=[tk_brow2]))
                sl = xf32[:, par * 1024 + oh2 * 512: par * 1024 + (oh2 + 1) * 512]
                stk = vop("tensor_tensor", sl, pb, sl, ALU.add, waits=[ms[-1], xl])
                s1_toks.append(stk)
                ev_par[mpar] = stk
            s1ap = xf32[:, par * 1024:(par + 1) * 1024]
            if DEBUG_S1:
                sdr = s1dbg.rearrange("(t p) c -> t p c", p=128)
                s1_toks.append(dma(sdr[s * 16 + tt], s1ap, "sync", waits=list(s1_toks)))
                y_toks.append(s1_toks[-1])
            w = s1_toks + ([lnst_free] if lnst_free is not None else [])
            b1 = vop("bn_stats", lnst[:, 0:6], s1ap[:, 0:512], waits=w, mark=True)
            b2 = vop("bn_stats", lnst[:, 6:12], s1ap[:, 512:1024], waits=w, mark=True)
            b3 = vop("bn_aggr", lnst[:, 12:14], lnst[:, 0:12],
                     waits=[b1, b2] + ([lnst_act_free] if lnst_act_free is not None else []),
                     keep_same_engine=True)
            l1 = act(lnst[:, 14:15], lnst[:, 13:14], AF.Ln, bias=cstf[:, 1:2], waits=[b3, tk_cst])
            l2 = act(lnst[:, 15:16], lnst[:, 14:15], AF.Exp, scale=-0.5, waits=[l1])
            lnst_act_free = l1
            if DEBUG_S1:
                lnt = dma(lndbg[s * 16 + tt], lnst[:, :], "sync", waits=[l2, b3])
                y_toks.append(lnt)
            yn = vop("tensor_scalar", s1ap, s1ap, lnst[:, 12:13], lnst[:, 15:16],
                     ALU.subtract, ALU.mult,
                     waits=[l2] + ([s1_toks[-1], lnt] if DEBUG_S1 else []))
            lnst_free = yn
            ysr = y1scr.rearrange("(t p) c -> t p c", p=128)
            yst = dma(ysr[s * 16 + tt], s1ap, "sync", waits=[yn])
            y1_toks.append(yst)
            if DEBUG_Y1:
                ydr = y1dbg.rearrange("(t p) c -> t p c", p=128)
                y_toks.append(dma(ydr[s * 16 + tt], s1ap, "sync", waits=[yn]))
            xl_free[par] = yst
        prev_seq_tok = y1_toks[-1]

    stackA.close()
    if PHASE_B:
        env = dict(nc=nc, P=P, stack=stack, sbuf=sbuf, sb=sb, psum=psum, psb=psb,
                   hb=hb, wscr=wscr, y1scr=y1scr, y_out=y_out, y1_toks=y1_toks,
                   dma=dma, mm=mm, tr=tr, act=act, acopy=acopy, vop=vop,
                   onesb=onesb, idnb_=idnb_, idn32=idn32, cstf=cstf, lnst=lnst,
                   prev_seq_tok=prev_seq_tok, cast_last=cast_last, APc=AP)
        y_toks += _emit_ffn(env)

    with nc.Block() as block:
        P.emit(block, stack, y_toks if y_toks else [y1_toks[-1]])
    stack.close()
    return nc


def _emit_ffn(env):
    nc, P, sbuf = env["nc"], env["P"], env["sbuf"]
    psum, psb = env["psum"], env["psb"]
    hb, wscr, y1scr, y_out = env["hb"], env["wscr"], env["y1scr"], env["y_out"]
    y1_toks = env["y1_toks"]
    dma, mm, tr, act, acopy, vop = (env["dma"], env["mm"], env["tr"], env["act"],
                                    env["acopy"], env["vop"])
    onesb, idnb_, idn32, cstf, lnst = (env["onesb"], env["idnb_"], env["idn32"],
                                       env["cstf"], env["lnst"])
    gate = env["prev_seq_tok"]
    AP_ = env["APc"]

    y1t  = sbuf("y1t",  [128, 4096], F32)
    itm  = sbuf("itm",  [128, 4096], F32)
    vtm2 = sbuf("vtm2", [128, 4096], F32)
    ctxf = sbuf("ctxf", [128, 16 * 512], BF16)
    hf   = sbuf("hf",   [128, 4 * 512], BF16)
    h1f  = sbuf("h1f",  [128, 4 * 512], BF16)
    h2f  = sbuf("h2f",  [128, 4 * 512], BF16)
    m1r  = sbuf("m1r",  [128, 4 * 512], BF16)
    m2r  = sbuf("m2r",  [128, 4 * 512], BF16)
    hsh  = sbuf("hsh",  [65, 512], BF16)
    rhid = sbuf("rhid", [128, 2 * 512], BF16)
    hhid = sbuf("hhid", [128, 2 * 512], BF16)
    m1f  = sbuf("m1f",  [8, 512], BF16)
    m2f  = sbuf("m2f",  [8, 512], BF16)
    m1t  = sbuf("m1t",  [128, 64], BF16)
    mq   = sbuf("mq",   [128, 32], F32)
    cols = sbuf("cols", [128, 32], F32)
    ew   = sbuf("ew",   [128, 16 * 512], F32)
    wa   = sbuf("wa",   [128, 32768], BF16)
    rha  = sbuf("rha",  [128, 4116], BF16)
    murep = sbuf("murep", [128, D], F32)
    iwrep = sbuf("iwrep", [128, D], F32)
    bct  = sbuf("bct",  [128, 32], F32)
    sbt  = sbuf("sbt",  [64, 1], F32)
    rows = sbuf("rows", [1, D + 16], BF16)
    selb = sbuf("selb", [8, HC], BF16)

    W1O, W2O, S1O, S2O, B2O = 0, 8192, 20480, 21504, 24576
    R1O, R2O = 0, 16384
    RWO, HWO, RW2O, HW2O = 0, 2048, 4096, 4112

    toks_y = []
    dbgB = {}
    if DEBUG_B:
        for nm, sh in (("d_m1t", (128, 64)), ("d_cols", (128, 32)), ("d_mq", (128, 32)),
                       ("d_rhid", (128, 1024)), ("d_hf", (128, 2048)), ("d_hsh", (65, 512)),
                       ("d_h1f", (128, 2048)), ("d_m1r", (128, 2048)),
                       ("d_i0", (128, 4096)), ("d_v0", (128, 4096)),
                       ("d_i1", (128, 4096)), ("d_v1", (128, 4096)),
                       ("d_ln2in", (128, 4096)), ("d_ctxf", (128, 8192)),
                       ("d_sa", (128, 512)), ("d_sb", (128, 512)), ("d_sg", (128, 512)), ("d_vn", (128, 512)), ("d_g", (128, 512)), ("d_cols2", (128, 32)), ("d_sb1", (128, 512)), ("d_sb2", (128, 512)), ("d_refiw", (128, 512)), ("d_vn0", (128, 512)), ("d_vn1", (128, 512)), ("d_vn2", (128, 512))):
            dbgB[nm] = nc.dram_tensor(nm, list(sh), F32 if nm not in ("d_m1t", "d_hf", "d_h1f", "d_m1r", "d_ctxf", "d_rhid", "d_hsh") else BF16, kind="ExternalOutput").ap()
        dbgB["d_hidr"] = nc.dram_tensor("d_hidr", [128, 8192], BF16, kind="ExternalOutput").ap()

    ld = []
    ld.append(dma(murep[:, :], hb["mu_rep"][:, :], "sync", waits=[gate]))
    ld.append(dma(iwrep[:, :], hb["iw_rep"][:, :], "sync", waits=[gate]))
    ld.append(dma(bct[:, 0:2], hb["rb1T"][:, :], "sync", waits=[gate]))
    ld.append(dma(bct[:, 2:4], hb["hb1T"][:, :], "sync", waits=[gate]))
    ld.append(dma(bct[:, 4:8], hb["b1catT"][:, :], "sync", waits=[gate]))
    ld.append(dma(bct[:, 8:24], hb["refb1T"][:, :], "sync", waits=[gate]))
    ld.append(dma(bct[:, 24:25], hb["hb2rep"][:, :], "sync", waits=[gate]))
    ld.append(dma(bct[:, 25:26], hb["swrep"][:, :], "sync", waits=[gate]))
    ld.append(dma(sbt[:, :], hb["sb1T"][:, :], "sync", waits=[gate]))
    tk_b = ld[-1]
    t1 = vop("tensor_scalar_mul", bct[:, 26:27], bct[:, 24:25], 0.5, waits=[ld[6]])
    t2 = act(bct[:, 27:28], bct[:, 25:26], AF.Tanh, scale=0.5, waits=[ld[7]])
    t3 = vop("tensor_scalar", bct[:, 28:29], bct[:, 27:28], 0.5, 0.5,
             ALU.mult, ALU.add, waits=[t2])
    t4 = vop("tensor_scalar", bct[:, 29:30], bct[:, 28:29], -1.0, 1.0,
             ALU.mult, ALU.add, waits=[t3], keep_same_engine=True, mark=True)
    t5 = vop("tensor_scalar_mul", bct[:, 30:31], bct[:, 29:30], 0.5,
             waits=[t4], keep_same_engine=True, mark=True)
    tk_cols = t5
    tmpr = ew[0:1, 0:D]
    r1 = dma(tmpr, hb["refb2_row"][:, :], "sync", waits=[gate])
    r2 = acopy(rows[0:1, 0:D], tmpr, waits=[r1])
    r3 = dma(ew[0:1, 0:E], hb["rb2_row"][:, :], "sync", waits=[r2])
    r4 = acopy(rows[0:1, D:D + E], ew[0:1, 0:E], waits=[r3])
    tk_rows = r4
    tk_hones = vop("memset", hsh[64:65, :], 1.0, waits=[gate])
    tk_sel = dma(selb[:, :], hb["sel"][:, :], "sync", waits=[gate])
    def arena_load(dst_t, dst_off, scr_ap, rows_, cols_, planes, waits):
        srcv = scr_ap.rearrange("(a p) c -> p a c", p=128) if rows_ > 128 else scr_ap
        if rows_ > 128:
            dstap = AP_(dst_t[:, :].tensor, dst_off,
                        [[dst_t.shape[1], 128], [cols_, planes], [1, cols_]])
        else:
            dstap = AP_(dst_t[:, :].tensor, dst_off, [[dst_t.shape[1], rows_], [1, cols_]])
        return dma(dstap, srcv, "sync", waits=waits, slow=(cols_ < 8))
    lw = []
    gate_c = [gate] + ([env["cast_last"]] if env.get("cast_last") is not None else [])
    lw.append(arena_load(rha, RWO, wscr["rw1"], D, 256, 8, gate_c))
    lw.append(arena_load(rha, HWO, wscr["hw1"], D, 256, 8, gate_c))
    lw.append(arena_load(rha, RW2O, wscr["rw2"], 256, 8, 2, gate_c))
    lw.append(arena_load(rha, HW2O, wscr["hw2"], 256, 1, 2, gate_c))
    tk_rh = lw[-1]

    bank_rd = [gate] * 8

    def mmg(outbank, outslice, parts, waits_first=()):
        t = None
        n = len(parts)
        for i, (lh, rh) in enumerate(parts):
            w = list(waits_first) + [bank_rd[outbank]] if i == 0 else []
            t = mm(outslice, lh, rh, i == 0, i == n - 1, waits=w)
        return t

    prev_tile_done = gate
    warena_free = gate
    ew_war = [None]
    for ft in range(NFT):
        tbase = ft * FT
        ysr = y1scr
        ysv = AP_(ysr.tensor, tbase * D, [[D, 128], [128 * D, 4], [1, D]])
        wv = [y1_toks[tbase // 128 + k] for k in range(4)] + [prev_tile_done]
        ly1 = dma(y1t[:, :], ysv, "sync", waits=wv)
        icp = vop("tensor_copy", itm[:, :], y1t[:, :], waits=[ly1])

        def to_fm(srctile, planes0, ctoks):
            evs = []
            for tc in range(4):
                trs = []
                for cb in range(8):
                    bk = 6 + cb // 4
                    w = list(ctoks) if (tc == 0 and cb == 0) else []
                    if cb % 4 == 0:
                        w.append(bank_rd[bk])
                    trs.append(tr(psum[:, bk * 512 + (cb % 4) * 128:
                                       bk * 512 + (cb % 4) * 128 + 128],
                                  srctile[:, tc * 1024 + cb * 128: tc * 1024 + (cb + 1) * 128],
                                  idn32[:, :], waits=w))
                ctxt = ctxf[:, :]
                for bk in range(2):
                    dst = AP_(ctxt.tensor, (planes0 + 4 * bk) * 512 + tc * 128,
                              [[16 * 512, 128], [512, 4], [1, 128]])
                    srcp = psum[:, (6 + bk) * 512:(6 + bk) * 512 + 512].rearrange(
                        "p (a b) -> p a b", a=4)
                    ev = acopy(dst, srcp, waits=trs[bk * 4:(bk + 1) * 4])
                    evs.append(ev)
                    bank_rd[6 + bk] = ev
            return evs
        y1fm_ev = to_fm(y1t, 0, [ly1])

        rh_ev = []
        for hc in range(2):
            bk = hc % 2
            parts = [(rha[:, RWO + cb * 256 + hc * 128: RWO + cb * 256 + (hc + 1) * 128],
                      ctxf[:, cb * 512:(cb + 1) * 512]) for cb in range(8)]
            t = mmg(bk, psum[:, bk * 512:(bk + 1) * 512], parts,
                    waits_first=[tk_rh] + y1fm_ev)
            ev = act(rhid[:, hc * 512:(hc + 1) * 512], psum[:, bk * 512:(bk + 1) * 512],
                     AF.Gelu, bias=bct[:, 0 + hc:1 + hc], scale=1.0, waits=[t, tk_b])
            rh_ev.append(ev)
            bank_rd[bk] = ev
        lg_toks = []
        for tc in range(4):
            sl = psum[:, 2 * 512 + tc * 8: 2 * 512 + (tc + 1) * 8]
            parts = [(rhid[:, hc * 512 + tc * 128: hc * 512 + (tc + 1) * 128],
                      rha[:, RW2O + hc * 8: RW2O + (hc + 1) * 8]) for hc in range(2)]
            parts.append((onesb[0:1, 0:128], rows[0:1, D:D + E]))
            lg_toks.append(mmg(2, sl, parts, waits_first=rh_ev + [tk_rows]))
        mt_toks = []
        alpha_toks = []
        for tc in range(4):
            sl = psum[:, 2 * 512 + tc * 8: 2 * 512 + (tc + 1) * 8]
            c0 = tc * 8
            q1 = vop("reduce_max", cols[:, c0:c0 + 1], sl, axis=AX.X, waits=lg_toks)
            q2 = vop("tensor_single_scalar", mq[:, 0:8], sl, cols[:, c0:c0 + 1],
                     ALU.is_equal, waits=[q1], keep_same_engine=True, mark=True)
            q3 = vop("scalar_tensor_tensor", mq[:, 8:16], mq[:, 0:8], -1e30, sl,
                     ALU.mult, ALU.add, waits=[q2], keep_same_engine=True, mark=True)
            q4 = vop("reduce_max", cols[:, c0 + 1:c0 + 2], mq[:, 8:16], axis=AX.X,
                     waits=[q3], keep_same_engine=True, mark=True)
            q5 = vop("tensor_single_scalar", mq[:, 16:24], mq[:, 8:16],
                     cols[:, c0 + 1:c0 + 2], ALU.is_equal, waits=[q4],
                     keep_same_engine=True, mark=True)
            q6 = vop("tensor_copy", m1t[:, tc * 8: tc * 8 + 8], mq[:, 0:8], waits=[q2])
            q7 = vop("tensor_copy", m1t[:, 32 + tc * 8: 32 + tc * 8 + 8], mq[:, 16:24],
                     waits=[q5], keep_same_engine=True, mark=True)
            mt_toks.append((q6, q7))
            q8 = vop("tensor_sub", cols[:, c0 + 2:c0 + 3], cols[:, c0:c0 + 1],
                     cols[:, c0 + 1:c0 + 2], waits=[q4], keep_same_engine=True, mark=True)
            q9 = act(cols[:, c0 + 3:c0 + 4], cols[:, c0 + 2:c0 + 3], AF.Tanh, scale=0.5,
                     waits=[q8])
            qa = vop("tensor_scalar", cols[:, c0 + 4:c0 + 5], cols[:, c0 + 3:c0 + 4],
                     bct[:, 30:31], bct[:, 30:31], ALU.mult, ALU.add,
                     waits=[q9, tk_cols], mark=True)
            qb = vop("tensor_sub", cols[:, c0 + 5:c0 + 6], bct[:, 29:30],
                     cols[:, c0 + 4:c0 + 5], waits=[qa], keep_same_engine=True,
                     mark=True)
            alpha_toks.append(qb)
        bank_rd[2] = alpha_toks[-1]
        mtr = [[], []]
        for mi, (mtile_off, dstf) in enumerate(((0, m1f), (32, m2f))):
            bk = 6 + mi
            for tc in range(4):
                w = [mt_toks[tc][mi], env["idnb_"] and None] if False else [mt_toks[tc][mi]]
                if tc == 0:
                    w.append(bank_rd[bk])
                mtr[mi].append(tr(psb[0:8, bk * 1024 + tc * 128: bk * 1024 + (tc + 1) * 128],
                                  m1t[:, mtile_off + tc * 8: mtile_off + (tc + 1) * 8],
                                  idnb_[:, :], waits=w))
            ev = acopy(dstf[0:8, :], psb[0:8, bk * 1024: bk * 1024 + 512],
                       waits=mtr[mi])
            bank_rd[bk] = ev
            mtr[mi] = ev
        mrep_ev = []
        for mi, (mf, mr) in enumerate(((m1f, m1r), (m2f, m2r))):
            for hc in range(4):
                bk = hc % 2
                t = mmg(bk, psum[:, bk * 512:(bk + 1) * 512],
                        [(selb[0:8, hc * 128:(hc + 1) * 128], mf[0:8, :])],
                        waits_first=[mtr[mi], tk_sel])
                ev = acopy(mr[:, hc * 512:(hc + 1) * 512],
                           psum[:, bk * 512:(bk + 1) * 512], waits=[t])
                mrep_ev.append(ev)
                bank_rd[bk] = ev

        if DEBUG_B and ft == 0:
            toks_y.append(dma(dbgB["d_m1t"][:, :], m1t[:, :], "sync",
                              waits=[t for pair in mt_toks for t in pair]))
            toks_y.append(dma(dbgB["d_cols"][:, :], cols[:, :], "sync", waits=alpha_toks))
            toks_y.append(dma(dbgB["d_rhid"][:, :], rhid[:, :], "sync", waits=rh_ev))
            toks_y.append(dma(dbgB["d_m1r"][:, :], m1r[:, :], "sync", waits=mrep_ev))

        wvt = [warena_free]
        if ft == 0 and env.get("cast_last") is not None:
            wvt = wvt + [env["cast_last"]]
        l1 = arena_load(wa, W1O, wscr["w1cat"], C2, HC, 16, wvt)
        l2 = arena_load(wa, W2O, wscr["w2cat"], HC, O3, 4, wvt)
        l3 = arena_load(wa, S1O, wscr["sw1"], C2, EH, 16, wvt)
        l4 = arena_load(wa, S2O, wscr["sw2p"], EH + 1, O3, 1, wvt)
        l5 = arena_load(wa, B2O, wscr["b2cat"], E, O3, 1, wvt)
        lwt = [l1, l2, l3, l4, l5]

        ctrl_last = None
        for it in range(2):
            ncc = 8 if it == 0 else 16
            if it == 1:
                w_iv = [iup_last, vup_last]
                if DEBUG_B and ft == 0:
                    t_i0 = dma(dbgB["d_i0"][:, :], itm[:, :], "sync", waits=[iup_last])
                    t_v0 = dma(dbgB["d_v0"][:, :], vtm2[:, :], "sync", waits=[vup_last])
                    toks_y += [t_i0, t_v0]
                    w_iv += [t_i0, t_v0]
                ifm_ev = to_fm(itm, 0, w_iv)
                vfm_ev = to_fm(vtm2, 8, w_iv)
                ctx_ev = ifm_ev + vfm_ev
            else:
                ctx_ev = y1fm_ev
            h_ev = []
            for hc in range(4):
                bk = hc % 2
                parts = [(wa[:, W1O + cb * 512 + hc * 128: W1O + cb * 512 + (hc + 1) * 128],
                          ctxf[:, cb * 512:(cb + 1) * 512]) for cb in range(ncc)]
                t = mmg(bk, psum[:, bk * 512:(bk + 1) * 512], parts,
                        waits_first=[l1] + ctx_ev)
                ev = act(hf[:, hc * 512:(hc + 1) * 512], psum[:, bk * 512:(bk + 1) * 512],
                         AF.Gelu, bias=bct[:, 4 + hc:5 + hc], waits=[t, tk_b])
                h_ev.append(ev)
                bank_rd[bk] = ev
            parts = [(wa[0:128, S1O + cb * 64: S1O + (cb + 1) * 64],
                      ctxf[:, cb * 512:(cb + 1) * 512]) for cb in range(ncc)]
            t = mmg(2, psum[0:64, 2 * 512: 2 * 512 + 512], parts,
                    waits_first=[l3] + ctx_ev)
            hsh_ev = act(hsh[0:64, :], psum[0:64, 2 * 512: 2 * 512 + 512], AF.Gelu,
                         bias=sbt[:, 0:1], waits=[t, tk_b, tk_hones])
            bank_rd[2] = hsh_ev
            hm1 = vop("tensor_mul", h1f[:, :], hf[:, :], m1r[:, :],
                      waits=h_ev + mrep_ev)
            hm2 = vop("tensor_mul", h2f[:, :], hf[:, :], m2r[:, :],
                      waits=h_ev + mrep_ev, keep_same_engine=True, mark=True)
            hh_ev = []
            for hc in range(2):
                parts = [(rha[:, HWO + cb * 256 + hc * 128: HWO + cb * 256 + (hc + 1) * 128],
                          ctxf[:, cb * 512:(cb + 1) * 512]) for cb in range(8)]
                t = mmg(3, psum[:, 3 * 512: 3 * 512 + 512], parts,
                        waits_first=[tk_rh] + ctx_ev)
                ev = act(hhid[:, hc * 512:(hc + 1) * 512], psum[:, 3 * 512: 3 * 512 + 512],
                         AF.Gelu, bias=bct[:, 2 + hc:3 + hc], waits=[t, tk_b])
                hh_ev.append(ev)
                bank_rd[3] = ev
            if DEBUG_B and ft == 0 and it == 0:
                toks_y.append(dma(dbgB["d_hf"][:, :], hf[:, :], "sync", waits=h_ev))
                toks_y.append(dma(dbgB["d_hsh"][:, :], hsh[:, :], "sync", waits=[hsh_ev]))
                toks_y.append(dma(dbgB["d_h1f"][:, :], h1f[:, :], "sync", waits=[hm1]))
            hc_mm = []
            for tc in range(4):
                sl = psum[:, 2 * 512 + 16 + tc: 2 * 512 + 17 + tc]
                parts = [(hhid[:, hc * 512 + tc * 128: hc * 512 + (tc + 1) * 128],
                          rha[:, HW2O + hc: HW2O + hc + 1]) for hc in range(2)]
                hc_mm.append(mmg(2, sl, parts, waits_first=hh_ev))
            hcol_toks = []
            for tc in range(4):
                sl = psum[:, 2 * 512 + 16 + tc: 2 * 512 + 17 + tc]
                th = act(cols[:, tc * 8 + 6: tc * 8 + 7], sl, AF.Tanh, scale=0.5,
                         bias=bct[:, 26:27], waits=hc_mm + [t1])
                u1 = vop("tensor_scalar", cols[:, tc * 8 + 7: tc * 8 + 8],
                         cols[:, tc * 8 + 6: tc * 8 + 7], 0.5, 0.5, ALU.mult, ALU.add,
                         waits=[th], mark=True)
                u2 = vop("tensor_scalar", mq[:, 24 + tc: 25 + tc],
                         cols[:, tc * 8 + 6: tc * 8 + 7], 0.05, 0.05, ALU.mult, ALU.add,
                         waits=[th], mark=True)
                hcol_toks.append((u1, u2))
            bank_rd[2] = hcol_toks[-1][0]
            if DEBUG_B and ft == 0 and it == 0:
                toks_y.append(dma(dbgB["d_mq"][:, :], mq[:, :], "sync",
                                  waits=[t for p_ in hcol_toks for t in p_]))
                toks_y.append(dma(dbgB["d_ctxf"][0:128, 0:8192], ctxf[:, 0:8192],
                                  "sync", waits=ctx_ev))
                toks_y.append(dma(dbgB["d_cols2"][:, :], cols[:, :], "sync",
                                  waits=[t for p_ in hcol_toks for t in p_]))
            for tc in range(4):
                def ewb(i):
                    return ew[:, i * 512:(i + 1) * 512]
                for ch in range(2):
                    c0 = tc * 1024 + ch * 512
                    dtok = vop("tensor_sub", ewb(0), itm[:, c0:c0 + 512],
                               murep[:, ch * 512:(ch + 1) * 512],
                               waits=[icp if it == 0 else iup_last, ld[0]])
                    ev_parts = {}
                    s_wave = [("sh", "a"), ("sh", "g"), ("r1", "a"), ("r1", "g"),
                              ("r2", "a"), ("r2", "g")]
                    e_wave = [("sh", "b"), ("r1", "b"), ("r2", "b")]
                    order = (s_wave + e_wave) if (tc * 2 + ch) % 2 == 0 else                             (e_wave + s_wave)
                    for gi, (br, pname) in enumerate(order):
                        bi = ("sh", "r1", "r2").index(br)
                        pi = ("a", "b", "g").index(pname)
                        bk = 4 + gi % 2
                        war = ([ew_war[0][(br, pname)]]
                               if (ew_war[0] is not None and (br, pname) in ew_war[0])
                               else [])
                        off = pi * 1024 + ch * 512
                        if br == "sh":
                            parts = [(hsh[0:65, tc * 128:(tc + 1) * 128],
                                      wa[0:65, S2O + off: S2O + off + 512])]
                            wf = [l4, hsh_ev]
                        else:
                            mfm = m1f if br == "r1" else m2f
                            hxf = h1f if br == "r1" else h2f
                            hmt = hm1 if br == "r1" else hm2
                            parts = [(hxf[:, hc2 * 512 + tc * 128: hc2 * 512 + (tc + 1) * 128],
                                      wa[:, W2O + hc2 * 3072 + off: W2O + hc2 * 3072 + off + 512])
                                     for hc2 in range(4)]
                            parts.append((mfm[0:8, tc * 128:(tc + 1) * 128],
                                          wa[0:8, B2O + off: B2O + off + 512]))
                            wf = [l2, l5, hmt, mtr[0 if br == "r1" else 1]]
                        t = mmg(bk, psum[:, bk * 512:(bk + 1) * 512], parts,
                                waits_first=wf)
                        bufi = {"a": 1, "b": 7, "g": 4}[pname] + bi
                        if pname == "b":
                            e1 = act(ewb(bufi), psum[:, bk * 512:(bk + 1) * 512],
                                     AF.Exp, waits=[t] + war)
                            e2 = act(ewb(bufi), ewb(bufi), AF.Ln, bias=1.0,
                                     waits=[e1])
                            ev_parts[(br, pname)] = e2
                            bank_rd[bk] = e1
                        else:
                            e1 = act(ewb(bufi), psum[:, bk * 512:(bk + 1) * 512],
                                     AF.Sigmoid, waits=[t] + war)
                            ev_parts[(br, pname)] = e1
                            bank_rd[bk] = e1
                    if DEBUG_B and ft == 0 and it == 0 and tc == 0 and ch == 0:
                        evp = list(ev_parts.values())
                        for dn, bi_ in (("d_sa", 1), ("d_sb", 7), ("d_sg", 4),
                                        ("d_sb1", 8), ("d_sb2", 9)):
                            toks_y.append(dma(dbgB[dn][:, :], ewb(bi_), "sync", waits=evp))
                        dbg_ev = toks_y[-5:]
                        acopy(ew[0:1, 0:1], ew[0:1, 0:1], waits=dbg_ev)
                    else:
                        dbg_ev = []
                    wv_ = [dtok] + dbg_ev + list(ev_parts.values()) + [alpha_toks[tc],
                                                              hcol_toks[tc][0], hcol_toks[tc][1]]
                    ch_toks = []
                    readers = {}
                    first = True
                    for bi, br in enumerate(("sh", "r1", "r2")):
                        acol = (bct[:, 28:29] if br == "sh" else
                                cols[:, tc * 8 + 4 + (0 if br == "r1" else 1):
                                     tc * 8 + 5 + (0 if br == "r1" else 1)])
                        SA, SB, SG = ewb(1 + bi), ewb(7 + bi), ewb(4 + bi)
                        if it == 0:
                            na = mq[:, 28 + bi:29 + bi]
                            natok = vop("tensor_scalar_mul", na, acol, -1.0,
                                        waits=wv_ if first else [],
                                        keep_same_engine=True, mark=True)
                            ch_toks.append(natok)
                            vnp = ewb(12 + bi)
                            ch_toks.append(vop("scalar_tensor_tensor", vnp, ewb(0),
                                               na, SB, ALU.mult, ALU.mult,
                                               waits=[natok], keep_same_engine=True,
                                               mark=True))
                            readers[(br, "b")] = ch_toks[-1]
                        else:
                            A, B = ewb(10), ewb(11)
                            ch_toks.append(vop("scalar_tensor_tensor", A,
                                               vtm2[:, c0:c0 + 512], acol, SA,
                                               ALU.mult, ALU.mult,
                                               waits=(wv_ + [vup_last]) if first else [],
                                               keep_same_engine=True, mark=True))
                            readers[(br, "a")] = ch_toks[-1]
                            ch_toks.append(vop("scalar_tensor_tensor", B, ewb(0),
                                               acol, SB, ALU.mult, ALU.mult, waits=[],
                                               keep_same_engine=True, mark=True))
                            readers[(br, "b")] = ch_toks[-1]
                            vnp = ewb(12 + bi)
                            ch_toks.append(vop("tensor_sub", vnp, A, B, waits=[],
                                               keep_same_engine=True, mark=True))
                        first = False
                        if bi == 0:
                            ch_toks.append(vop("tensor_mul", ewb(15), SG, vnp, waits=[],
                                               keep_same_engine=True, mark=True))
                            readers[(br, "g")] = ch_toks[-1]
                        else:
                            ch_toks.append(vop("tensor_mul", ewb(10), SG, vnp, waits=[],
                                               keep_same_engine=True, mark=True))
                            readers[(br, "g")] = ch_toks[-1]
                            ch_toks.append(vop("tensor_add", ewb(15), ewb(15), ewb(10),
                                               waits=[], keep_same_engine=True, mark=True))
                    ch_toks.append(vop("tensor_add", ewb(11), ewb(12), ewb(13), waits=[],
                                       keep_same_engine=True, mark=True))
                    ch_toks.append(vop("tensor_add", ewb(11), ewb(11), ewb(14), waits=[],
                                       keep_same_engine=True, mark=True))
                    if DEBUG_B and ft == 0 and it == 0 and tc == 0 and ch == 0:
                        tvn = dma(dbgB["d_vn"][:, :], ewb(11), "sync", waits=[ch_toks[-1]])
                        tgg = dma(dbgB["d_g"][:, :], ewb(15), "sync", waits=[ch_toks[-1]])
                        for dn, bi_ in (("d_vn0", 12), ("d_vn1", 13), ("d_vn2", 14)):
                            toks_y.append(dma(dbgB[dn][:, :], ewb(bi_), "sync",
                                              waits=[ch_toks[-1]]))
                        tvn2 = toks_y[-3:]
                        toks_y += [tvn, tgg]
                        tvn = tvn; tgg = tgg
                        ch_toks.append(vop("tensor_copy", ew[0:1, 0:1], ew[0:1, 0:1],
                                           waits=[tvn, tgg] + tvn2))
                    iup = vop("scalar_tensor_tensor", itm[:, c0:c0 + 512], ewb(15),
                              mq[:, 24 + tc: 25 + tc], itm[:, c0:c0 + 512],
                              ALU.mult, ALU.add, waits=[hcol_toks[tc][1]],
                              keep_same_engine=True, mark=True)
                    if it == 0:
                        vup = vop("tensor_scalar_mul", vtm2[:, c0:c0 + 512], ewb(11),
                                  cols[:, tc * 8 + 7: tc * 8 + 8],
                                  waits=[hcol_toks[tc][0]],
                                  keep_same_engine=True, mark=True)
                    else:
                        ch_toks.append(vop("tensor_sub", ewb(10), ewb(11),
                                           vtm2[:, c0:c0 + 512], waits=[],
                                           keep_same_engine=True, mark=True))
                        vup = vop("scalar_tensor_tensor", vtm2[:, c0:c0 + 512], ewb(10),
                                  cols[:, tc * 8 + 7: tc * 8 + 8], vtm2[:, c0:c0 + 512],
                                  ALU.mult, ALU.add, waits=[hcol_toks[tc][0]],
                                  keep_same_engine=True, mark=True)
                    iup_last, vup_last = iup, vup
                    ew_war[0] = readers
            ctrl_last = iup_last

        if DEBUG_B and ft == 0:
            t_i1 = dma(dbgB["d_i1"][:, :], itm[:, :], "sync", waits=[iup_last])
            t_v1 = dma(dbgB["d_v1"][:, :], vtm2[:, :], "sync", waits=[vup_last])
            toks_y += [t_i1, t_v1]
            dbg_iv = [t_i1, t_v1]
        else:
            dbg_iv = []

        lr1 = arena_load(wa, R1O, wscr["refw1"], D, C2, 8, [ctrl_last] + dbg_iv)
        lr2 = arena_load(wa, R2O, wscr["refw2"], C2, D, 16, [ctrl_last])
        rfm_ev = to_fm(itm, 0, [iup_last] + dbg_iv)
        def hidr_plane(hc):
            tens = (hf, h1f, h2f, m1r)[hc // 4]
            return tens[:, (hc % 4) * 512:((hc % 4) + 1) * 512]
        hr_ev = []
        for hc in range(16):
            bk = hc % 2
            parts = [(wa[:, R1O + cb * 2048 + hc * 128: R1O + cb * 2048 + (hc + 1) * 128],
                      ctxf[:, cb * 512:(cb + 1) * 512]) for cb in range(8)]
            t = mmg(bk, psum[:, bk * 512:(bk + 1) * 512], parts,
                    waits_first=[lr1] + rfm_ev + ([hm1, hm2] if hc == 4 else []) +
                                (mrep_ev if hc == 12 else []))
            ev = act(hidr_plane(hc), psum[:, bk * 512:(bk + 1) * 512], AF.Gelu,
                     bias=bct[:, 8 + hc:9 + hc],
                     waits=[t, tk_b] + ([hm1, hm2] if hc in (4, 8, 12) else []))
            hr_ev.append(ev)
            bank_rd[bk] = ev
        if DEBUG_B and ft == 0:
            hr_t = dma(dbgB["d_hidr"][:, 0:2048], hf[:, :], "sync", waits=hr_ev)
            hr_t2 = dma(dbgB["d_hidr"][:, 2048:4096], h1f[:, :], "sync", waits=hr_ev)
            hr_t3 = dma(dbgB["d_hidr"][:, 4096:6144], h2f[:, :], "sync", waits=hr_ev)
            hr_t4 = dma(dbgB["d_hidr"][:, 6144:8192], m1r[:, :], "sync", waits=hr_ev)
            toks_y += [hr_t, hr_t2, hr_t3, hr_t4]
        ln_in_toks = []
        for tc in range(4):
            for chf in range(2):
                bk = 2 + (2 * tc + chf) % 2
                c0 = tc * 1024 + chf * 512
                parts = [(hidr_plane(hc)[:, tc * 128:(tc + 1) * 128],
                          wa[:, R2O + hc * 1024 + chf * 512: R2O + hc * 1024 + chf * 512 + 512])
                         for hc in range(16)]
                parts.append((onesb[0:1, 0:128], rows[0:1, chf * 512: (chf + 1) * 512]))
                t = mmg(bk, psum[:, bk * 512:(bk + 1) * 512], parts,
                        waits_first=hr_ev + [lr2, tk_rows])
                u1 = vop("tensor_mul", ew[:, 0:512], psum[:, bk * 512:(bk + 1) * 512],
                         iwrep[:, (c0 % 1024):(c0 % 1024) + 512], waits=[t, ld[1]])
                if DEBUG_B and ft == 0 and tc == 0 and chf == 0:
                    t_ri = dma(dbgB["d_refiw"][:, :], ew[:, 0:512], "sync", waits=[u1])
                    toks_y.append(t_ri)
                    u1 = t_ri
                u2 = vop("scalar_tensor_tensor", y1t[:, c0:c0 + 512], y1t[:, c0:c0 + 512],
                         2.0, ew[:, 0:512], ALU.mult, ALU.add, waits=[u1],
                         keep_same_engine=True, mark=True)
                ln_in_toks.append(u2)
                bank_rd[bk] = u2
        if DEBUG_B and ft == 0:
            t_li = dma(dbgB["d_ln2in"][:, :], y1t[:, :], "sync", waits=ln_in_toks)
            toks_y.append(t_li)
            ln_in_toks = ln_in_toks + [t_li]
        lnf = None
        lnaf = None
        for tc in range(4):
            s1ap = y1t[:, tc * 1024:(tc + 1) * 1024]
            w = [ln_in_toks[2 * tc], ln_in_toks[2 * tc + 1]] + ([lnf] if lnf else [])
            b1 = vop("bn_stats", lnst[:, 0:6], s1ap[:, 0:512], waits=w, mark=True)
            b2 = vop("bn_stats", lnst[:, 6:12], s1ap[:, 512:1024], waits=w, mark=True)
            b3 = vop("bn_aggr", lnst[:, 12:14], lnst[:, 0:12],
                     waits=[b1, b2] + ([lnaf] if lnaf else []), keep_same_engine=True)
            l1_ = act(lnst[:, 14:15], lnst[:, 13:14], AF.Ln, bias=cstf[:, 1:2],
                      waits=[b3])
            l2_ = act(lnst[:, 15:16], lnst[:, 14:15], AF.Exp, scale=-0.5, waits=[l1_])
            lnaf = l1_
            yn = vop("tensor_scalar", s1ap, s1ap, lnst[:, 12:13], lnst[:, 15:16],
                     ALU.subtract, ALU.mult, waits=[l2_])
            lnf = yn
            yr = y_out.rearrange("(t p) c -> t p c", p=128)
            yst = dma(yr[(tbase // 128) + tc], s1ap, "sync", waits=[yn])
            toks_y.append(yst)
        prev_tile_done = toks_y[-1]
        warena_free = ln_in_toks[-1]
    return toks_y


def _run(inputs):
    consts = _host_consts()
    lay = _host_layout(inputs)
    x = np.asarray(inputs["L0_attn_x"])
    nc = build_nc()
    in_maps = []
    for c in range(N_CORES):
        m = dict(lay)
        m.update(consts)
        m["x"] = np.ascontiguousarray(x[2 * c:2 * c + 2].reshape(TOK, D))
        in_maps.append(m)
    res = run_bass_kernel_spmd(nc, in_maps, core_ids=list(range(N_CORES)))
    return res


def kernel(**inputs):
    res = _run(inputs)
    return np.concatenate(
        [res.results[c]["y"].reshape(2, L, D) for c in range(N_CORES)], 0)


# version 11 (final)
# speedup vs baseline: 1.1817x; 1.0016x over previous
import numpy as np
import ml_dtypes
import contextlib
import concourse.bass as bass
import concourse.mybir as mybir
from concourse.bass_utils import run_bass_kernel_spmd
from concourse.ap import AP

F32 = mybir.dt.float32
BF16 = mybir.dt.bfloat16
AF = mybir.ActivationFunctionType
ALU = mybir.AluOpType
AX = mybir.AxisListType
BF = ml_dtypes.bfloat16

N_CORES = 8
SEQS = 2
L = 2048
D = 1024
H = 16
HD = 64
TOK = SEQS * L
CSH = 24.0
EPS = 1e-5
NBK = 32
TBL = 4095
GW = 3968
E = 8
EH = 64
C2 = 2048
HC = 512
O3 = 3072
FT = 512
NFT = TOK // FT
DT_ = 0.1

PHASE_B = True
DEBUG_Y1 = False
DEBUG_ATT = False
DEBUG_S1 = False
BARRIER_OPROJ = False
DEBUG_B = False

HL = H * 65



def _bucket_of_delta(delta):
    n = np.asarray(delta, np.int64)
    out = np.zeros_like(n)
    small = (n >= 1) & (n < 16)
    out[small] = n[small]
    big = n >= 16
    nf = n[big].astype(np.float32)
    v = np.log(nf / np.float32(16.0)) / np.float32(np.log(128.0 / 16.0)) * np.float32(16.0)
    out[big] = np.minimum(16 + v.astype(np.int32), 31)
    return out


def _host_consts():
    c = {}
    c["idn32"] = np.eye(128, dtype=np.float32)
    c["idnb"] = np.eye(128, dtype=np.float32).astype(BF)
    m = np.arange(TBL)
    bk = _bucket_of_delta(2047 - m)
    oh = np.zeros((NBK, TBL), np.float32)
    oh[bk, m] = 1.0
    c["ohrev"] = oh.astype(BF)
    sel = np.zeros((E, HC), np.float32)
    for e in range(E):
        sel[e, e * EH:(e + 1) * EH] = 1.0
    c["sel"] = sel.astype(BF)
    c["cstf"] = np.ascontiguousarray(
        np.broadcast_to(np.array([[-CSH, EPS]], np.float32), (128, 2)))
    return c


def _host_layout(inp):
    a = {}
    g = lambda k: np.asarray(inp[k])
    for nm in ("wq", "wk", "wv", "wo"):
        a[nm] = np.ascontiguousarray(g("L0_attn_" + nm))
    a["bqT"] = np.ascontiguousarray(g("L0_attn_bq").reshape(8, 128).T)
    a["bkT"] = np.ascontiguousarray(g("L0_attn_bk").reshape(8, 128).T)
    a["bv_row"] = np.ascontiguousarray(g("L0_attn_bv").reshape(1, D))
    a["bo_row"] = np.ascontiguousarray(g("L0_attn_bo").reshape(1, D))
    a["rel_bias"] = np.ascontiguousarray(g("L0_attn_rel_bias"))
    a["rw1"] = np.ascontiguousarray(g("L0_ffn_router_w1"))
    a["rb1T"] = np.ascontiguousarray(g("L0_ffn_router_b1").reshape(2, 128).T)
    a["rw2"] = np.ascontiguousarray(g("L0_ffn_router_w2"))
    a["rb2_row"] = np.ascontiguousarray(g("L0_ffn_router_b2").reshape(1, E))
    a["hw1"] = np.ascontiguousarray(g("L0_ffn_halt_w1"))
    a["hb1T"] = np.ascontiguousarray(g("L0_ffn_halt_b1").reshape(2, 128).T)
    a["hw2"] = np.ascontiguousarray(g("L0_ffn_halt_w2"))
    a["hb2rep"] = np.ascontiguousarray(
        np.broadcast_to(g("L0_ffn_halt_b2").reshape(1, 1), (128, 1)))
    ew1 = g("L0_ffn_expert_w1")
    a["w1cat"] = np.ascontiguousarray(np.transpose(ew1, (1, 0, 2)).reshape(C2, HC))
    a["b1catT"] = np.ascontiguousarray(g("L0_ffn_expert_b1").reshape(-1).reshape(4, 128).T)
    a["w2cat"] = np.ascontiguousarray(g("L0_ffn_expert_w2").reshape(HC, O3))
    a["b2cat"] = np.ascontiguousarray(g("L0_ffn_expert_b2"))
    a["sw1"] = np.ascontiguousarray(g("L0_ffn_shared_w1"))
    a["sb1T"] = np.ascontiguousarray(g("L0_ffn_shared_b1").reshape(64, 1))
    a["sw2p"] = np.ascontiguousarray(
        np.concatenate([g("L0_ffn_shared_w2"), g("L0_ffn_shared_b2").reshape(1, O3)], 0))
    a["swrep"] = np.ascontiguousarray(
        np.broadcast_to(np.asarray(g("L0_ffn_shared_weight"), np.float32).reshape(1, 1), (128, 1)))
    a["refw1"] = np.ascontiguousarray(g("L0_ffn_refine_w1"))
    a["refb1T"] = np.ascontiguousarray(g("L0_ffn_refine_b1").reshape(16, 128).T)
    a["refw2"] = np.ascontiguousarray(g("L0_ffn_refine_w2"))
    a["refb2_row"] = np.ascontiguousarray(g("L0_ffn_refine_b2").reshape(1, D))
    a["mu_rep"] = np.ascontiguousarray(np.broadcast_to(g("L0_ffn_mu").reshape(1, D), (128, D)))
    a["iw_rep"] = np.ascontiguousarray(
        np.broadcast_to(g("L0_ffn_integration_weight").reshape(1, D), (128, D)))
    return a



ENGS = ("tensor", "vector", "scalar", "gpsimd", "sync")
DMA_ENGS = ("gpsimd", "sync")


class Tok:
    __slots__ = ("eng", "idx")
    def __init__(self, eng, idx):
        self.eng, self.idx = eng, idx


class Prog:
    def __init__(self, nc):
        self.nc = nc
        self.ops = {e: [] for e in ENGS}

    def op(self, eng, fn, waits=(), mark=False, keep_same_engine=False):
        ws = []
        for w in waits:
            if w is None or (w.eng == eng and eng not in DMA_ENGS
                             and not keep_same_engine):
                continue
            ws.append(w)
        self.ops[eng].append([fn, ws, mark, 0])
        return Tok(eng, len(self.ops[eng]) - 1)

    def emit(self, block, stack, final_toks):
        sems = {e: stack.enter_context(self.nc.semaphore("sem_" + e)) for e in ENGS}
        for e in DMA_ENGS:
            for rec in self.ops[e]:
                rec[2] = True
        for e in ENGS:
            for rec in self.ops[e]:
                for w in rec[1]:
                    self.ops[w.eng][w.idx][2] = True
        for w in final_toks:
            self.ops[w.eng][w.idx][2] = True
        for e in ENGS:
            acc = 0
            step = 16 if e in DMA_ENGS else 1
            for rec in self.ops[e]:
                if rec[2]:
                    acc += step
                rec[3] = acc

        def mk_stream(e):
            ops = self.ops[e]
            step = 16 if e in DMA_ENGS else 1
            def body(eng):
                for fn, ws, marked, val in ops:
                    byeng = {}
                    for w in ws:
                        wv = self.ops[w.eng][w.idx][3]
                        byeng[w.eng] = max(byeng.get(w.eng, 0), wv)
                    for we, wv in sorted(byeng.items()):
                        eng.wait_ge(sems[we], wv)
                    inst = fn(eng)
                    if marked:
                        inst.then_inc(sems[e], step)
            return body

        for e in ENGS:
            if self.ops[e]:
                getattr(block, e)(mk_stream(e))
        byeng = {}
        for w in final_toks:
            byeng[w.eng] = max(byeng.get(w.eng, 0), self.ops[w.eng][w.idx][3])
        def fin(eng):
            for we, wv in sorted(byeng.items()):
                if we != "sync":
                    eng.wait_ge(sems[we], wv)
            if "sync" in byeng:
                eng.wait_ge(sems["sync"], byeng["sync"])
        block.sync(fin)



def build_nc():
    nc = bass.Bass("TRN2", target_bir_lowering=False, debug=False)
    dt = nc.dram_tensor

    x_in = dt("x", [TOK, D], F32, kind="ExternalInput").ap()
    y_out = dt("y", [TOK, D], F32, kind="ExternalOutput").ap()
    y1scr = dt("y1scr", [TOK, D], F32, kind="Internal").ap()
    y1dbg = dt("y1dbg", [TOK, D], F32, kind="ExternalOutput").ap() if DEBUG_Y1 else None
    if DEBUG_ATT:
        dbg = {nm: dt(nm, [128, 8 * L], BF16, kind="ExternalOutput").ap()
               for nm in ("qdbg", "kdbg", "ctxdbg")}
        dbg["vdbg"] = dt("vdbg", [128, 16 * HL], BF16, kind="ExternalOutput").ap()
        dbg["crawdbg"] = dt("crawdbg", [128, 1024], F32, kind="ExternalOutput").ap()
        dbg["srowdbg"] = dt("srowdbg", [1, 1024], F32, kind="ExternalOutput").ap()
    s1dbg = dt("s1dbg", [TOK, D], F32, kind="ExternalOutput").ap() if DEBUG_S1 else None
    lndbg = dt("lndbg", [32, 128, 16], F32, kind="ExternalOutput").ap() if DEBUG_S1 else None

    hb = {}
    for nm, sh, dd in [
            ("wq", (D, D), F32), ("wk", (D, D), F32), ("wv", (D, D), F32), ("wo", (D, D), F32),
            ("bqT", (128, 8), F32), ("bkT", (128, 8), F32),
            ("bv_row", (1, D), F32), ("bo_row", (1, D), F32), ("rel_bias", (NBK, H), F32),
            ("rw1", (D, 256), F32), ("rb1T", (128, 2), F32), ("rw2", (256, E), F32),
            ("rb2_row", (1, E), F32),
            ("hw1", (D, 256), F32), ("hb1T", (128, 2), F32), ("hw2", (256, 1), F32),
            ("hb2rep", (128, 1), F32),
            ("w1cat", (C2, HC), F32), ("b1catT", (128, 4), F32), ("w2cat", (HC, O3), F32),
            ("b2cat", (E, O3), F32),
            ("sw1", (C2, EH), F32), ("sb1T", (EH, 1), F32), ("sw2p", (EH + 1, O3), F32),
            ("swrep", (128, 1), F32),
            ("refw1", (D, C2), F32), ("refb1T", (128, 16), F32), ("refw2", (C2, D), F32),
            ("refb2_row", (1, D), F32),
            ("mu_rep", (128, D), F32), ("iw_rep", (128, D), F32),
            ("idn32", (128, 128), F32), ("idnb", (128, 128), BF16),
            ("cstf", (128, 2), F32),
            ("ohrev", (NBK, TBL), BF16), ("sel", (E, HC), BF16)]:
        hb[nm] = dt(nm, list(sh), dd, kind="ExternalInput").ap()

    WSCR_SHAPES = {"w1cat": (C2, HC), "w2cat": (HC, O3), "sw1": (C2, EH),
                   "sw2p": (EH + 1, O3), "rw1": (D, 256), "hw1": (D, 256),
                   "refw1": (D, C2), "refw2": (C2, D), "rw2": (256, E),
                   "hw2": (256, 1), "b2cat": (E, O3)}
    wscr = {nm: dt("scr_" + nm, list(sh), BF16, kind="Internal").ap()
            for nm, sh in WSCR_SHAPES.items()}

    stack = contextlib.ExitStack()
    stackA = contextlib.ExitStack()
    sb = {}
    def sbuf(name, shape, dtype, st=None):
        sb[name] = (st or stack).enter_context(nc.sbuf_tensor(name, list(shape), dtype))
        return sb[name]
    def sbufA(name, shape, dtype):
        return sbuf(name, shape, dtype, st=stackA)

    onesb = sbuf("onesb", [1, 128], BF16)

    lnst  = sbuf("lnst",  [128, 18], F32)
    idn32 = sbuf("idn32s", [128, 128], F32)
    idnb_ = sbuf("idnbs", [128, 128], BF16)
    cstf  = sbuf("cstfs", [128, 2], F32)
    xbt   = sbufA("xbt",   [128, D], F32)
    xfm   = sbufA("xfm",   [128, 8 * L], BF16)
    qfm   = sbufA("qfm",   [128, 8 * L], BF16)
    kfm   = sbufA("kfm",   [128, 8 * L], BF16)
    vtm   = sbufA("vtm",   [128, 16 * HL], BF16)
    wtm   = sbufA("wtm",   [128, 2 * D], F32)
    wT    = sbufA("wT",    [128, 8 * D], BF16)
    erev  = sbufA("erev",  [H, TBL], BF16)
    ptile = sbufA("ptile", [128, 2 * 1024], BF16)
    craw  = sbufA("craw",  [128, 1024], F32)
    xf32  = sbufA("xf32",  [128, 2 * 1024], F32)
    srow  = sbufA("srow",  [1, 1024], F32)
    scol  = sbufA("scol",  [128, 16], F32)
    brow  = sbufA("brow",  [1, 2 * D], BF16)
    bias8 = sbufA("bias8", [128, 24], F32)
    relb  = sbufA("relb",  [NBK, H], BF16)
    relb32 = sbufA("relb32", [NBK, H], F32)
    ones32 = sbufA("ones32", [1, 64], F32)

    psum = stack.enter_context(nc.psum_tensor("PS", [128, 4096], F32))
    psb = psum[:, :].bitcast(BF16)

    def pbank(i, w=1):
        return psum[:, i * 512:(i + w) * 512]

    def pbankb(i, off, w):
        return psb[:, i * 1024 + off: i * 1024 + off + w]

    P = Prog(nc)

    def dma(dst, src, eng="sync", waits=(), slow=False):
        def fn(e, dst=dst, src=src, slow=slow):
            return e.dma_start(out=dst, in_=src, allow_slow_non_contiguous=slow)
        return P.op(eng, fn, waits)

    def mm(out, lhsT, rhs, start, stop, waits=()):
        def fn(e, out=out, lhsT=lhsT, rhs=rhs, start=start, stop=stop):
            return e.matmul(out, lhsT, rhs, start=start, stop=stop)
        return P.op("tensor", fn, waits)

    def tr(out, in_, ident, waits=()):
        def fn(e, out=out, in_=in_, ident=ident):
            return e.transpose(out, in_, ident)
        return P.op("tensor", fn, waits)

    def act(out, in_, func, bias=0.0, scale=1.0, waits=()):
        def fn(e, out=out, in_=in_, func=func, bias=bias, scale=scale):
            return e.activation(out, in_, func, bias=bias, scale=scale)
        return P.op("scalar", fn, waits)

    def acopy(out, in_, waits=()):
        def fn(e, out=out, in_=in_):
            return e.copy(out, in_)
        return P.op("scalar", fn, waits)

    def vop(name, *args, waits=(), mark=False, keep_same_engine=False, **kw):
        def fn(e, name=name, args=args, kw=kw):
            return getattr(e, name)(*args, **kw)
        return P.op("vector", fn, waits, mark=mark, keep_same_engine=keep_same_engine)

    tk_id32 = dma(idn32[:, :], hb["idn32"][:, :])
    tk_cst = dma(cstf[:, :], hb["cstf"][:, :])
    tk_idb = dma(idnb_[:, :], hb["idnb"][:, :])
    ohv_stack = contextlib.ExitStack()
    ohv = ohv_stack.enter_context(nc.sbuf_tensor("ohv", [NBK, TBL], BF16))
    tk_relb0 = dma(relb32[:, :], hb["rel_bias"][:, :], "sync")
    tk_relb = vop("tensor_copy", relb[:, :], relb32[:, :], waits=[tk_relb0])
    tk_oh = dma(ohv[:, :], hb["ohrev"][:, :], "sync")
    tk_ones = vop("memset", onesb[:, :], 1.0)
    tk_ones32 = vop("memset", ones32[:, :], 1.0)
    tk_srow1 = dma(srow[0:1, :], hb["bv_row"][:, :], "sync")
    tk_brow = acopy(brow[0:1, 0:D], srow[0:1, :], waits=[tk_srow1])
    tk_srow2 = dma(srow[0:1, :], hb["bo_row"][:, :], "sync", waits=[tk_brow])
    tk_brow2 = acopy(brow[0:1, D:2 * D], srow[0:1, :], waits=[tk_srow2])
    tk_srow_free = tk_brow2
    tk_b8 = dma(bias8[:, 0:8], hb["bqT"][:, :])
    tk_b8b = dma(bias8[:, 8:16], hb["bkT"][:, :])
    tk_bqs = vop("tensor_scalar_mul", bias8[:, 16:24], bias8[:, 0:8], 0.125,
                 waits=[tk_b8])

    erev_mm = []
    for j in range(8):
        c0, c1 = j * 512, min((j + 1) * 512, TBL)
        erev_mm.append(mm(psum[0:H, c0:c1], relb[:, :], ohv[:, c0:c1], True, True,
                          waits=[tk_relb, tk_oh]))
    tk_erev = act(erev[:, :], psum[0:H, 0:TBL], AF.Exp, waits=erev_mm)
    ohv_stack.close()
    gtab = sbufA("gtab", [128, 2 * GW], BF16)

    tk_vones = vop("memset", vtm[:, :], 1.0)

    pieces = []
    for nm in WSCR_SHAPES:
        rows, cols = WSCR_SHAPES[nm]
        srct, dsth = hb[nm], wscr[nm]
        nrb = (rows + 127) // 128
        for rb in range(nrb):
            r0 = rb * 128
            rr = min(128, rows - r0)
            for cc0 in range(0, cols, 1024):
                cc1 = min(cc0 + 1024, cols)
                pieces.append((srct, dsth, r0, rr, cc0, cc1))
    NSL = 6
    cast_lds = []
    cast_sts = []
    ncast = len(pieces)

    def _cast_load():
        ci = len(cast_lds)
        if ci >= ncast:
            return
        srct, dsth, r0, rr, cc0, cc1 = pieces[ci]
        sl = ci % NSL
        w = []
        if ci >= NSL:
            w.append(cast_sts[ci - NSL])
        if ci == 0:
            w.append(tk_erev)
        cast_lds.append(dma(gtab[0:rr, sl * 1024: sl * 1024 + cc1 - cc0],
                            srct[r0:r0 + rr, cc0:cc1], "gpsimd", waits=w))

    def _cast_store():
        ci = len(cast_sts)
        if ci >= ncast or len(cast_lds) < min(ci + 2, ncast):
            return
        srct, dsth, r0, rr, cc0, cc1 = pieces[ci]
        sl = ci % NSL
        if ci + 1 < ncast:
            settle = cast_lds[ci + 1]
        else:
            settle = dma(scol[0:1, 8:16], hb["bo_row"][0:1, 0:8], "gpsimd",
                         waits=[cast_lds[ci]])
        cast_sts.append(dma(dsth[r0:r0 + rr, cc0:cc1],
                            gtab[0:rr, sl * 1024: sl * 1024 + cc1 - cc0], "sync",
                            waits=[cast_lds[ci], settle]))

    def emit_cast_chunk(nl, ns):
        for _ in range(nl):
            _cast_load()
        for _ in range(ns):
            _cast_store()

    def drain_casts():
        while len(cast_lds) < ncast or len(cast_sts) < ncast:
            emit_cast_chunk(2, 2)
        return cast_sts[-1]

    emit_cast_chunk(3, 0)
    cast_last = None

    y_toks = []
    y1_toks = []

    def weight_transpose(name, first_wait, bank_waits=(None, None), wT_wait=None):
        xr = hb[name].rearrange("(a p) c -> a p c", p=128)
        evs = []
        slot_free = [None] * 2
        bank_free = list(bank_waits)
        lts = []

        def emit_load(rb):
            sl = rb % 2
            w = []
            if first_wait is not None and rb == 0:
                w.append(first_wait)
            if slot_free[sl] is not None:
                w.append(slot_free[sl])
            lts.append(dma(wtm[:, sl * D:(sl + 1) * D], xr[rb], "sync", waits=w))

        emit_load(0)
        for rb in range(8):
            sl = rb % 2
            if rb + 1 < 8:
                emit_load(rb + 1)
                settle = lts[rb + 1]
            else:
                settle = dma(scol[0:1, 0:8], hb["bo_row"][0:1, 0:8], "sync")
            trs = []
            for cb in range(8):
                bk = cb // 4
                tw = [lts[rb], tk_id32]
                if cb == 0:
                    tw.append(settle)
                if cb % 4 == 0 and bank_free[bk] is not None:
                    tw.append(bank_free[bk])
                trs.append(tr(psum[:, (6 + bk) * 512 + (cb % 4) * 128:
                                   (6 + bk) * 512 + (cb % 4) * 128 + 128],
                              wtm[:, sl * D + cb * 128: sl * D + (cb + 1) * 128],
                              idn32[:, :], waits=tw))
            slot_free[sl] = trs[-1]
            wTt = wT[:, :]
            for bk in range(2):
                dst = AP(wTt.tensor, (4 * bk) * D + rb * 128, [[8 * D, 128], [D, 4], [1, 128]])
                srcp = psum[:, (6 + bk) * 512:(6 + bk) * 512 + 512].rearrange(
                    "p (a b) -> p a b", a=4)
                ew = list(trs[bk * 4:(bk + 1) * 4])
                if rb == 0 and wT_wait is not None:
                    ew.append(wT_wait)
                ev = acopy(dst, srcp, waits=ew)
                evs.append(ev)
                bank_free[bk] = ev
        return evs[-1], evs

    prev_seq_tok = None
    for s in range(SEQS):
        xr = x_in.rearrange("(t p) c -> t p c", p=128)
        x_ev = []
        bank_free = [tk_erev, tk_erev] if s == 0 else [None, None]
        xslots = [xbt[:, :], xf32[:, 0:1024], xf32[:, 1024:2048]]
        xslot_free = [None, None, None]
        xlts = []

        def emit_xload(tt):
            sl3 = tt % 3
            w = []
            if tt == 0 and prev_seq_tok is not None:
                w.append(prev_seq_tok)
            if xslot_free[sl3] is not None:
                w.append(xslot_free[sl3])
            xlts.append(dma(xslots[sl3], xr[s * 16 + tt], "sync", waits=w))

        emit_xload(0)
        for tt in range(16):
            sl3 = tt % 3
            if tt + 1 < 16:
                emit_xload(tt + 1)
                settle = xlts[tt + 1]
            else:
                settle = dma(scol[0:1, 0:8], hb["bo_row"][0:1, 0:8], "sync")
            trs = []
            for cb in range(8):
                bk = cb // 4
                tw = [xlts[tt], tk_id32]
                if cb == 0:
                    tw.append(settle)
                if cb % 4 == 0 and bank_free[bk] is not None:
                    tw.append(bank_free[bk])
                trs.append(tr(psum[:, bk * 512 + (cb % 4) * 128:
                                   bk * 512 + (cb % 4) * 128 + 128],
                              xslots[sl3][:, cb * 128:(cb + 1) * 128],
                              idn32[:, :], waits=tw))
            xslot_free[sl3] = trs[-1]
            if s == 0:
                emit_cast_chunk(3, 3)
            xfmt = xfm[:, :]
            for bk in range(2):
                dst = AP(xfmt.tensor, (4 * bk) * L + tt * 128, [[8 * L, 128], [L, 4], [1, 128]])
                srcp = psum[:, bk * 512:bk * 512 + 512].rearrange("p (a b) -> p a b", a=4)
                ev = acopy(dst, srcp, waits=trs[bk * 4:(bk + 1) * 4])
                x_ev.append(ev)
                bank_free[bk] = ev
        xf32_guard = [xslot_free[1], xslot_free[2]]

        wq_tok, _ = weight_transpose("wq", prev_seq_tok,
            bank_waits=((tk_erev, tk_erev) if s == 0 else (prev_seq_tok, prev_seq_tok)),
            wT_wait=prev_seq_tok)
        q_ev = []
        ev_par = [None, None]
        for db in range(8):
            for t4 in range(4):
                par = (4 * db + t4) % 2
                pb = pbank(2 + 2 * par)
                ms = []
                for cb in range(8):
                    w = []
                    if cb == 0:
                        w = [wq_tok] + (x_ev if db == 0 and t4 == 0 else [])
                        if ev_par[par] is not None:
                            w.append(ev_par[par])
                    ms.append(mm(pb, wT[:, cb * D + db * 128: cb * D + (db + 1) * 128],
                                 xfm[:, cb * L + t4 * 512: cb * L + (t4 + 1) * 512],
                                 cb == 0, cb == 7, waits=w))
                ev = act(qfm[:, db * L + t4 * 512: db * L + (t4 + 1) * 512], pb, AF.Identity,
                         bias=bias8[:, 16 + db:17 + db], scale=0.125,
                         waits=[ms[-1], tk_bqs])
                q_ev.append(ev)
                ev_par[par] = ev
                if s == 0:
                    emit_cast_chunk(1, 1)
        wk_tok, _ = weight_transpose("wk", q_ev[-1],
            bank_waits=(q_ev[-1], q_ev[-1]), wT_wait=q_ev[-1])
        k_ev = []
        ev_par = [None, None]
        for db in range(8):
            for t4 in range(4):
                par = (4 * db + t4) % 2
                pb = pbank(2 + 2 * par)
                ms = []
                for cb in range(8):
                    w = []
                    if cb == 0:
                        w = [wk_tok]
                        if ev_par[par] is not None:
                            w.append(ev_par[par])
                        elif par == (4 * 7 + 3) % 2:
                            w.append(q_ev[-1])
                        else:
                            w.append(q_ev[-2])
                    ms.append(mm(pb, wT[:, cb * D + db * 128: cb * D + (db + 1) * 128],
                                 xfm[:, cb * L + t4 * 512: cb * L + (t4 + 1) * 512],
                                 cb == 0, cb == 7, waits=w))
                ev = act(kfm[:, db * L + t4 * 512: db * L + (t4 + 1) * 512], pb, AF.Identity,
                         bias=bias8[:, 8 + db:9 + db], scale=1.0,
                         waits=[ms[-1], tk_b8b])
                k_ev.append(ev)
                ev_par[par] = ev
                if s == 0:
                    emit_cast_chunk(1, 1)
        wv_tok, _ = weight_transpose("wv", k_ev[-1],
            bank_waits=(k_ev[-1], k_ev[-1]), wT_wait=k_ev[-1])
        v_ev = []
        ev_par = [None, None]
        gi = 0
        for tt in range(16):
            for dvh in range(2):
                par = gi % 2
                gi += 1
                pb = pbank(2 + 2 * par)
                ms = []
                for cb in range(8):
                    w = []
                    if cb == 0:
                        w = [wv_tok]
                        if ev_par[par] is not None:
                            w.append(ev_par[par])
                        else:
                            w.append(k_ev[-1])
                    ms.append(mm(pb, xfm[:, cb * L + tt * 128: cb * L + (tt + 1) * 128],
                                 wT[:, cb * D + dvh * 512: cb * D + (dvh + 1) * 512],
                                 cb == 0, False, waits=w))
                ms.append(mm(pb, onesb[0:1, 0:128],
                             brow[0:1, dvh * 512:(dvh + 1) * 512], False, True,
                             waits=[tk_ones, tk_brow]))
                vt = vtm[:, :]
                dst = AP(vt.tensor, tt * HL + dvh * 8 * 65,
                         [[16 * HL, 128], [65, 8], [1, 64]])
                src = pb.rearrange("p (a b) -> p a b", a=8)
                ev = acopy(dst, src, waits=[ms[-1], tk_vones])
                v_ev.append(ev)
                ev_par[par] = ev
                if s == 0:
                    emit_cast_chunk(1, 1)

        if s == 0:
            cast_last = drain_casts()

        ctx_done = []
        g_free = [None, None]
        st_free = [None, None]
        p_free = [None, None]
        pv_free = [None, None]
        craw_free = [None, None]
        glue_free = tk_srow_free if s == 0 else None
        first_st = True
        epi = None
        m = 0

        def emit_epilogue_a(e):
            nonlocal glue_free
            hh_, qh_, pvb_, par_, pvl_ = e
            cr = craw[:, 0:1024]
            w = list(pvl_)
            if craw_free[0] is not None:
                w.append(craw_free[0])
            evt = acopy(cr[0:65, :], psum[0:65, pvb_ * 512: pvb_ * 512 + 1024], waits=w)
            return e + (evt,)

        def emit_sums(e, jj):
            nonlocal glue_free
            hh_, qh_, pvb_, par_, pvl_, evt = e[:6]
            w = list(pvl_) + [evt]
            if glue_free is not None:
                w.append(glue_free)
            t3 = vop("reciprocal", srow[0:1, jj * 512:(jj + 1) * 512],
                     psum[64:65, (pvb_ + jj) * 512:(pvb_ + jj) * 512 + 512],
                     waits=w)
            return e + (t3,)

        def emit_epilogue_b(e):
            nonlocal glue_free
            hh_, qh_, pvb_, par_, pvl_, evt, t3a, t3b = e
            hc_, hr_ = hh_ // 2, (hh_ % 2) * 64
            cr = craw[:, 0:1024]
            rep = []
            for jj in range(2):
                rep.append(mm(psum[0:64, (pvb_ + jj) * 512:(pvb_ + jj) * 512 + 512],
                              ones32[0:1, 0:64], srow[0:1, jj * 512:(jj + 1) * 512],
                              True, True, waits=[t3a if jj == 0 else t3b,
                                                 tk_ones32, evt]))
            glue_free = rep[-1]
            nrm = []
            for jj in range(2):
                dst = xfm[hr_:hr_ + 64, hc_ * L + qh_ * 1024 + jj * 512:
                          hc_ * L + qh_ * 1024 + (jj + 1) * 512]
                w = [rep[jj], evt]
                if hh_ == 0 and qh_ == 0 and jj == 0:
                    w += v_ev
                nrm.append(vop("tensor_tensor", dst, cr[0:64, jj * 512:(jj + 1) * 512],
                               psum[0:64, (pvb_ + jj) * 512:(pvb_ + jj) * 512 + 512],
                               ALU.mult, waits=w))
            craw_free[0] = nrm[-1]
            pv_free[(pvb_ - 4) // 2] = nrm[-1]
            ctx_done.extend(nrm)

        for h in range(H):
            gpar = h % 2
            w = [tk_erev]
            if h < 2:
                w.append(q_ev[0])
                if s == 0:
                    w.append(cast_last)
            if g_free[gpar] is not None:
                w.append(g_free[gpar])
            ert = erev[:, :]
            gsrc = AP(ert.tensor, h * TBL, [[TBL, 1], [1, 128], [1, GW]])
            gtok = dma(gtab[:, gpar * GW:(gpar + 1) * GW], gsrc, "sync", waits=w)
            hc, hr = h // 2, (h % 2) * 64
            last_pm = None
            for qh in range(2):
                if epi is not None:
                    epi = emit_epilogue_a(epi)
                pvpair = m % 2
                pvb = 4 + 2 * pvpair
                pv_ms = []
                ets = {}
                pms = {}
                def emit_st(kt):
                    spar = kt % 2
                    ms = []
                    nonlocal first_st
                    for jj in range(2):
                        w = []
                        if jj == 0:
                            if first_st:
                                w = q_ev + k_ev + v_ev
                                first_st = False
                            if st_free[spar] is not None:
                                w.append(st_free[spar])
                        ms.append(mm(psum[:, (2 * spar) * 512 + jj * 512:
                                          (2 * spar) * 512 + (jj + 1) * 512],
                                     kfm[hr:hr + 64, hc * L + kt * 128: hc * L + (kt + 1) * 128],
                                     qfm[hr:hr + 64, hc * L + qh * 1024 + jj * 512:
                                         hc * L + qh * 1024 + (jj + 1) * 512],
                                     True, True, waits=w))
                    return ms
                def emit_exp_mult(kt, ms):
                    spar = kt % 2
                    ppar = kt % 2
                    w = ms + ([p_free[ppar]] if p_free[ppar] is not None else [])
                    et = act(ptile[:, ppar * 1024:(ppar + 1) * 1024],
                             psum[:, (2 * spar) * 512:(2 * spar) * 512 + 1024],
                             AF.Exp, bias=cstf[:, 0:1], waits=w)
                    st_free[spar] = et
                    off = 1024 - 1024 * qh + 128 * kt
                    gt_t = gtab[:, :]
                    grev = AP(gt_t.tensor, gpar * GW + off + 1023, [[2 * GW, 128], [-1, 1024]])
                    pm = vop("tensor_tensor",
                             ptile[:, ppar * 1024:(ppar + 1) * 1024],
                             ptile[:, ppar * 1024:(ppar + 1) * 1024],
                             grev, ALU.mult, waits=[et, gtok])
                    return pm
                def emit_pv(kt, pm):
                    ppar = kt % 2
                    for jj in range(2):
                        w = [pm]
                        if kt == 0 and pv_free[pvpair] is not None:
                            w.append(pv_free[pvpair])
                        pv_ms.append(mm(psum[0:65, (pvb + jj) * 512:(pvb + jj + 1) * 512],
                                        vtm[:, kt * HL + h * 65: kt * HL + h * 65 + 65],
                                        ptile[:, ppar * 1024 + jj * 512:
                                              ppar * 1024 + (jj + 1) * 512],
                                        kt == 0, kt == 15, waits=w))
                    p_free[ppar] = pv_ms[-1]
                ms0 = emit_st(0)
                pms[0] = emit_exp_mult(0, ms0)
                for kt in range(16):
                    if kt + 1 < 16:
                        msn = emit_st(kt + 1)
                        pms[kt + 1] = emit_exp_mult(kt + 1, msn)
                    emit_pv(kt, pms[kt])
                    if kt == 5 and epi is not None and len(epi) == 6:
                        epi = emit_sums(epi, 0)
                    if kt == 11 and epi is not None and len(epi) == 7:
                        epi = emit_sums(epi, 1)
                last_pm = pms[15]
                if epi is not None:
                    emit_epilogue_b(epi)
                epi = (h, qh, pvb, m % 2, [pv_ms[-1], pv_ms[-2]])
                m += 1
            g_free[gpar] = last_pm
        epi = emit_epilogue_a(epi)
        epi = emit_sums(epi, 0)
        epi = emit_sums(epi, 1)
        emit_epilogue_b(epi)

        wo_tok, _ = weight_transpose("wo", v_ev[-1],
            bank_waits=(pv_free[1], pv_free[1]), wT_wait=v_ev[-1])
        xl_free = [None, None]
        ev_par = [None, None]
        lnst_free = None
        lnst_act_free = None
        first_o = True
        for tt in range(16):
            par = tt % 2
            w = []
            if xl_free[par] is not None:
                w.append(xl_free[par])
            else:
                w.append(xf32_guard[par])
            xl = dma(xf32[:, par * 1024:(par + 1) * 1024], xr[s * 16 + tt], "sync", waits=w)
            s1_toks = []
            for oh2 in range(2):
                mpar = (2 * tt + oh2) % 2
                pb = pbank(2 + 2 * mpar)
                ms = []
                for cb in range(8):
                    w = []
                    if cb == 0:
                        w = [wo_tok]
                        if first_o:
                            w += ctx_done
                            if BARRIER_OPROJ:
                                w += [pv_free, glue_free] + rep_free + \
                                     [t for t in p_free if t is not None] + \
                                     [t for t in craw_free if t is not None]
                            first_o = False
                        if ev_par[mpar] is not None:
                            w.append(ev_par[mpar])
                        else:
                            w += [t for t in st_free if t is not None]
                            w += [t for t in pv_free if t is not None]
                    ms.append(mm(pb, xfm[:, cb * L + tt * 128: cb * L + (tt + 1) * 128],
                                 wT[:, cb * D + oh2 * 512: cb * D + (oh2 + 1) * 512],
                                 cb == 0, False, waits=w))
                ms.append(mm(pb, onesb[0:1, 0:128],
                             brow[0:1, D + oh2 * 512: D + (oh2 + 1) * 512],
                             False, True, waits=[tk_brow2]))
                sl = xf32[:, par * 1024 + oh2 * 512: par * 1024 + (oh2 + 1) * 512]
                stk = vop("tensor_tensor", sl, pb, sl, ALU.add, waits=[ms[-1], xl])
                s1_toks.append(stk)
                ev_par[mpar] = stk
            s1ap = xf32[:, par * 1024:(par + 1) * 1024]
            if DEBUG_S1:
                sdr = s1dbg.rearrange("(t p) c -> t p c", p=128)
                s1_toks.append(dma(sdr[s * 16 + tt], s1ap, "sync", waits=list(s1_toks)))
                y_toks.append(s1_toks[-1])
            w = s1_toks + ([lnst_free] if lnst_free is not None else [])
            b1 = vop("bn_stats", lnst[:, 0:6], s1ap[:, 0:512], waits=w, mark=True)
            b2 = vop("bn_stats", lnst[:, 6:12], s1ap[:, 512:1024], waits=w, mark=True)
            b3 = vop("bn_aggr", lnst[:, 12:14], lnst[:, 0:12],
                     waits=[b1, b2] + ([lnst_act_free] if lnst_act_free is not None else []),
                     keep_same_engine=True)
            l1 = act(lnst[:, 14:15], lnst[:, 13:14], AF.Ln, bias=cstf[:, 1:2], waits=[b3, tk_cst])
            l2 = act(lnst[:, 15:16], lnst[:, 14:15], AF.Exp, scale=-0.5, waits=[l1])
            lnst_act_free = l1
            if DEBUG_S1:
                lnt = dma(lndbg[s * 16 + tt], lnst[:, :], "sync", waits=[l2, b3])
                y_toks.append(lnt)
            yn = vop("tensor_scalar", s1ap, s1ap, lnst[:, 12:13], lnst[:, 15:16],
                     ALU.subtract, ALU.mult,
                     waits=[l2] + ([s1_toks[-1], lnt] if DEBUG_S1 else []))
            lnst_free = yn
            ysr = y1scr.rearrange("(t p) c -> t p c", p=128)
            yst = dma(ysr[s * 16 + tt], s1ap, "sync", waits=[yn])
            y1_toks.append(yst)
            if DEBUG_Y1:
                ydr = y1dbg.rearrange("(t p) c -> t p c", p=128)
                y_toks.append(dma(ydr[s * 16 + tt], s1ap, "sync", waits=[yn]))
            xl_free[par] = yst
        prev_seq_tok = y1_toks[-1]

    stackA.close()
    if PHASE_B:
        env = dict(nc=nc, P=P, stack=stack, sbuf=sbuf, sb=sb, psum=psum, psb=psb,
                   hb=hb, wscr=wscr, y1scr=y1scr, y_out=y_out, y1_toks=y1_toks,
                   dma=dma, mm=mm, tr=tr, act=act, acopy=acopy, vop=vop,
                   onesb=onesb, idnb_=idnb_, idn32=idn32, cstf=cstf, lnst=lnst,
                   prev_seq_tok=prev_seq_tok, cast_last=cast_last, APc=AP)
        y_toks += _emit_ffn(env)

    with nc.Block() as block:
        P.emit(block, stack, y_toks if y_toks else [y1_toks[-1]])
    stack.close()
    return nc


def _emit_ffn(env):
    nc, P, sbuf = env["nc"], env["P"], env["sbuf"]
    psum, psb = env["psum"], env["psb"]
    hb, wscr, y1scr, y_out = env["hb"], env["wscr"], env["y1scr"], env["y_out"]
    y1_toks = env["y1_toks"]
    dma, mm, tr, act, acopy, vop = (env["dma"], env["mm"], env["tr"], env["act"],
                                    env["acopy"], env["vop"])
    onesb, idnb_, idn32, cstf, lnst = (env["onesb"], env["idnb_"], env["idn32"],
                                       env["cstf"], env["lnst"])
    gate = env["prev_seq_tok"]
    AP_ = env["APc"]

    y1t  = sbuf("y1t",  [128, 4096], F32)
    itm  = sbuf("itm",  [128, 4096], F32)
    vtm2 = sbuf("vtm2", [128, 4096], F32)
    ctxf = sbuf("ctxf", [128, 16 * 512], BF16)
    hf   = sbuf("hf",   [128, 4 * 512], BF16)
    h1f  = sbuf("h1f",  [128, 4 * 512], BF16)
    h2f  = sbuf("h2f",  [128, 4 * 512], BF16)
    m1r  = sbuf("m1r",  [128, 4 * 512], BF16)
    m2r  = sbuf("m2r",  [128, 4 * 512], BF16)
    hsh  = sbuf("hsh",  [65, 512], BF16)
    rhid = sbuf("rhid", [128, 2 * 512], BF16)
    hhid = sbuf("hhid", [128, 2 * 512], BF16)
    m1f  = sbuf("m1f",  [8, 512], BF16)
    m2f  = sbuf("m2f",  [8, 512], BF16)
    m1t  = sbuf("m1t",  [128, 64], BF16)
    mq   = sbuf("mq",   [128, 32], F32)
    cols = sbuf("cols", [128, 32], F32)
    ew   = sbuf("ew",   [128, 16 * 512], F32)
    wa   = sbuf("wa",   [128, 32768], BF16)
    rha  = sbuf("rha",  [128, 4116], BF16)
    murep = sbuf("murep", [128, D], F32)
    iwrep = sbuf("iwrep", [128, D], F32)
    bct  = sbuf("bct",  [128, 32], F32)
    sbt  = sbuf("sbt",  [64, 1], F32)
    rows = sbuf("rows", [1, D + 16], BF16)
    selb = sbuf("selb", [8, HC], BF16)

    W1O, W2O, S1O, S2O, B2O = 0, 8192, 20480, 21504, 24576
    R1O, R2O = 0, 16384
    RWO, HWO, RW2O, HW2O = 0, 2048, 4096, 4112

    toks_y = []
    dbgB = {}
    if DEBUG_B:
        for nm, sh in (("d_m1t", (128, 64)), ("d_cols", (128, 32)), ("d_mq", (128, 32)),
                       ("d_rhid", (128, 1024)), ("d_hf", (128, 2048)), ("d_hsh", (65, 512)),
                       ("d_h1f", (128, 2048)), ("d_m1r", (128, 2048)),
                       ("d_i0", (128, 4096)), ("d_v0", (128, 4096)),
                       ("d_i1", (128, 4096)), ("d_v1", (128, 4096)),
                       ("d_ln2in", (128, 4096)), ("d_ctxf", (128, 8192)),
                       ("d_sa", (128, 512)), ("d_sb", (128, 512)), ("d_sg", (128, 512)), ("d_vn", (128, 512)), ("d_g", (128, 512)), ("d_cols2", (128, 32)), ("d_sb1", (128, 512)), ("d_sb2", (128, 512)), ("d_refiw", (128, 512)), ("d_vn0", (128, 512)), ("d_vn1", (128, 512)), ("d_vn2", (128, 512))):
            dbgB[nm] = nc.dram_tensor(nm, list(sh), F32 if nm not in ("d_m1t", "d_hf", "d_h1f", "d_m1r", "d_ctxf", "d_rhid", "d_hsh") else BF16, kind="ExternalOutput").ap()
        dbgB["d_hidr"] = nc.dram_tensor("d_hidr", [128, 8192], BF16, kind="ExternalOutput").ap()

    ld = []
    ld.append(dma(murep[:, :], hb["mu_rep"][:, :], "sync", waits=[gate]))
    ld.append(dma(iwrep[:, :], hb["iw_rep"][:, :], "sync", waits=[gate]))
    ld.append(dma(bct[:, 0:2], hb["rb1T"][:, :], "sync", waits=[gate]))
    ld.append(dma(bct[:, 2:4], hb["hb1T"][:, :], "sync", waits=[gate]))
    ld.append(dma(bct[:, 4:8], hb["b1catT"][:, :], "sync", waits=[gate]))
    ld.append(dma(bct[:, 8:24], hb["refb1T"][:, :], "sync", waits=[gate]))
    ld.append(dma(bct[:, 24:25], hb["hb2rep"][:, :], "sync", waits=[gate]))
    ld.append(dma(bct[:, 25:26], hb["swrep"][:, :], "sync", waits=[gate]))
    ld.append(dma(sbt[:, :], hb["sb1T"][:, :], "sync", waits=[gate]))
    tk_b = ld[-1]
    t1 = vop("tensor_scalar_mul", bct[:, 26:27], bct[:, 24:25], 0.5, waits=[ld[6]])
    t2 = act(bct[:, 27:28], bct[:, 25:26], AF.Tanh, scale=0.5, waits=[ld[7]])
    t3 = vop("tensor_scalar", bct[:, 28:29], bct[:, 27:28], 0.5, 0.5,
             ALU.mult, ALU.add, waits=[t2])
    t4 = vop("tensor_scalar", bct[:, 29:30], bct[:, 28:29], -1.0, 1.0,
             ALU.mult, ALU.add, waits=[t3], keep_same_engine=True, mark=True)
    t5 = vop("tensor_scalar_mul", bct[:, 30:31], bct[:, 29:30], 0.5,
             waits=[t4], keep_same_engine=True, mark=True)
    tk_cols = t5
    tmpr = ew[0:1, 0:D]
    r1 = dma(tmpr, hb["refb2_row"][:, :], "sync", waits=[gate])
    r2 = acopy(rows[0:1, 0:D], tmpr, waits=[r1])
    r3 = dma(ew[0:1, 0:E], hb["rb2_row"][:, :], "sync", waits=[r2])
    r4 = acopy(rows[0:1, D:D + E], ew[0:1, 0:E], waits=[r3])
    tk_rows = r4
    tk_hones = vop("memset", hsh[64:65, :], 1.0, waits=[gate])
    tk_sel = dma(selb[:, :], hb["sel"][:, :], "sync", waits=[gate])
    def arena_load(dst_t, dst_off, scr_ap, rows_, cols_, planes, waits):
        srcv = scr_ap.rearrange("(a p) c -> p a c", p=128) if rows_ > 128 else scr_ap
        if rows_ > 128:
            dstap = AP_(dst_t[:, :].tensor, dst_off,
                        [[dst_t.shape[1], 128], [cols_, planes], [1, cols_]])
        else:
            dstap = AP_(dst_t[:, :].tensor, dst_off, [[dst_t.shape[1], rows_], [1, cols_]])
        return dma(dstap, srcv, "sync", waits=waits, slow=(cols_ < 8))
    lw = []
    gate_c = [gate] + ([env["cast_last"]] if env.get("cast_last") is not None else [])
    lw.append(arena_load(rha, RWO, wscr["rw1"], D, 256, 8, gate_c))
    lw.append(arena_load(rha, HWO, wscr["hw1"], D, 256, 8, gate_c))
    lw.append(arena_load(rha, RW2O, wscr["rw2"], 256, 8, 2, gate_c))
    lw.append(arena_load(rha, HW2O, wscr["hw2"], 256, 1, 2, gate_c))
    tk_rh = lw[-1]

    bank_rd = [gate] * 8

    def mmg(outbank, outslice, parts, waits_first=()):
        t = None
        n = len(parts)
        for i, (lh, rh) in enumerate(parts):
            w = list(waits_first) + [bank_rd[outbank]] if i == 0 else []
            t = mm(outslice, lh, rh, i == 0, i == n - 1, waits=w)
        return t

    prev_tile_done = gate
    warena_free = gate
    ew_war = [None]
    for ft in range(NFT):
        tbase = ft * FT
        ysr = y1scr
        ysv = AP_(ysr.tensor, tbase * D, [[D, 128], [128 * D, 4], [1, D]])
        wv = [y1_toks[tbase // 128 + k] for k in range(4)] + [prev_tile_done]
        ly1 = dma(y1t[:, :], ysv, "sync", waits=wv)
        ly1s = dma(lnst[0:1, 16:18], hb["bo_row"][0:1, 0:2], "sync", waits=[ly1])
        icp = vop("tensor_copy", itm[:, :], y1t[:, :], waits=[ly1, ly1s])

        def to_fm(srctile, planes0, ctoks):
            evs = []
            for tc in range(4):
                trs = []
                for cb in range(8):
                    bk = 6 + cb // 4
                    w = list(ctoks) if (tc == 0 and cb == 0) else []
                    if cb % 4 == 0:
                        w.append(bank_rd[bk])
                    trs.append(tr(psum[:, bk * 512 + (cb % 4) * 128:
                                       bk * 512 + (cb % 4) * 128 + 128],
                                  srctile[:, tc * 1024 + cb * 128: tc * 1024 + (cb + 1) * 128],
                                  idn32[:, :], waits=w))
                ctxt = ctxf[:, :]
                for bk in range(2):
                    dst = AP_(ctxt.tensor, (planes0 + 4 * bk) * 512 + tc * 128,
                              [[16 * 512, 128], [512, 4], [1, 128]])
                    srcp = psum[:, (6 + bk) * 512:(6 + bk) * 512 + 512].rearrange(
                        "p (a b) -> p a b", a=4)
                    ev = acopy(dst, srcp, waits=trs[bk * 4:(bk + 1) * 4])
                    evs.append(ev)
                    bank_rd[6 + bk] = ev
            return evs
        y1fm_ev = to_fm(y1t, 0, [ly1, ly1s])

        rh_ev = []
        for hc in range(2):
            bk = hc % 2
            parts = [(rha[:, RWO + cb * 256 + hc * 128: RWO + cb * 256 + (hc + 1) * 128],
                      ctxf[:, cb * 512:(cb + 1) * 512]) for cb in range(8)]
            t = mmg(bk, psum[:, bk * 512:(bk + 1) * 512], parts,
                    waits_first=[tk_rh] + y1fm_ev)
            ev = act(rhid[:, hc * 512:(hc + 1) * 512], psum[:, bk * 512:(bk + 1) * 512],
                     AF.Gelu, bias=bct[:, 0 + hc:1 + hc], scale=1.0, waits=[t, tk_b])
            rh_ev.append(ev)
            bank_rd[bk] = ev
        lg_toks = []
        for tc in range(4):
            sl = psum[:, 2 * 512 + tc * 8: 2 * 512 + (tc + 1) * 8]
            parts = [(rhid[:, hc * 512 + tc * 128: hc * 512 + (tc + 1) * 128],
                      rha[:, RW2O + hc * 8: RW2O + (hc + 1) * 8]) for hc in range(2)]
            parts.append((onesb[0:1, 0:128], rows[0:1, D:D + E]))
            lg_toks.append(mmg(2, sl, parts, waits_first=rh_ev + [tk_rows]))
        mt_toks = []
        alpha_toks = []
        for tc in range(4):
            sl = psum[:, 2 * 512 + tc * 8: 2 * 512 + (tc + 1) * 8]
            c0 = tc * 8
            q1 = vop("reduce_max", cols[:, c0:c0 + 1], sl, axis=AX.X, waits=lg_toks)
            q2 = vop("tensor_single_scalar", mq[:, 0:8], sl, cols[:, c0:c0 + 1],
                     ALU.is_equal, waits=[q1], keep_same_engine=True, mark=True)
            q3 = vop("scalar_tensor_tensor", mq[:, 8:16], mq[:, 0:8], -1e30, sl,
                     ALU.mult, ALU.add, waits=[q2], keep_same_engine=True, mark=True)
            q4 = vop("reduce_max", cols[:, c0 + 1:c0 + 2], mq[:, 8:16], axis=AX.X,
                     waits=[q3], keep_same_engine=True, mark=True)
            q5 = vop("tensor_single_scalar", mq[:, 16:24], mq[:, 8:16],
                     cols[:, c0 + 1:c0 + 2], ALU.is_equal, waits=[q4],
                     keep_same_engine=True, mark=True)
            q6 = vop("tensor_copy", m1t[:, tc * 8: tc * 8 + 8], mq[:, 0:8], waits=[q2])
            q7 = vop("tensor_copy", m1t[:, 32 + tc * 8: 32 + tc * 8 + 8], mq[:, 16:24],
                     waits=[q5], keep_same_engine=True, mark=True)
            mt_toks.append((q6, q7))
            q8 = vop("tensor_sub", cols[:, c0 + 2:c0 + 3], cols[:, c0:c0 + 1],
                     cols[:, c0 + 1:c0 + 2], waits=[q4], keep_same_engine=True, mark=True)
            q9 = act(cols[:, c0 + 3:c0 + 4], cols[:, c0 + 2:c0 + 3], AF.Tanh, scale=0.5,
                     waits=[q8])
            qa = vop("tensor_scalar", cols[:, c0 + 4:c0 + 5], cols[:, c0 + 3:c0 + 4],
                     bct[:, 30:31], bct[:, 30:31], ALU.mult, ALU.add,
                     waits=[q9, tk_cols], mark=True)
            qb = vop("tensor_sub", cols[:, c0 + 5:c0 + 6], bct[:, 29:30],
                     cols[:, c0 + 4:c0 + 5], waits=[qa], keep_same_engine=True,
                     mark=True)
            alpha_toks.append(qb)
        bank_rd[2] = alpha_toks[-1]
        mtr = [[], []]
        for mi, (mtile_off, dstf) in enumerate(((0, m1f), (32, m2f))):
            bk = 6 + mi
            for tc in range(4):
                w = [mt_toks[tc][mi], env["idnb_"] and None] if False else [mt_toks[tc][mi]]
                if tc == 0:
                    w.append(bank_rd[bk])
                mtr[mi].append(tr(psb[0:8, bk * 1024 + tc * 128: bk * 1024 + (tc + 1) * 128],
                                  m1t[:, mtile_off + tc * 8: mtile_off + (tc + 1) * 8],
                                  idnb_[:, :], waits=w))
            ev = acopy(dstf[0:8, :], psb[0:8, bk * 1024: bk * 1024 + 512],
                       waits=mtr[mi])
            bank_rd[bk] = ev
            mtr[mi] = ev
        mrep_ev = []
        for mi, (mf, mr) in enumerate(((m1f, m1r), (m2f, m2r))):
            for hc in range(4):
                bk = hc % 2
                t = mmg(bk, psum[:, bk * 512:(bk + 1) * 512],
                        [(selb[0:8, hc * 128:(hc + 1) * 128], mf[0:8, :])],
                        waits_first=[mtr[mi], tk_sel])
                ev = acopy(mr[:, hc * 512:(hc + 1) * 512],
                           psum[:, bk * 512:(bk + 1) * 512], waits=[t])
                mrep_ev.append(ev)
                bank_rd[bk] = ev

        if DEBUG_B and ft == 0:
            toks_y.append(dma(dbgB["d_m1t"][:, :], m1t[:, :], "sync",
                              waits=[t for pair in mt_toks for t in pair]))
            toks_y.append(dma(dbgB["d_cols"][:, :], cols[:, :], "sync", waits=alpha_toks))
            toks_y.append(dma(dbgB["d_rhid"][:, :], rhid[:, :], "sync", waits=rh_ev))
            toks_y.append(dma(dbgB["d_m1r"][:, :], m1r[:, :], "sync", waits=mrep_ev))

        wvt = [warena_free]
        if ft == 0 and env.get("cast_last") is not None:
            wvt = wvt + [env["cast_last"]]
        l1 = arena_load(wa, W1O, wscr["w1cat"], C2, HC, 16, wvt)
        l2 = arena_load(wa, W2O, wscr["w2cat"], HC, O3, 4, wvt)
        l3 = arena_load(wa, S1O, wscr["sw1"], C2, EH, 16, wvt)
        l4 = arena_load(wa, S2O, wscr["sw2p"], EH + 1, O3, 1, wvt)
        l5 = arena_load(wa, B2O, wscr["b2cat"], E, O3, 1, wvt)
        lwt = [l1, l2, l3, l4, l5]

        ctrl_last = None
        for it in range(2):
            ncc = 8 if it == 0 else 16
            if it == 1:
                w_iv = [iup_last, vup_last]
                if DEBUG_B and ft == 0:
                    t_i0 = dma(dbgB["d_i0"][:, :], itm[:, :], "sync", waits=[iup_last])
                    t_v0 = dma(dbgB["d_v0"][:, :], vtm2[:, :], "sync", waits=[vup_last])
                    toks_y += [t_i0, t_v0]
                    w_iv += [t_i0, t_v0]
                ifm_ev = to_fm(itm, 0, w_iv)
                vfm_ev = to_fm(vtm2, 8, w_iv)
                ctx_ev = ifm_ev + vfm_ev
            else:
                ctx_ev = y1fm_ev
            h_ev = []
            for hc in range(4):
                bk = hc % 2
                parts = [(wa[:, W1O + cb * 512 + hc * 128: W1O + cb * 512 + (hc + 1) * 128],
                          ctxf[:, cb * 512:(cb + 1) * 512]) for cb in range(ncc)]
                t = mmg(bk, psum[:, bk * 512:(bk + 1) * 512], parts,
                        waits_first=[l1] + ctx_ev)
                ev = act(hf[:, hc * 512:(hc + 1) * 512], psum[:, bk * 512:(bk + 1) * 512],
                         AF.Gelu, bias=bct[:, 4 + hc:5 + hc], waits=[t, tk_b])
                h_ev.append(ev)
                bank_rd[bk] = ev
            parts = [(wa[0:128, S1O + cb * 64: S1O + (cb + 1) * 64],
                      ctxf[:, cb * 512:(cb + 1) * 512]) for cb in range(ncc)]
            t = mmg(2, psum[0:64, 2 * 512: 2 * 512 + 512], parts,
                    waits_first=[l3] + ctx_ev)
            hsh_ev = act(hsh[0:64, :], psum[0:64, 2 * 512: 2 * 512 + 512], AF.Gelu,
                         bias=sbt[:, 0:1], waits=[t, tk_b, tk_hones])
            bank_rd[2] = hsh_ev
            hm1 = vop("tensor_mul", h1f[:, :], hf[:, :], m1r[:, :],
                      waits=h_ev + mrep_ev)
            hm2 = vop("tensor_mul", h2f[:, :], hf[:, :], m2r[:, :],
                      waits=h_ev + mrep_ev, keep_same_engine=True, mark=True)
            hh_ev = []
            for hc in range(2):
                parts = [(rha[:, HWO + cb * 256 + hc * 128: HWO + cb * 256 + (hc + 1) * 128],
                          ctxf[:, cb * 512:(cb + 1) * 512]) for cb in range(8)]
                t = mmg(3, psum[:, 3 * 512: 3 * 512 + 512], parts,
                        waits_first=[tk_rh] + ctx_ev)
                ev = act(hhid[:, hc * 512:(hc + 1) * 512], psum[:, 3 * 512: 3 * 512 + 512],
                         AF.Gelu, bias=bct[:, 2 + hc:3 + hc], waits=[t, tk_b])
                hh_ev.append(ev)
                bank_rd[3] = ev
            if DEBUG_B and ft == 0 and it == 0:
                toks_y.append(dma(dbgB["d_hf"][:, :], hf[:, :], "sync", waits=h_ev))
                toks_y.append(dma(dbgB["d_hsh"][:, :], hsh[:, :], "sync", waits=[hsh_ev]))
                toks_y.append(dma(dbgB["d_h1f"][:, :], h1f[:, :], "sync", waits=[hm1]))
            hc_mm = []
            for tc in range(4):
                sl = psum[:, 2 * 512 + 16 + tc: 2 * 512 + 17 + tc]
                parts = [(hhid[:, hc * 512 + tc * 128: hc * 512 + (tc + 1) * 128],
                          rha[:, HW2O + hc: HW2O + hc + 1]) for hc in range(2)]
                hc_mm.append(mmg(2, sl, parts, waits_first=hh_ev))
            hcol_toks = []
            for tc in range(4):
                sl = psum[:, 2 * 512 + 16 + tc: 2 * 512 + 17 + tc]
                th = act(cols[:, tc * 8 + 6: tc * 8 + 7], sl, AF.Tanh, scale=0.5,
                         bias=bct[:, 26:27], waits=hc_mm + [t1])
                u1 = vop("tensor_scalar", cols[:, tc * 8 + 7: tc * 8 + 8],
                         cols[:, tc * 8 + 6: tc * 8 + 7], 0.5, 0.5, ALU.mult, ALU.add,
                         waits=[th], mark=True)
                u2 = vop("tensor_scalar", mq[:, 24 + tc: 25 + tc],
                         cols[:, tc * 8 + 6: tc * 8 + 7], 0.05, 0.05, ALU.mult, ALU.add,
                         waits=[th], mark=True)
                hcol_toks.append((u1, u2))
            bank_rd[2] = hcol_toks[-1][0]
            if DEBUG_B and ft == 0 and it == 0:
                toks_y.append(dma(dbgB["d_mq"][:, :], mq[:, :], "sync",
                                  waits=[t for p_ in hcol_toks for t in p_]))
                toks_y.append(dma(dbgB["d_ctxf"][0:128, 0:8192], ctxf[:, 0:8192],
                                  "sync", waits=ctx_ev))
                toks_y.append(dma(dbgB["d_cols2"][:, :], cols[:, :], "sync",
                                  waits=[t for p_ in hcol_toks for t in p_]))
            for tc in range(4):
                def ewb(i):
                    return ew[:, i * 512:(i + 1) * 512]
                for ch in range(2):
                    c0 = tc * 1024 + ch * 512
                    dtok = vop("tensor_sub", ewb(0), itm[:, c0:c0 + 512],
                               murep[:, ch * 512:(ch + 1) * 512],
                               waits=[icp if it == 0 else iup_last, ld[0]])
                    ev_parts = {}
                    s_wave = [("sh", "a"), ("sh", "g"), ("r1", "a"), ("r1", "g"),
                              ("r2", "a"), ("r2", "g")]
                    e_wave = [("sh", "b"), ("r1", "b"), ("r2", "b")]
                    order = (s_wave + e_wave) if (tc * 2 + ch) % 2 == 0 else                             (e_wave + s_wave)
                    for gi, (br, pname) in enumerate(order):
                        bi = ("sh", "r1", "r2").index(br)
                        pi = ("a", "b", "g").index(pname)
                        bk = 4 + gi % 2
                        war = ([ew_war[0][(br, pname)]]
                               if (ew_war[0] is not None and (br, pname) in ew_war[0])
                               else [])
                        off = pi * 1024 + ch * 512
                        if br == "sh":
                            parts = [(hsh[0:65, tc * 128:(tc + 1) * 128],
                                      wa[0:65, S2O + off: S2O + off + 512])]
                            wf = [l4, hsh_ev]
                        else:
                            mfm = m1f if br == "r1" else m2f
                            hxf = h1f if br == "r1" else h2f
                            hmt = hm1 if br == "r1" else hm2
                            parts = [(hxf[:, hc2 * 512 + tc * 128: hc2 * 512 + (tc + 1) * 128],
                                      wa[:, W2O + hc2 * 3072 + off: W2O + hc2 * 3072 + off + 512])
                                     for hc2 in range(4)]
                            parts.append((mfm[0:8, tc * 128:(tc + 1) * 128],
                                          wa[0:8, B2O + off: B2O + off + 512]))
                            wf = [l2, l5, hmt, mtr[0 if br == "r1" else 1]]
                        t = mmg(bk, psum[:, bk * 512:(bk + 1) * 512], parts,
                                waits_first=wf)
                        bufi = {"a": 1, "b": 7, "g": 4}[pname] + bi
                        if pname == "b":
                            e1 = act(ewb(bufi), psum[:, bk * 512:(bk + 1) * 512],
                                     AF.Exp, waits=[t] + war)
                            e2 = act(ewb(bufi), ewb(bufi), AF.Ln, bias=1.0,
                                     waits=[e1])
                            ev_parts[(br, pname)] = e2
                            bank_rd[bk] = e1
                        else:
                            e1 = act(ewb(bufi), psum[:, bk * 512:(bk + 1) * 512],
                                     AF.Sigmoid, waits=[t] + war)
                            ev_parts[(br, pname)] = e1
                            bank_rd[bk] = e1
                    if DEBUG_B and ft == 0 and it == 0 and tc == 0 and ch == 0:
                        evp = list(ev_parts.values())
                        for dn, bi_ in (("d_sa", 1), ("d_sb", 7), ("d_sg", 4),
                                        ("d_sb1", 8), ("d_sb2", 9)):
                            toks_y.append(dma(dbgB[dn][:, :], ewb(bi_), "sync", waits=evp))
                        dbg_ev = toks_y[-5:]
                        acopy(ew[0:1, 0:1], ew[0:1, 0:1], waits=dbg_ev)
                    else:
                        dbg_ev = []
                    wv_ = [dtok] + dbg_ev + list(ev_parts.values()) + [alpha_toks[tc],
                                                              hcol_toks[tc][0], hcol_toks[tc][1]]
                    ch_toks = []
                    readers = {}
                    first = True
                    for bi, br in enumerate(("sh", "r1", "r2")):
                        acol = (bct[:, 28:29] if br == "sh" else
                                cols[:, tc * 8 + 4 + (0 if br == "r1" else 1):
                                     tc * 8 + 5 + (0 if br == "r1" else 1)])
                        SA, SB, SG = ewb(1 + bi), ewb(7 + bi), ewb(4 + bi)
                        if it == 0:
                            na = mq[:, 28 + bi:29 + bi]
                            natok = vop("tensor_scalar_mul", na, acol, -1.0,
                                        waits=wv_ if first else [],
                                        keep_same_engine=True, mark=True)
                            ch_toks.append(natok)
                            vnp = ewb(12 + bi)
                            ch_toks.append(vop("scalar_tensor_tensor", vnp, ewb(0),
                                               na, SB, ALU.mult, ALU.mult,
                                               waits=[natok], keep_same_engine=True,
                                               mark=True))
                            readers[(br, "b")] = ch_toks[-1]
                        else:
                            A, B = ewb(10), ewb(11)
                            ch_toks.append(vop("scalar_tensor_tensor", A,
                                               vtm2[:, c0:c0 + 512], acol, SA,
                                               ALU.mult, ALU.mult,
                                               waits=(wv_ + [vup_last]) if first else [],
                                               keep_same_engine=True, mark=True))
                            readers[(br, "a")] = ch_toks[-1]
                            ch_toks.append(vop("scalar_tensor_tensor", B, ewb(0),
                                               acol, SB, ALU.mult, ALU.mult, waits=[],
                                               keep_same_engine=True, mark=True))
                            readers[(br, "b")] = ch_toks[-1]
                            vnp = ewb(12 + bi)
                            ch_toks.append(vop("tensor_sub", vnp, A, B, waits=[],
                                               keep_same_engine=True, mark=True))
                        first = False
                        if bi == 0:
                            ch_toks.append(vop("tensor_mul", ewb(15), SG, vnp, waits=[],
                                               keep_same_engine=True, mark=True))
                            readers[(br, "g")] = ch_toks[-1]
                        else:
                            ch_toks.append(vop("tensor_mul", ewb(10), SG, vnp, waits=[],
                                               keep_same_engine=True, mark=True))
                            readers[(br, "g")] = ch_toks[-1]
                            ch_toks.append(vop("tensor_add", ewb(15), ewb(15), ewb(10),
                                               waits=[], keep_same_engine=True, mark=True))
                    ch_toks.append(vop("tensor_add", ewb(11), ewb(12), ewb(13), waits=[],
                                       keep_same_engine=True, mark=True))
                    ch_toks.append(vop("tensor_add", ewb(11), ewb(11), ewb(14), waits=[],
                                       keep_same_engine=True, mark=True))
                    if DEBUG_B and ft == 0 and it == 0 and tc == 0 and ch == 0:
                        tvn = dma(dbgB["d_vn"][:, :], ewb(11), "sync", waits=[ch_toks[-1]])
                        tgg = dma(dbgB["d_g"][:, :], ewb(15), "sync", waits=[ch_toks[-1]])
                        for dn, bi_ in (("d_vn0", 12), ("d_vn1", 13), ("d_vn2", 14)):
                            toks_y.append(dma(dbgB[dn][:, :], ewb(bi_), "sync",
                                              waits=[ch_toks[-1]]))
                        tvn2 = toks_y[-3:]
                        toks_y += [tvn, tgg]
                        tvn = tvn; tgg = tgg
                        ch_toks.append(vop("tensor_copy", ew[0:1, 0:1], ew[0:1, 0:1],
                                           waits=[tvn, tgg] + tvn2))
                    iup = vop("scalar_tensor_tensor", itm[:, c0:c0 + 512], ewb(15),
                              mq[:, 24 + tc: 25 + tc], itm[:, c0:c0 + 512],
                              ALU.mult, ALU.add, waits=[hcol_toks[tc][1]],
                              keep_same_engine=True, mark=True)
                    if it == 0:
                        vup = vop("tensor_scalar_mul", vtm2[:, c0:c0 + 512], ewb(11),
                                  cols[:, tc * 8 + 7: tc * 8 + 8],
                                  waits=[hcol_toks[tc][0]],
                                  keep_same_engine=True, mark=True)
                    else:
                        ch_toks.append(vop("tensor_sub", ewb(10), ewb(11),
                                           vtm2[:, c0:c0 + 512], waits=[],
                                           keep_same_engine=True, mark=True))
                        vup = vop("scalar_tensor_tensor", vtm2[:, c0:c0 + 512], ewb(10),
                                  cols[:, tc * 8 + 7: tc * 8 + 8], vtm2[:, c0:c0 + 512],
                                  ALU.mult, ALU.add, waits=[hcol_toks[tc][0]],
                                  keep_same_engine=True, mark=True)
                    iup_last, vup_last = iup, vup
                    ew_war[0] = readers
            ctrl_last = iup_last

        if DEBUG_B and ft == 0:
            t_i1 = dma(dbgB["d_i1"][:, :], itm[:, :], "sync", waits=[iup_last])
            t_v1 = dma(dbgB["d_v1"][:, :], vtm2[:, :], "sync", waits=[vup_last])
            toks_y += [t_i1, t_v1]
            dbg_iv = [t_i1, t_v1]
        else:
            dbg_iv = []

        lr1 = arena_load(wa, R1O, wscr["refw1"], D, C2, 8, [ctrl_last] + dbg_iv)
        lr2 = arena_load(wa, R2O, wscr["refw2"], C2, D, 16, [ctrl_last])
        rfm_ev = to_fm(itm, 0, [iup_last] + dbg_iv)
        def hidr_plane(hc):
            tens = (hf, h1f, h2f, m1r)[hc // 4]
            return tens[:, (hc % 4) * 512:((hc % 4) + 1) * 512]
        hr_ev = []
        for hc in range(16):
            bk = hc % 2
            parts = [(wa[:, R1O + cb * 2048 + hc * 128: R1O + cb * 2048 + (hc + 1) * 128],
                      ctxf[:, cb * 512:(cb + 1) * 512]) for cb in range(8)]
            t = mmg(bk, psum[:, bk * 512:(bk + 1) * 512], parts,
                    waits_first=[lr1] + rfm_ev + ([hm1, hm2] if hc == 4 else []) +
                                (mrep_ev if hc == 12 else []))
            ev = act(hidr_plane(hc), psum[:, bk * 512:(bk + 1) * 512], AF.Gelu,
                     bias=bct[:, 8 + hc:9 + hc],
                     waits=[t, tk_b] + ([hm1, hm2] if hc in (4, 8, 12) else []))
            hr_ev.append(ev)
            bank_rd[bk] = ev
        if DEBUG_B and ft == 0:
            hr_t = dma(dbgB["d_hidr"][:, 0:2048], hf[:, :], "sync", waits=hr_ev)
            hr_t2 = dma(dbgB["d_hidr"][:, 2048:4096], h1f[:, :], "sync", waits=hr_ev)
            hr_t3 = dma(dbgB["d_hidr"][:, 4096:6144], h2f[:, :], "sync", waits=hr_ev)
            hr_t4 = dma(dbgB["d_hidr"][:, 6144:8192], m1r[:, :], "sync", waits=hr_ev)
            toks_y += [hr_t, hr_t2, hr_t3, hr_t4]
        ln_in_toks = []
        for tc in range(4):
            for chf in range(2):
                bk = 2 + (2 * tc + chf) % 2
                c0 = tc * 1024 + chf * 512
                parts = [(hidr_plane(hc)[:, tc * 128:(tc + 1) * 128],
                          wa[:, R2O + hc * 1024 + chf * 512: R2O + hc * 1024 + chf * 512 + 512])
                         for hc in range(16)]
                parts.append((onesb[0:1, 0:128], rows[0:1, chf * 512: (chf + 1) * 512]))
                t = mmg(bk, psum[:, bk * 512:(bk + 1) * 512], parts,
                        waits_first=hr_ev + [lr2, tk_rows])
                u1 = vop("tensor_mul", ew[:, 0:512], psum[:, bk * 512:(bk + 1) * 512],
                         iwrep[:, (c0 % 1024):(c0 % 1024) + 512], waits=[t, ld[1]])
                if DEBUG_B and ft == 0 and tc == 0 and chf == 0:
                    t_ri = dma(dbgB["d_refiw"][:, :], ew[:, 0:512], "sync", waits=[u1])
                    toks_y.append(t_ri)
                    u1 = t_ri
                u2 = vop("scalar_tensor_tensor", y1t[:, c0:c0 + 512], y1t[:, c0:c0 + 512],
                         2.0, ew[:, 0:512], ALU.mult, ALU.add, waits=[u1],
                         keep_same_engine=True, mark=True)
                ln_in_toks.append(u2)
                bank_rd[bk] = u2
        if DEBUG_B and ft == 0:
            t_li = dma(dbgB["d_ln2in"][:, :], y1t[:, :], "sync", waits=ln_in_toks)
            toks_y.append(t_li)
            ln_in_toks = ln_in_toks + [t_li]
        lnf = None
        lnaf = None
        for tc in range(4):
            s1ap = y1t[:, tc * 1024:(tc + 1) * 1024]
            w = [ln_in_toks[2 * tc], ln_in_toks[2 * tc + 1]] + ([lnf] if lnf else [])
            b1 = vop("bn_stats", lnst[:, 0:6], s1ap[:, 0:512], waits=w, mark=True)
            b2 = vop("bn_stats", lnst[:, 6:12], s1ap[:, 512:1024], waits=w, mark=True)
            b3 = vop("bn_aggr", lnst[:, 12:14], lnst[:, 0:12],
                     waits=[b1, b2] + ([lnaf] if lnaf else []), keep_same_engine=True)
            l1_ = act(lnst[:, 14:15], lnst[:, 13:14], AF.Ln, bias=cstf[:, 1:2],
                      waits=[b3])
            l2_ = act(lnst[:, 15:16], lnst[:, 14:15], AF.Exp, scale=-0.5, waits=[l1_])
            lnaf = l1_
            yn = vop("tensor_scalar", s1ap, s1ap, lnst[:, 12:13], lnst[:, 15:16],
                     ALU.subtract, ALU.mult, waits=[l2_])
            lnf = yn
            yr = y_out.rearrange("(t p) c -> t p c", p=128)
            yst = dma(yr[(tbase // 128) + tc], s1ap, "sync", waits=[yn])
            toks_y.append(yst)
        prev_tile_done = toks_y[-1]
        warena_free = ln_in_toks[-1]
    return toks_y


def _run(inputs):
    consts = _host_consts()
    lay = _host_layout(inputs)
    x = np.asarray(inputs["L0_attn_x"])
    nc = build_nc()
    in_maps = []
    for c in range(N_CORES):
        m = dict(lay)
        m.update(consts)
        m["x"] = np.ascontiguousarray(x[2 * c:2 * c + 2].reshape(TOK, D))
        in_maps.append(m)
    res = run_bass_kernel_spmd(nc, in_maps, core_ids=list(range(N_CORES)))
    return res


def kernel(**inputs):
    res = _run(inputs)
    return np.concatenate(
        [res.results[c]["y"].reshape(2, L, D) for c in range(N_CORES)], 0)
